# version 01
# baseline (speedup 1.0000x reference)
import math
from collections import defaultdict

import numpy as np
import ml_dtypes

import concourse.bass as bass
import concourse.mybir as mybir
from concourse.bass_utils import run_bass_kernel_spmd

F32 = mybir.dt.float32
BF16 = mybir.dt.bfloat16
I32 = mybir.dt.int32
U32 = mybir.dt.uint32
AF_ = mybir.ActivationFunctionType
ALU = mybir.AluOpType
AX = mybir.AxisListType

B, S, HID = 4, 4096, 1024
NCORES = 8
T = 2048
TT = 16
TC = 4
FC = 8
NH = 16
DPHI = 153
CH = 8
NE = 64
EPC = 8
CAP = 512
TOPK = 8
ISH = 2048
IEX = 512

RDs = 4.0
RRD = 2.0
R2 = math.sqrt(2.0)
FSC = 1.0 / math.sqrt(32.0)
FB = 3.0 * FSC
BIG = 1.0e9
LNEPS = 1e-5

AB_XN = 0
AB_QF = 16384
AB_KF = 24576
AB_VT = 32768
AB_YF = 49408
AB_N = 65792
AB_PHIQA = AB_XN + 0
AB_PHIQB = AB_XN + 2048
AB_PHIK = AB_XN + 4096
AB_A4 = AB_XN + 5120
AB_STG = AB_XN + 7168
AB_BUF_FM = AB_QF + 0
AB_BUF_T = AB_QF + 4096
AB_EW = AB_KF + 0
AB_ACT = AB_VT + 0
AB_WOUT = AB_VT + 2048
AB_ACT4 = AB_VT + 4096
AB_SHACC = AB_YF

AF_S1 = 0
AF_S2 = 2048
AF_DEN = 4096
AF_SSB = 4352
AF_CAR = 5392
AF_RS = 5408
AF_RK = 6944
AF_ROWS = 7456
AF_LNR = 8480
AF_N = 12608


def _build_selection():
    L = np.zeros((32, DPHI), np.float32)
    R = np.zeros((32, DPHI), np.float32)
    L[16, 0] = 1.0
    R[16, 0] = 1.0
    for i in range(16):
        L[i, 1 + i] = 1.0 / RRD
        R[16, 1 + i] = 1.0
    for i in range(16):
        L[i, 17 + i] = 1.0 / (RDs * R2)
        R[i, 17 + i] = 1.0
    cc = 33
    for i in range(16):
        for j in range(i + 1, 16):
            L[i, cc] = 1.0 / RDs
            R[j, cc] = 1.0
            cc += 1
    assert cc == DPHI
    Ls = np.zeros((128, DPHI), np.float32)
    Rs = np.zeros((128, DPHI), np.float32)
    for g in range(4):
        Ls[g * 32:(g + 1) * 32] = L
        Rs[g * 32:(g + 1) * 32] = R
    return Ls, Rs


def _pad_qk_weight(W):
    Wp = np.zeros((HID, 512), np.float32)
    for h in range(NH):
        Wp[:, h * 32: h * 32 + 16] = W[h * 16:(h + 1) * 16, :].T
    return Wp


def _causal_masks():
    j = np.arange(256)[None, :]
    p = np.arange(128)[:, None]
    m0 = (p <= j).astype(np.float32)
    m1 = (p + 128 <= j).astype(np.float32)
    return np.concatenate([m0, m1], axis=1)


ENGINES = ("sync", "scalar", "vector", "tensor", "gpsimd")


class P:
    def __init__(self, nc, sems):
        self.nc = nc
        self.sems = sems
        self.streams = {e: [] for e in ENGINES}
        self.cnt = defaultdict(int)

    def op(self, eng, fn, sem=None, n=1):
        if eng == "vector":
            assert sem == "dve", "vector ops must increment dve"
            self.wait(eng, "dve")
        elif eng == "scalar" and sem == "act":
            self.wait(eng, "act")
        if sem is None:
            self.streams[eng].append(fn)
        else:
            self.cnt[sem] += n
            s = self.sems[sem]
            self.streams[eng].append(lambda e, fn=fn, s=s, n=n: fn(e).then_inc(s, n))

    def wait(self, eng, sem, val=None):
        v = self.cnt[sem] if val is None else val
        if v <= 0:
            return
        s = self.sems[sem]
        self.streams[eng].append(lambda e, s=s, v=v: e.wait_ge(s, v))

    def dma(self, eng, out, in_, sem, **kw):
        self.op(eng, lambda e, out=out, in_=in_, kw=kw: e.dma_start(out=out, in_=in_, **kw),
                sem=sem, n=16)

    def replay(self, block):
        def runner(name):
            def run(e):
                for f in self.streams[name]:
                    f(e)
            return run
        block.sync(runner("sync"))
        block.scalar(runner("scalar"))
        block.vector(runner("vector"))
        block.tensor(runner("tensor"))
        block.gpsimd(runner("gpsimd"))


def build_program(debug_stage=None):
    nc = bass.Bass("TRN2", target_bir_lowering=False)
    dt = nc.dram_tensor

    io = {}
    io["x_fm"] = dt("x_fm", [HID, T], F32, kind="ExternalInput").ap()
    io["wqk"] = dt("wqk", [HID, 1024], F32, kind="ExternalInput").ap()
    io["wv"] = dt("wv", [HID, HID], F32, kind="ExternalInput").ap()
    io["wo"] = dt("wo", [HID, HID], F32, kind="ExternalInput").ap()
    io["gate"] = dt("gate", [HID, NE], F32, kind="ExternalInput").ap()
    io["sgw"] = dt("sgw", [HID, 1], F32, kind="ExternalInput").ap()
    io["sgu"] = dt("sgu", [HID, 2 * ISH], F32, kind="ExternalInput").ap()
    io["sdw"] = dt("sdw", [ISH, HID], F32, kind="ExternalInput").ap()
    io["egu"] = dt("egu", [EPC, HID, 2 * IEX], F32, kind="ExternalInput").ap()
    io["edw"] = dt("edw", [EPC, IEX, HID], F32, kind="ExternalInput").ap()
    io["lsel"] = dt("lsel", [128, DPHI], BF16, kind="ExternalInput").ap()
    io["rsel"] = dt("rsel", [128, DPHI], BF16, kind="ExternalInput").ap()
    io["msk"] = dt("msk", [128, 512], BF16, kind="ExternalInput").ap()
    io["idf"] = dt("idf", [128, 128], F32, kind="ExternalInput").ap()
    io["idb"] = dt("idb", [128, 128], BF16, kind="ExternalInput").ap()
    io["iota64"] = dt("iota64", [128, NE], F32, kind="ExternalInput").ap()
    io["tokid"] = dt("tokid", [128, TT], F32, kind="ExternalInput").ap()
    io["rankm"] = dt("rankm", [8, 1], F32, kind="ExternalInput").ap()
    io["selm"] = dt("selm", [128, 8], F32, kind="ExternalInput").ap()
    io["recidx"] = dt("recidx", [128, EPC * 4], I32, kind="ExternalInput").ap()
    io["cvals"] = dt("cvals", [128, 8], F32, kind="ExternalInput").ap()
    io["out_fm"] = dt("out_fm", [HID, T], F32, kind="ExternalOutput").ap()

    io["x1d"] = dt("x1d", [HID, T], F32).ap()
    io["onesd"] = dt("onesd", [2048], BF16).ap()
    io["lnd"] = dt("lnd", [4096], F32).ap()
    io["sgd"] = dt("sgd", [T], F32).ap()
    io["sagi"] = dt("sagi", [160, NH * 65], F32).ap()
    io["sago"] = dt("sago", [8 * 160, NH * 65], F32).ap()
    io["cnti"] = dt("cnti", [NE, 1], F32).ap()
    io["cnto"] = dt("cnto", [8 * NE, 1], F32).ap()
    io["recm"] = dt("recm", [NE * 513, 2], F32).ap()
    io["recmr"] = dt("recmr", [NE * 513, 2], F32).ap()
    io["xgi"] = dt("xgi", [T, HID], BF16).ap()
    io["xgo"] = dt("xgo", [NCORES * T, HID], BF16).ap()
    io["woutl"] = dt("woutl", [EPC * CAP, HID], BF16).ap()
    io["ymd"] = dt("ymd", [T, HID], F32).ap()
    io["woutg"] = dt("woutg", [NE * CAP, HID], BF16).ap()

    if debug_stage is not None:
        io["dbg_ab"] = dt("dbg_ab", [128, AB_N], F32, kind="ExternalOutput").ap()
        io["dbg_af"] = dt("dbg_af", [128, AF_N], F32, kind="ExternalOutput").ap()
        io["dbg_rec"] = dt("dbg_rec", [NE * 513, 2], F32, kind="ExternalOutput").ap()
        io["dbg_sag"] = dt("dbg_sag", [160, NH * 65], F32, kind="ExternalOutput").ap()
        io["dbg_cum"] = dt("dbg_cum", [64, 2048], F32, kind="ExternalOutput").ap()
        io["dbg_widx"] = dt("dbg_widx", [128, 128], I32, kind="ExternalOutput").ap()
        io["dbg_wout"] = dt("dbg_wout", [EPC * CAP, HID], F32, kind="ExternalOutput").ap()

    from contextlib import ExitStack
    with ExitStack() as ctx:
        en = ctx.enter_context
        AB = en(nc.sbuf_tensor("AB", [128, AB_N], BF16))
        AFr = en(nc.sbuf_tensor("AFr", [128, AF_N], F32))
        cumt = en(nc.sbuf_tensor("cum", [64, 2048], F32))
        ssbB = en(nc.sbuf_tensor("ssbB", [32, NH * 65], F32))
        sbf = en(nc.sbuf_tensor("sbf", [128, NH * 65], BF16))
        sbfB = en(nc.sbuf_tensor("sbfB", [32, NH * 65], BF16))
        spv = en(nc.sbuf_tensor("spv", [128, NH * 65], BF16))
        spvB = en(nc.sbuf_tensor("spvB", [32, NH * 65], BF16))
        lselt = en(nc.sbuf_tensor("lsel_sb", [128, DPHI], BF16))
        rselt = en(nc.sbuf_tensor("rsel_sb", [128, DPHI], BF16))
        mskt = en(nc.sbuf_tensor("mskt", [128, 512], BF16))
        idft = en(nc.sbuf_tensor("idf_sb", [128, 128], F32))
        idbt = en(nc.sbuf_tensor("idb_sb", [128, 128], BF16))
        iota64t = en(nc.sbuf_tensor("iota64_sb", [128, NE], F32))
        tokidt = en(nc.sbuf_tensor("tokid_sb", [128, TT], F32))
        selmt = en(nc.sbuf_tensor("selm_sb", [128, 8], F32))
        rankmt = en(nc.sbuf_tensor("rankm_sb", [8, 1], F32))
        recidxt = en(nc.sbuf_tensor("recidxt", [128, EPC * 4], I32))
        ones1 = en(nc.sbuf_tensor("ones1", [1, 512], F32))
        onesb = en(nc.sbuf_tensor("onesb", [1, 2048], BF16))
        onesc = en(nc.sbuf_tensor("onesc", [128, 1], F32))
        sgwb = en(nc.sbuf_tensor("sgwb", [128, 8], BF16))
        cvalt = en(nc.sbuf_tensor("cvalt", [128, 8], F32))
        idx8 = en(nc.sbuf_tensor("idx8", [128, 8], U32))
        sidx = en(nc.sbuf_tensor("sidx", [128, 8], I32))
        psA = en(nc.psum_tensor("psA", [128, 512], F32))
        psB = en(nc.psum_tensor("psB", [128, 512], F32))
        psC = en(nc.psum_tensor("psC", [128, 512], F32))
        psD = en(nc.psum_tensor("psD", [128, 512], F32))
        psY = en(nc.psum_tensor("psY", [65, 1024], F32))
        psS = en(nc.psum_tensor("psS", [128, 260], F32))
        psSB = en(nc.psum_tensor("psSB", [32, 260], F32))
        s_ld = en(nc.semaphore("ld"))
        s_wld = en(nc.semaphore("wld"))
        s_pe = en(nc.semaphore("pe"))
        s_act = en(nc.semaphore("act"))
        s_dve = en(nc.semaphore("dve"))
        s_gd = en(nc.semaphore("gd"))
        s_cc = en(nc.semaphore("cc"))
        s_st = en(nc.semaphore("st"))
        block = en(nc.Block())
        sems = dict(ld=s_ld, wld=s_wld, pe=s_pe, act=s_act, dve=s_dve,
                    gd=s_gd, cc=s_cc, st=s_st)
        p = P(nc, sems)
        c = dict(io)
        c.update(nc=nc, p=p, AB=AB, AF=AFr, cum=cumt, ssbB=ssbB, sbf=sbf,
                 sbfB=sbfB, spv=spv, spvB=spvB, lselt=lselt, rselt=rselt,
                 mskt=mskt, idft=idft, idbt=idbt, iota64t=iota64t,
                 tokidt=tokidt, selmt=selmt, rankmt=rankmt, recidxt=recidxt,
                 ones1=ones1, onesb=onesb, onesc=onesc,
                 onesf=AFr[0:64, AF_LNR + 2048:AF_LNR + 2048 + 128],
                 widx=AFr[:, AF_LNR + 2304:AF_LNR + 2304 + 128].bitcast(I32),
                 sgwb=sgwb, idx8=idx8, sidx=sidx,
                 psA=psA, psB=psB, psC=psC, psD=psD, psY=psY, psS=psS,
                 psSB=psSB, cvalt=cvalt, debug_stage=debug_stage)
        nc.const_aps.register(F32, FSC, cvalt[:, 0:1])
        nc.const_aps.register(F32, FB, cvalt[:, 1:2])
        nc.const_aps.register(F32, LNEPS, cvalt[:, 2:3])
        emit_all(c)
        p.replay(block)
    return nc


def ab(c, off, n):
    return c["AB"][:, off:off + n]


def abr(c, r0, r1, off, n):
    return c["AB"][r0:r1, off:off + n]


def af(c, off, n):
    return c["AF"][:, off:off + n]


def afr(c, r0, r1, off, n):
    return c["AF"][r0:r1, off:off + n]


def emit_all(c):
    p = c["p"]
    emit_setup(c)
    emit_ln(c, src="x0")
    emit_projections(c)
    if c["debug_stage"] == "proj":
        emit_dump(c, regions=["ab", "af"])
        return
    emit_scan(c)
    if c["debug_stage"] == "scan":
        emit_dump(c, regions=["ab", "af", "sag"])
        return
    emit_handoff(c)
    emit_correction(c)
    if c["debug_stage"] == "attn":
        emit_dump(c, regions=["ab", "af", "sag"])
        return
    emit_norm_wo_resid(c)
    emit_ln(c, src="x1")
    if c["debug_stage"] == "ln2":
        emit_dump(c, regions=["ab"])
        return
    emit_xg(c)
    emit_routing(c)
    if c["debug_stage"] == "routing":
        emit_dump(c, regions=["af", "rec", "cumd", "widxd"])
        return
    emit_experts(c)
    if c["debug_stage"] == "experts":
        emit_dump(c, regions=["af", "rec", "widxd", "woutd"])
        return
    emit_shared(c)
    if c["debug_stage"] == "shared":
        emit_dump(c, regions=["ab", "af"])
        return
    emit_combine(c)
    if c["debug_stage"] == "full":
        emit_dump(c, regions=["ab", "af"])
        return
    p.wait("gpsimd", "st")


def emit_setup(c):
    p = c["p"]
    c["bregs"] = {}

    def _alloc_bounds(e):
        for v in (NE * 513 - 1, NCORES * T - 1, NE * CAP - 1):
            r = e.alloc_register()
            e.reg_mov(r, v)
            c["bregs"][v] = r
        return None
    p.streams["gpsimd"].append(lambda e: _alloc_bounds(e))
    for name, tile in [("lsel", c["lselt"]), ("rsel", c["rselt"]),
                       ("msk", c["mskt"]), ("idf", c["idft"]), ("idb", c["idbt"]),
                       ("iota64", c["iota64t"]), ("tokid", c["tokidt"]),
                       ("selm", c["selmt"]), ("rankm", c["rankmt"]),
                       ("recidx", c["recidxt"]), ("cvals", c["cvalt"])]:
        p.dma("sync", tile[:], c[name][:], "ld")
    p.op("vector", lambda e: e.memset(c["ones1"][:], 1.0), sem="dve")
    p.op("vector", lambda e: e.memset(c["onesb"][:], 1.0), sem="dve")
    p.op("vector", lambda e: e.memset(c["onesc"][:], 1.0), sem="dve")
    p.op("vector", lambda e: e.memset(af(c, AF_SSB, 1040), 0.0), sem="dve")
    p.op("vector", lambda e: e.memset(c["ssbB"][:], 0.0), sem="dve")
    p.op("vector", lambda e: e.memset(c["sbf"][:], 0.0), sem="dve")
    p.op("vector", lambda e: e.memset(c["sbfB"][:], 0.0), sem="dve")
    p.op("vector", lambda e: e.memset(af(c, AF_DEN, 256), 0.0), sem="dve")
    p.wait("sync", "dve")
    p.dma("sync", c["onesd"].rearrange("(o a) -> o a", o=1), c["onesb"][0:1, :], "ld")


def emit_ln(c, src):
    p = c["p"]
    xsrc = c["x_fm"] if src == "x0" else c["x1d"]
    psA, psB, psC, psD = c["psA"], c["psB"], c["psC"], c["psD"]
    p.wait("sync", "gd")

    for tc in range(TC):
        for fc in range(FC):
            slot = fc % 2
            xs = af(c, AF_S1 + slot * 512, 512)
            sq = af(c, AF_S2 + slot * 512, 512)
            p.wait("sync", "pe")
            p.wait("sync", "act")
            p.dma("sync", xs, xsrc[fc * 128:(fc + 1) * 128, tc * 512:(tc + 1) * 512], "ld")
            p.wait("scalar", "ld")
            p.wait("scalar", "pe")
            p.op("scalar", lambda e, xs=xs, sq=sq: e.activation(
                out=sq, in_=xs, func=AF_.Square), sem="act")
            if fc == 0:
                p.wait("tensor", "dve")
            p.wait("tensor", "ld")
            p.wait("tensor", "act")
            p.op("tensor", lambda e, xs=xs, fc=fc: e.matmul(
                psA[0:1, 0:512], c["onesc"][:, 0:1], xs,
                start=(fc == 0), stop=(fc == FC - 1)), sem="pe")
            p.op("tensor", lambda e, sq=sq, fc=fc: e.matmul(
                psB[0:1, 0:512], c["onesc"][:, 0:1], sq,
                start=(fc == 0), stop=(fc == FC - 1)), sem="pe")
        p.wait("vector", "pe")
        p.wait("vector", "ld")
        p.op("vector", lambda e: e.tensor_copy(af(c, AF_ROWS, 512)[0:1, :], psA[0:1, 0:512]), sem="dve")
        p.wait("sync", "dve")
        p.dma("sync", c["lnd"][tc * 512:(tc + 1) * 512].rearrange("(o a) -> o a", o=1),
              af(c, AF_ROWS, 512)[0:1, :], "ld")
        p.wait("vector", "ld")
        p.op("vector", lambda e: e.tensor_copy(af(c, AF_ROWS, 512)[0:1, :], psB[0:1, 0:512]), sem="dve")
        p.wait("sync", "dve")
        p.dma("sync", c["lnd"][2048 + tc * 512:2048 + (tc + 1) * 512].rearrange("(o a) -> o a", o=1),
              af(c, AF_ROWS, 512)[0:1, :], "ld")

    p.wait("sync", "ld")
    p.wait("sync", "pe")
    p.wait("sync", "act")
    p.dma("sync", af(c, AF_S1, 32), c["lnd"].rearrange("(p a) -> p a", p=128), "ld")
    p.wait("vector", "ld")
    inv = 1.0 / HID
    p.op("vector", lambda e: e.tensor_copy(afr(c, 0, 64, AF_S1 + 32, 32),
                                           afr(c, 64, 128, AF_S1, 32)), sem="dve")
    p.op("vector", lambda e: e.tensor_scalar(afr(c, 0, 64, AF_S2, 32), afr(c, 0, 64, AF_S1, 32),
                                             inv, None, op0=ALU.mult), sem="dve")
    p.op("vector", lambda e: e.tensor_scalar(afr(c, 0, 64, AF_S2 + 32, 32),
                                             afr(c, 0, 64, AF_S1 + 32, 32),
                                             inv, None, op0=ALU.mult), sem="dve")
    p.op("vector", lambda e: e.tensor_mul(afr(c, 0, 64, AF_S1 + 64, 32),
                                          afr(c, 0, 64, AF_S2, 32),
                                          afr(c, 0, 64, AF_S2, 32)), sem="dve")
    p.op("vector", lambda e: e.tensor_sub(afr(c, 0, 64, AF_S1 + 96, 32),
                                          afr(c, 0, 64, AF_S2 + 32, 32),
                                          afr(c, 0, 64, AF_S1 + 64, 32)), sem="dve")
    p.wait("scalar", "dve")
    p.op("scalar", lambda e: e.activation(out=afr(c, 0, 64, AF_S2 + 64, 32),
                                          in_=afr(c, 0, 64, AF_S1 + 96, 32),
                                          func=AF_.Sqrt, bias=LNEPS), sem="act")
    p.wait("vector", "act")
    p.op("vector", lambda e: e.reciprocal(afr(c, 0, 64, AF_S2 + 96, 32),
                                          afr(c, 0, 64, AF_S2 + 64, 32)), sem="dve")
    p.op("vector", lambda e: e.tensor_mul(afr(c, 0, 64, AF_S1 + 64, 32),
                                          afr(c, 0, 64, AF_S2, 32),
                                          afr(c, 0, 64, AF_S2 + 96, 32)), sem="dve")
    p.wait("sync", "dve")
    p.dma("sync", c["lnd"][0:2048].rearrange("(p a) -> p a", p=64),
          afr(c, 0, 64, AF_S2 + 96, 32), "ld")
    p.dma("sync", c["lnd"][2048:4096].rearrange("(p a) -> p a", p=64),
          afr(c, 0, 64, AF_S1 + 64, 32), "ld")
    p.wait("sync", "ld")
    p.dma("sync", af(c, AF_LNR, 2048)[0:1, :],
          c["lnd"][0:2048].rearrange("(o a) -> o a", o=1), "ld")
    p.dma("sync", af(c, AF_LNR + 2048, 2048)[0:1, :],
          c["lnd"][2048:4096].rearrange("(o a) -> o a", o=1), "ld")

    for tc in range(TC):
        p.wait("tensor", "ld")
        p.wait("tensor", "dve")
        p.op("tensor", lambda e, tc=tc: e.matmul(
            psC[:, 0:512], c["ones1"][0:1, 0:128],
            af(c, AF_LNR + tc * 512, 512)[0:1, :], start=True, stop=True), sem="pe")
        p.op("tensor", lambda e, tc=tc: e.matmul(
            psD[:, 0:512], c["ones1"][0:1, 0:128],
            af(c, AF_LNR + 2048 + tc * 512, 512)[0:1, :], start=True, stop=True), sem="pe")
        for fc in range(FC):
            slot = fc % 2
            xs = af(c, AF_S1 + slot * 512, 512)
            tmp = af(c, AF_S2 + slot * 512, 512)
            p.wait("sync", "dve")
            p.dma("sync", xs, xsrc[fc * 128:(fc + 1) * 128, tc * 512:(tc + 1) * 512], "ld")
            p.wait("vector", "ld")
            p.wait("vector", "pe")
            p.op("vector", lambda e, xs=xs, tmp=tmp: e.tensor_mul(tmp, xs, psC[:, 0:512]), sem="dve")
            p.op("vector", lambda e, tmp=tmp, fc=fc, tc=tc: e.tensor_sub(
                ab(c, AB_XN + fc * T + tc * 512, 512), tmp, psD[:, 0:512]), sem="dve")


def emit_projections(c):
    p = c["p"]
    psA, psB = c["psA"], c["psB"]
    WQK = AB_YF
    for fc in range(FC):
        p.dma("gpsimd", ab(c, WQK + fc * 1024, 1024),
              c["wqk"][fc * 128:(fc + 1) * 128, :], "wld")
    p.wait("tensor", "wld")
    for mj in range(8):
        for tc in range(TC):
            ps = psA if (mj * TC + tc) % 2 == 0 else psB
            p.wait("tensor", "dve")
            for fc in range(FC):
                p.op("tensor", lambda e, ps=ps, mj=mj, tc=tc, fc=fc: e.matmul(
                    ps[:, 0:512],
                    ab(c, WQK + fc * 1024 + mj * 128, 128),
                    ab(c, AB_XN + fc * T + tc * 512, 512),
                    start=(fc == 0), stop=(fc == FC - 1)), sem="pe")
            dst_off = (AB_QF + mj * T if mj < 4 else AB_KF + (mj - 4) * T) + tc * 512
            p.wait("vector", "pe")
            p.op("vector", lambda e, ps=ps, dst_off=dst_off: e.tensor_copy(
                ab(c, dst_off, 512), ps[:, 0:512]), sem="dve")
    p.wait("gpsimd", "dve")
    for base in (AB_QF, AB_KF):
        for g in range(4):
            row = 16 + 32 * g
            src = c["onesd"].rearrange("(o b a) -> o b a", o=1, b=1).to_broadcast([1, 4, 2048])
            p.dma("gpsimd",
                  c["AB"][row:row + 1, base:base + 8192].rearrange("p (a b) -> p a b", b=2048),
                  src, "wld")
    p.wait("gpsimd", "pe")
    for fc in range(FC):
        p.dma("gpsimd", ab(c, WQK + fc * 1024, 1024),
              c["wv"][fc * 128:(fc + 1) * 128, :], "wld")
    p.wait("tensor", "wld")
    for i in range(TT):
        for n in range(2):
            ps = psA if (i * 2 + n) % 2 == 0 else psB
            p.wait("tensor", "dve")
            for fc in range(FC):
                p.op("tensor", lambda e, ps=ps, i=i, n=n, fc=fc: e.matmul(
                    ps[:, 0:512],
                    ab(c, AB_XN + fc * T + i * 128, 128),
                    ab(c, WQK + fc * 1024 + n * 512, 512),
                    start=(fc == 0), stop=(fc == FC - 1)), sem="pe")
            p.wait("vector", "pe")
            dst = c["AB"][:, AB_VT + i * 1040 + n * 8 * 65: AB_VT + i * 1040 + (n * 8 + 8) * 65]
            dstv = dst.rearrange("p (h d) -> p h d", d=65)[:, :, 0:64]
            srcv = ps[:, 0:512].rearrange("p (h d) -> p h d", d=64)
            p.op("vector", lambda e, dstv=dstv, srcv=srcv: e.tensor_copy(dstv, srcv), sem="dve")
        dstones = c["AB"][:, AB_VT + i * 1040: AB_VT + (i + 1) * 1040]
        dstones = dstones.rearrange("p (h d) -> p h d", d=65)[:, :, 64:65]
        p.op("vector", lambda e, dstones=dstones: e.memset(dstones, 1.0), sem="dve")


SCAN_G = 4
SCAN_CH = CH
SCAN_PARTS = 5
SCAN_MASK = True
SCAN_F = True


def emit_scan(c):
    p = c["p"]
    psA, psB, psC, psD = c["psA"], c["psB"], c["psC"], c["psD"]
    psY, psS, psSB = c["psY"], c["psS"], c["psSB"]
    p.wait("tensor", "wld")

    for g in range(SCAN_G):
        for ch in range(SCAN_CH):
            tq0 = ch * 256
            pss = (psA, psB, psC, psD)
            p.wait("tensor", "dve")
            for hp in range(4):
                p.op("tensor", lambda e, hp=hp, g=g, tq0=tq0: e.matmul(
                    pss[hp][:, 0:256],
                    c["lselt"][32 * hp:32 * hp + 32, 0:128],
                    ab(c, AB_QF + g * T + tq0, 256)[32 * hp:32 * hp + 32, :],
                    start=True, stop=True, tile_position=(32 * hp, 0)), sem="pe")
                p.op("tensor", lambda e, hp=hp, g=g, tq0=tq0: e.matmul(
                    pss[hp][0:25, 256:512],
                    c["lselt"][32 * hp:32 * hp + 32, 128:153],
                    ab(c, AB_QF + g * T + tq0, 256)[32 * hp:32 * hp + 32, :],
                    start=True, stop=True, tile_position=(32 * hp, 0)), sem="pe")
            p.wait("vector", "pe")
            for hp in range(4):
                p.op("vector", lambda e, hp=hp: e.tensor_copy(
                    ab(c, AB_STG + hp * 256, 256), pss[hp][:, 0:256]), sem="dve")
                p.op("vector", lambda e, hp=hp: e.tensor_copy(
                    abr(c, 0, 25, AB_STG + 1024 + hp * 256, 256),
                    pss[hp][0:25, 256:512]), sem="dve")
            p.wait("tensor", "dve")
            for hp in range(4):
                p.op("tensor", lambda e, hp=hp, g=g, tq0=tq0: e.matmul(
                    pss[hp][:, 0:256],
                    c["rselt"][32 * hp:32 * hp + 32, 0:128],
                    ab(c, AB_QF + g * T + tq0, 256)[32 * hp:32 * hp + 32, :],
                    start=True, stop=True, tile_position=(32 * hp, 0)), sem="pe")
                p.op("tensor", lambda e, hp=hp, g=g, tq0=tq0: e.matmul(
                    pss[hp][0:25, 256:512],
                    c["rselt"][32 * hp:32 * hp + 32, 128:153],
                    ab(c, AB_QF + g * T + tq0, 256)[32 * hp:32 * hp + 32, :],
                    start=True, stop=True, tile_position=(32 * hp, 0)), sem="pe")
            p.wait("vector", "pe")
            for hp in range(4):
                p.op("vector", lambda e, hp=hp: e.tensor_mul(
                    ab(c, AB_PHIQA + hp * 256, 256),
                    ab(c, AB_STG + hp * 256, 256), pss[hp][:, 0:256]), sem="dve")
                p.op("vector", lambda e, hp=hp: e.tensor_mul(
                    abr(c, 0, 25, AB_PHIQB + hp * 256, 256),
                    abr(c, 0, 25, AB_STG + 1024 + hp * 256, 256),
                    pss[hp][0:25, 256:512]), sem="dve")

            for tk in range(2 if SCAN_PARTS >= 2 else 0):
                p.wait("tensor", "dve")
                if tk == 1:
                    p.wait("tensor", "act")
                for hp in range(4):
                    p.op("tensor", lambda e, hp=hp, g=g, tq0=tq0, tk=tk: e.matmul(
                        pss[hp][:, 0:256],
                        ab(c, AB_KF + g * T + tq0 + tk * 128, 128)[32 * hp:32 * hp + 32, :],
                        ab(c, AB_QF + g * T + tq0, 256)[32 * hp:32 * hp + 32, :],
                        start=True, stop=True, tile_position=(32 * hp, 0)), sem="pe")
                a4o = AB_A4 + tk * 1024
                p.wait("scalar", "pe")
                for hp in range(4):
                    p.op("scalar", lambda e, hp=hp, a4o=a4o: e.activation(
                        out=ab(c, a4o + hp * 256, 256), in_=pss[hp][:, 0:256],
                        func=AF_.Square, scale=FSC, bias=FB), sem="act")
                p.wait("vector", "act")
                a4v = ab(c, a4o, 1024).rearrange("p (h n) -> p h n", n=256)
                mskv = c["mskt"][:, tk * 256:(tk + 1) * 256].rearrange(
                    "p (o n) -> p o n", o=1).to_broadcast([128, 4, 256])
                if SCAN_MASK:
                    p.op("vector", lambda e, a4v=a4v, mskv=mskv: e.scalar_tensor_tensor(
                        out=a4v, in0=a4v, scalar=0.5, in1=mskv,
                        op0=ALU.add, op1=ALU.mult), sem="dve")

            p.wait("tensor", "dve")
            for hp in range(4 if SCAN_PARTS >= 3 else 0):
                h = 4 * g + hp
                col = hp * 256
                for tk in range(2):
                    p.op("tensor", lambda e, h=h, hp=hp, col=col, tk=tk, ch=ch: e.matmul(
                        psY[0:65, col:col + 256],
                        ab(c, AB_VT + (2 * ch + tk) * 1040 + h * 65, 65),
                        ab(c, AB_A4 + tk * 1024 + hp * 256, 256),
                        start=(tk == 0), stop=False), sem="pe")
                p.op("tensor", lambda e, h=h, hp=hp, col=col: e.matmul(
                    psY[0:65, col:col + 256],
                    c["sbf"][:, h * 65:h * 65 + 65],
                    ab(c, AB_PHIQA + hp * 256, 256),
                    start=False, stop=False), sem="pe")
                p.op("tensor", lambda e, h=h, hp=hp, col=col: e.matmul(
                    psY[0:65, col:col + 256],
                    c["sbfB"][0:25, h * 65:h * 65 + 65],
                    abr(c, 0, 25, AB_PHIQB + hp * 256, 256),
                    start=False, stop=True), sem="pe")
            p.wait("vector", "pe")
            p.wait("vector", "gd")
            for hp in range(4 if SCAN_PARTS >= 3 else 0):
                h = 4 * g + hp
                p.op("vector", lambda e, h=h, hp=hp, tq0=tq0: e.tensor_copy(
                    abr(c, (h % 2) * 64, (h % 2) * 64 + 64, AB_YF + (h // 2) * T + tq0, 256),
                    psY[0:64, hp * 256:hp * 256 + 256]), sem="dve")
            if SCAN_PARTS >= 3:
                p.op("vector", lambda e: e.tensor_copy(
                    af(c, AF_ROWS, 1024)[0:1, :], psY[64:65, 0:1024]), sem="dve")
                p.wait("sync", "dve")
                dstden = afr(c, ch * 16 + g * 4, ch * 16 + g * 4 + 4, AF_DEN, 256)
                srcden = af(c, AF_ROWS, 1024)[0:1, :].rearrange("p (a b) -> p a b", b=256)
                p.op("sync", lambda e, dstden=dstden, srcden=srcden: e.dma_start(
                    out=dstden, in_=srcden), sem="gd", n=16)

            first_state = True
            for tk in range(2 if SCAN_PARTS >= 4 else 0):
                p.wait("tensor", "dve")
                for hp in range(4):
                    p.op("tensor", lambda e, hp=hp, g=g, tq0=tq0, tk=tk: e.matmul(
                        pss[hp][:, 0:153],
                        ab(c, AB_KF + g * T + tq0 + tk * 128, 128)[32 * hp:32 * hp + 32, :],
                        c["lselt"][32 * hp:32 * hp + 32, 0:153],
                        start=True, stop=True, tile_position=(32 * hp, 0)), sem="pe")
                p.wait("vector", "pe")
                for hp in range(4):
                    p.op("vector", lambda e, hp=hp: e.tensor_copy(
                        ab(c, AB_STG + hp * 153, 153), pss[hp][:, 0:153]), sem="dve")
                p.wait("tensor", "dve")
                for hp in range(4):
                    p.op("tensor", lambda e, hp=hp, g=g, tq0=tq0, tk=tk: e.matmul(
                        pss[hp][:, 0:153],
                        ab(c, AB_KF + g * T + tq0 + tk * 128, 128)[32 * hp:32 * hp + 32, :],
                        c["rselt"][32 * hp:32 * hp + 32, 0:153],
                        start=True, stop=True, tile_position=(32 * hp, 0)), sem="pe")
                p.wait("vector", "pe")
                for hp in range(4):
                    p.op("vector", lambda e, hp=hp: e.tensor_mul(
                        ab(c, AB_PHIK + hp * 153, 153),
                        ab(c, AB_STG + hp * 153, 153), pss[hp][:, 0:153]), sem="dve")
                p.wait("tensor", "dve")
                for hp in range(4):
                    h = 4 * g + hp
                    p.op("tensor", lambda e, hp=hp, h=h, ch=ch, tk=tk, fs=first_state: e.matmul(
                        psS[:, hp * 65:hp * 65 + 65],
                        ab(c, AB_PHIK + hp * 153, 128),
                        ab(c, AB_VT + (2 * ch + tk) * 1040 + h * 65, 65),
                        start=(fs and hp == 0), stop=(tk == 1 and hp == 3)), sem="pe")
                    p.op("tensor", lambda e, hp=hp, h=h, ch=ch, tk=tk, fs=first_state: e.matmul(
                        psSB[0:25, hp * 65:hp * 65 + 65],
                        ab(c, AB_PHIK + hp * 153 + 128, 25),
                        ab(c, AB_VT + (2 * ch + tk) * 1040 + h * 65, 65),
                        start=(fs and hp == 0), stop=(tk == 1 and hp == 3)), sem="pe")
                first_state = False
            if SCAN_PARTS < 5:
                continue
            p.wait("vector", "pe")
            p.op("vector", lambda e, g=g: e.tensor_add(
                af(c, AF_SSB + g * 260, 260), af(c, AF_SSB + g * 260, 260),
                psS[:, 0:260]), sem="dve")
            p.op("vector", lambda e, g=g: e.tensor_add(
                c["ssbB"][0:25, g * 260:g * 260 + 260],
                c["ssbB"][0:25, g * 260:g * 260 + 260],
                psSB[0:25, 0:260]), sem="dve")
            p.op("vector", lambda e, g=g: e.tensor_copy(
                c["sbf"][:, g * 260:g * 260 + 260], af(c, AF_SSB + g * 260, 260)), sem="dve")
            p.op("vector", lambda e, g=g: e.tensor_copy(
                c["sbfB"][0:25, g * 260:g * 260 + 260],
                c["ssbB"][0:25, g * 260:g * 260 + 260]), sem="dve")


def emit_handoff(c):
    p = c["p"]
    p.wait("sync", "dve")
    p.dma("sync", c["sagi"][0:128, :], af(c, AF_SSB, 1040), "gd")
    p.dma("sync", c["sagi"][128:160, :], c["ssbB"][:, :], "gd")
    p.wait("gpsimd", "gd")
    p.op("gpsimd", lambda e: e.collective_compute(
        "AllGather", ALU.bypass, replica_groups=[list(range(NCORES))],
        ins=[c["sagi"].opt()], outs=[c["sago"].opt()]), sem="cc")
    p.wait("sync", "cc")
    p.op("vector", lambda e: e.memset(af(c, AF_S1, 1040), 0.0), sem="dve")
    p.op("vector", lambda e: e.memset(afr(c, 32, 64, AF_S2, 1040), 0.0), sem="dve")
    for r in range(NCORES):
        p.wait("sync", "dve")
        p.dma("sync", af(c, AF_RS, 1040), c["sago"][r * 160:r * 160 + 128, :], "ld")
        p.dma("sync", afr(c, 32, 64, AF_LNR, 1040), c["sago"][r * 160 + 128:(r + 1) * 160, :], "ld")
        p.wait("vector", "ld")
        p.op("vector", lambda e, r=r: e.scalar_tensor_tensor(
            out=af(c, AF_S1, 1040), in0=af(c, AF_RS, 1040),
            scalar=c["selmt"][:, r:r + 1], in1=af(c, AF_S1, 1040),
            op0=ALU.mult, op1=ALU.add), sem="dve")
        p.op("vector", lambda e, r=r: e.scalar_tensor_tensor(
            out=afr(c, 32, 64, AF_S2, 1040), in0=afr(c, 32, 64, AF_LNR, 1040),
            scalar=c["selmt"][32:64, r:r + 1], in1=afr(c, 32, 64, AF_S2, 1040),
            op0=ALU.mult, op1=ALU.add), sem="dve")
    p.op("vector", lambda e: e.tensor_copy(c["spv"][:, :], af(c, AF_S1, 1040)), sem="dve")
    p.op("vector", lambda e: e.tensor_copy(c["spvB"][:, :], afr(c, 32, 64, AF_S2, 1040)), sem="dve")


def emit_correction(c):
    p = c["p"]
    psA, psB, psC, psD, psY = c["psA"], c["psB"], c["psC"], c["psD"], c["psY"]
    pss = (psA, psB, psC, psD)
    for g in range(4):
        for c2 in range(TC):
            tq0 = c2 * 512
            p.wait("tensor", "dve")
            for hp in range(4):
                p.op("tensor", lambda e, hp=hp, g=g, tq0=tq0: e.matmul(
                    pss[hp][:, 0:512],
                    c["lselt"][32 * hp:32 * hp + 32, 0:128],
                    ab(c, AB_QF + g * T + tq0, 512)[32 * hp:32 * hp + 32, :],
                    start=True, stop=True, tile_position=(32 * hp, 0)), sem="pe")
            p.wait("vector", "pe")
            for hp in range(4):
                p.op("vector", lambda e, hp=hp: e.tensor_copy(
                    ab(c, AB_STG + hp * 512, 512), pss[hp][:, 0:512]), sem="dve")
            p.wait("tensor", "dve")
            for hp in range(4):
                p.op("tensor", lambda e, hp=hp, g=g, tq0=tq0: e.matmul(
                    pss[hp][:, 0:512],
                    c["rselt"][32 * hp:32 * hp + 32, 0:128],
                    ab(c, AB_QF + g * T + tq0, 512)[32 * hp:32 * hp + 32, :],
                    start=True, stop=True, tile_position=(32 * hp, 0)), sem="pe")
            p.wait("vector", "pe")
            for hp in range(4):
                p.op("vector", lambda e, hp=hp: e.tensor_mul(
                    ab(c, AB_PHIQA + hp * 512, 512),
                    ab(c, AB_STG + hp * 512, 512), pss[hp][:, 0:512]), sem="dve")
            p.wait("tensor", "dve")
            for hp in range(4):
                p.op("tensor", lambda e, hp=hp, g=g, tq0=tq0: e.matmul(
                    pss[hp][0:25, 0:512],
                    c["lselt"][32 * hp:32 * hp + 32, 128:153],
                    ab(c, AB_QF + g * T + tq0, 512)[32 * hp:32 * hp + 32, :],
                    start=True, stop=True, tile_position=(32 * hp, 0)), sem="pe")
            p.wait("vector", "pe")
            for hp in range(4):
                p.op("vector", lambda e, hp=hp: e.tensor_copy(
                    abr(c, 0, 25, AB_STG + hp * 512, 512), pss[hp][0:25, 0:512]), sem="dve")
            p.wait("tensor", "dve")
            for hp in range(4):
                p.op("tensor", lambda e, hp=hp, g=g, tq0=tq0: e.matmul(
                    pss[hp][0:25, 0:512],
                    c["rselt"][32 * hp:32 * hp + 32, 128:153],
                    ab(c, AB_QF + g * T + tq0, 512)[32 * hp:32 * hp + 32, :],
                    start=True, stop=True, tile_position=(32 * hp, 0)), sem="pe")
            p.wait("vector", "pe")
            for hp in range(4):
                p.op("vector", lambda e, hp=hp: e.tensor_mul(
                    abr(c, 0, 25, AB_PHIQB + hp * 512, 512),
                    abr(c, 0, 25, AB_STG + hp * 512, 512), pss[hp][0:25, 0:512]), sem="dve")
            for pr in range(2):
                p.wait("tensor", "dve")
                if pr == 1:
                    p.wait("tensor", "gd")
                for hh in range(2):
                    hp = pr * 2 + hh
                    h = 4 * g + hp
                    p.op("tensor", lambda e, h=h, hp=hp, hh=hh: e.matmul(
                        psY[0:65, hh * 512:hh * 512 + 512],
                        c["spv"][:, h * 65:h * 65 + 65],
                        ab(c, AB_PHIQA + hp * 512, 512),
                        start=True, stop=False), sem="pe")
                    p.op("tensor", lambda e, h=h, hp=hp, hh=hh: e.matmul(
                        psY[0:65, hh * 512:hh * 512 + 512],
                        c["spvB"][0:25, h * 65:h * 65 + 65],
                        abr(c, 0, 25, AB_PHIQB + hp * 512, 512),
                        start=False, stop=True), sem="pe")
                p.wait("vector", "pe")
                p.wait("vector", "gd")
                for hh in range(2):
                    hp = pr * 2 + hh
                    h = 4 * g + hp
                    ysl = abr(c, (h % 2) * 64, (h % 2) * 64 + 64, AB_YF + (h // 2) * T + tq0, 512)
                    p.op("vector", lambda e, ysl=ysl, hh=hh: e.tensor_add(
                        ysl, ysl, psY[0:64, hh * 512:hh * 512 + 512]), sem="dve")
                p.op("vector", lambda e: e.tensor_copy(
                    af(c, AF_ROWS, 1024)[0:1, :], psY[64:65, 0:1024]), sem="dve")
                p.wait("gpsimd", "dve")
                for hh in range(2):
                    hp = pr * 2 + hh
                    for cc2 in range(2):
                        row = (2 * c2 + cc2) * 16 + g * 4 + hp
                        p.op("gpsimd", lambda e, row=row, hh=hh, cc2=cc2: e.dma_start(
                            out=afr(c, row, row + 1, AF_DEN, 256),
                            in_=af(c, AF_ROWS + hh * 512 + cc2 * 256, 256)[0:1, :],
                            accum_op=ALU.add), sem="gd", n=16)


def emit_norm_wo_resid(c):
    p = c["p"]
    psA, psB, psC = c["psA"], c["psB"], c["psC"]
    p.wait("vector", "gd")
    p.op("vector", lambda e: e.tensor_scalar(af(c, AF_DEN, 256), af(c, AF_DEN, 256),
                                             1e-12, None, op0=ALU.add), sem="dve")
    p.op("vector", lambda e: e.reciprocal(af(c, AF_DEN, 256), af(c, AF_DEN, 256)), sem="dve")
    p.wait("sync", "gd")
    for fc in range(FC):
        for tc in range(TC):
            p.wait("sync", "dve")
            p.wait("sync", "pe")
            for hh in range(2):
                h = 2 * fc + hh
                for cc2 in range(2):
                    chv = 2 * tc + cc2
                    row = chv * 16 + (h // 4) * 4 + (h % 4)
                    p.dma("sync", af(c, AF_ROWS + hh * 512 + cc2 * 256, 256)[0:1, :],
                          afr(c, row, row + 1, AF_DEN, 256), "ld")
            p.wait("tensor", "ld")
            p.wait("tensor", "dve")
            p.op("tensor", lambda e: e.matmul(
                psC[0:64, 0:512], c["ones1"][0:1, 0:64],
                af(c, AF_ROWS, 512)[0:1, :], start=True, stop=True), sem="pe")
            p.op("tensor", lambda e: e.matmul(
                psC[64:128, 0:512], c["ones1"][0:1, 0:64],
                af(c, AF_ROWS + 512, 512)[0:1, :], start=True, stop=True), sem="pe")
            p.wait("vector", "pe")
            ysl = ab(c, AB_YF + fc * T + tc * 512, 512)
            p.op("vector", lambda e, ysl=ysl: e.tensor_mul(ysl, ysl, psC[:, 0:512]), sem="dve")
    WO = AB_QF
    p.wait("gpsimd", "dve")
    for fcw in range(FC):
        p.dma("gpsimd", ab(c, WO + fcw * 1024, 1024),
              c["wo"][fcw * 128:(fcw + 1) * 128, :], "wld")
    p.wait("tensor", "wld")
    for oc in range(FC):
        for tc in range(TC):
            ps = psA if (oc * TC + tc) % 2 == 0 else psB
            p.wait("tensor", "dve")
            for hd in range(FC):
                p.op("tensor", lambda e, ps=ps, oc=oc, tc=tc, hd=hd: e.matmul(
                    ps[:, 0:512],
                    ab(c, WO + hd * 1024 + oc * 128, 128),
                    ab(c, AB_YF + hd * T + tc * 512, 512),
                    start=(hd == 0), stop=(hd == FC - 1)), sem="pe")
            slot = (oc * TC + tc) % 2
            xs = af(c, AF_S1 + slot * 512, 512)
            res = af(c, AF_S2 + slot * 512, 512)
            p.wait("sync", "dve")
            p.dma("sync", xs, c["x_fm"][oc * 128:(oc + 1) * 128, tc * 512:(tc + 1) * 512], "ld")
            p.wait("vector", "ld")
            p.wait("vector", "pe")
            p.wait("vector", "gd")
            p.op("vector", lambda e, ps=ps, xs=xs, res=res: e.tensor_add(res, xs, ps[:, 0:512]), sem="dve")
            p.wait("sync", "dve")
            p.dma("sync", c["x1d"][oc * 128:(oc + 1) * 128, tc * 512:(tc + 1) * 512], res, "gd")


def emit_xg(c):
    p = c["p"]
    pst = c["psS"][:, 0:64].bitcast(BF16)
    for i in range(TT):
        for fc in range(FC):
            p.wait("tensor", "dve")
            if fc == 0:
                p.wait("tensor", "gd")
            p.op("tensor", lambda e, i=i, fc=fc, pst=pst: e.transpose(
                pst, ab(c, AB_XN + fc * T + i * 128, 128), c["idbt"][:, :]), sem="pe")
            p.wait("vector", "pe")
            p.op("vector", lambda e, fc=fc, pst=pst: e.tensor_copy(
                ab(c, AB_BUF_T + fc * 128, 128), pst), sem="dve")
        p.wait("sync", "dve")
        p.dma("sync", c["xgi"][i * 128:(i + 1) * 128, :], ab(c, AB_BUF_T, 1024), "gd")
    p.wait("gpsimd", "gd")
    p.op("gpsimd", lambda e: e.collective_compute(
        "AllGather", ALU.bypass, replica_groups=[list(range(NCORES))],
        ins=[c["xgi"].opt()], outs=[c["xgo"].opt()]), sem="cc")


def emit_routing(c):
    p = c["p"]
    psA, psS = c["psA"], c["psS"]
    GW = AB_EW
    p.wait("gpsimd", "pe")
    for fc in range(FC):
        p.dma("gpsimd", ab(c, GW + fc * 64, 64),
              c["gate"][fc * 128:(fc + 1) * 128, :], "wld")
    p.op("vector", lambda e: e.memset(afr(c, 0, 64, AF_CAR, 1), 0.0), sem="dve")
    p.op("vector", lambda e: e.memset(c["onesf"][:, :], 1.0), sem="dve")
    p.wait("tensor", "wld")
    for i in range(TT):
        p.wait("tensor", "dve")
        for fc in range(FC):
            p.op("tensor", lambda e, i=i, fc=fc: e.matmul(
                psA[:, 0:64],
                ab(c, AB_XN + fc * T + i * 128, 128),
                ab(c, GW + fc * 64, 64),
                start=(fc == 0), stop=(fc == FC - 1)), sem="pe")
        p.wait("vector", "pe")
        p.op("vector", lambda e: e.tensor_copy(af(c, AF_RS, 64), psA[:, 0:64]), sem="dve")
        p.op("vector", lambda e: e.max(out=af(c, AF_RS + 64, 8), in_=af(c, AF_RS, 64)), sem="dve")
        p.op("vector", lambda e: e.max_index(out=c["idx8"][:, :], in_max=af(c, AF_RS + 64, 8),
                                             in_values=af(c, AF_RS, 64)), sem="dve")
        p.op("vector", lambda e: e.tensor_copy(af(c, AF_RS + 96, 8), c["idx8"][:, :]), sem="dve")
        p.op("vector", lambda e: e.tensor_scalar(af(c, AF_RS + 72, 1), af(c, AF_RS + 64, 1),
                                                 -1.0, None, op0=ALU.mult), sem="dve")
        p.wait("scalar", "dve")
        p.op("scalar", lambda e: e.activation(out=af(c, AF_RS + 80, 8), in_=af(c, AF_RS + 64, 8),
                                              func=AF_.Exp, bias=af(c, AF_RS + 72, 1)), sem="act")
        p.wait("vector", "act")
        p.op("vector", lambda e: e.reduce_sum(af(c, AF_RS + 73, 1), af(c, AF_RS + 80, 8), axis=AX.X), sem="dve")
        p.op("vector", lambda e: e.reciprocal(af(c, AF_RS + 74, 1), af(c, AF_RS + 73, 1)), sem="dve")
        p.op("vector", lambda e: e.tensor_scalar(af(c, AF_RS + 88, 8), af(c, AF_RS + 80, 8),
                                                 af(c, AF_RS + 74, 1), None, op0=ALU.mult), sem="dve")
        p.op("vector", lambda e, i=i: e.tensor_copy(af(c, AF_RK + i * 24, 8), af(c, AF_RS + 96, 8)), sem="dve")
        p.op("vector", lambda e, i=i: e.tensor_copy(af(c, AF_RK + i * 24 + 8, 8), af(c, AF_RS + 88, 8)), sem="dve")
        io64 = c["iota64t"][:, :].rearrange("p (e z) -> p e z", z=1).to_broadcast([128, 64, 8])
        idv = af(c, AF_RS + 96, 8).rearrange("p (z j) -> p z j", z=1).to_broadcast([128, 64, 8])
        p.op("vector", lambda e, io64=io64, idv=idv: e.tensor_tensor(
            af(c, AF_RS + 512, 512).rearrange("p (ee j) -> p ee j", j=8), io64, idv,
            op=ALU.is_equal), sem="dve")
        wvv = af(c, AF_RS + 88, 8).rearrange("p (z j) -> p z j", z=1).to_broadcast([128, 64, 8])
        p.op("vector", lambda e, wvv=wvv: e.tensor_tensor(
            af(c, AF_RS + 1024, 512).rearrange("p (ee j) -> p ee j", j=8),
            af(c, AF_RS + 512, 512).rearrange("p (ee j) -> p ee j", j=8), wvv,
            op=ALU.mult), sem="dve")
        p.op("vector", lambda e: e.reduce_sum(
            af(c, AF_RS + 128, 64), af(c, AF_RS + 1024, 512).rearrange("p (ee j) -> p ee j", j=8),
            axis=AX.X), sem="dve")
        p.op("vector", lambda e: e.tensor_scalar(af(c, AF_RS + 192, 64), af(c, AF_RS + 128, 64),
                                                 0.0, None, op0=ALU.is_gt), sem="dve")
        p.wait("tensor", "dve")
        p.op("tensor", lambda e: e.transpose(psS[0:64, 0:128], af(c, AF_RS + 192, 64),
                                             c["idft"][:, :]), sem="pe")
        p.wait("vector", "pe")
        p.op("vector", lambda e: e.tensor_copy(afr(c, 0, 64, AF_RS + 256, 128), psS[0:64, 0:128]), sem="dve")
        p.op("vector", lambda e, i=i: e.tensor_tensor_scan(
            out=c["cum"][:, i * 128:(i + 1) * 128],
            data0=c["onesf"][:, :], data1=afr(c, 0, 64, AF_RS + 256, 128),
            initial=afr(c, 0, 64, AF_CAR, 1), op0=ALU.mult, op1=ALU.add), sem="dve")
        p.op("vector", lambda e, i=i: e.tensor_copy(
            afr(c, 0, 64, AF_CAR, 1), c["cum"][:, i * 128 + 127:(i + 1) * 128]), sem="dve")
    p.wait("sync", "dve")
    p.dma("sync", c["cnti"][:, :], afr(c, 0, 64, AF_CAR, 1), "gd")
    p.wait("gpsimd", "gd")
    p.op("gpsimd", lambda e: e.collective_compute(
        "AllGather", ALU.bypass, replica_groups=[list(range(NCORES))],
        ins=[c["cnti"].opt()], outs=[c["cnto"].opt()]), sem="cc")
    p.wait("sync", "cc")
    p.dma("sync", afr(c, 0, 8, AF_RS, 64),
          c["cnto"].rearrange("(r e) o -> r (e o)", e=NE), "ld")
    p.wait("tensor", "ld")
    p.wait("tensor", "dve")
    p.op("tensor", lambda e: e.matmul(psS[0:64, 128:129], afr(c, 0, 8, AF_RS, 64),
                                      c["rankmt"][:, :], start=True, stop=True), sem="pe")
    p.wait("vector", "pe")
    p.op("vector", lambda e: e.tensor_copy(afr(c, 0, 64, AF_CAR + 1, 1), psS[0:64, 128:129]), sem="dve")
    p.op("vector", lambda e: e.tensor_scalar(afr(c, 0, 64, AF_CAR + 2, 1),
                                             afr(c, 0, 64, AF_CAR + 1, 1),
                                             -1.0, None, op0=ALU.add), sem="dve")
    p.op("vector", lambda e: e.memset(af(c, AF_S1, 513), -1.0), sem="dve")
    p.wait("gpsimd", "dve")
    p.dma("gpsimd", c["recm"].flatten().rearrange("(p a) -> p a", p=128),
          af(c, AF_S1, 513), "gd")
    p.wait("gpsimd", "gd")
    for i in range(TT):
        p.wait("vector", "dve")
        p.wait("vector", "gd")
        p.op("vector", lambda e, i=i: e.tensor_scalar(
            afr(c, 0, 64, AF_S2, 128), c["cum"][:, i * 128:(i + 1) * 128],
            afr(c, 0, 64, AF_CAR + 2, 1), None, op0=ALU.add), sem="dve")
        p.wait("tensor", "dve")
        p.op("tensor", lambda e: e.transpose(psS[:, 0:64], afr(c, 0, 64, AF_S2, 128),
                                             c["idft"][0:64, 0:64]), sem="pe")
        p.wait("vector", "pe")
        p.op("vector", lambda e: e.tensor_copy(af(c, AF_S2 + 128, 64), psS[:, 0:64]), sem="dve")
        io64 = c["iota64t"][:, :].rearrange("p (z e) -> p z e", z=1).to_broadcast([128, 8, 64])
        idv = af(c, AF_RK + i * 24, 8).rearrange("p (j z) -> p j z", z=1).to_broadcast([128, 8, 64])
        p.op("vector", lambda e, io64=io64, idv=idv: e.tensor_tensor(
            af(c, AF_RS + 512, 512).rearrange("p (j ee) -> p j ee", ee=64), io64, idv,
            op=ALU.is_equal), sem="dve")
        pos = af(c, AF_S2 + 128, 64).rearrange("p (z e) -> p z e", z=1).to_broadcast([128, 8, 64])
        p.op("vector", lambda e, pos=pos: e.tensor_tensor(
            af(c, AF_RS + 1024, 512).rearrange("p (j ee) -> p j ee", ee=64),
            af(c, AF_RS + 512, 512).rearrange("p (j ee) -> p j ee", ee=64), pos,
            op=ALU.mult), sem="dve")
        p.op("vector", lambda e, i=i: e.reduce_sum(
            af(c, AF_RK + i * 24 + 16, 8),
            af(c, AF_RS + 1024, 512).rearrange("p (j ee) -> p j ee", ee=64), axis=AX.X), sem="dve")
        sl = af(c, AF_RK + i * 24 + 16, 8)
        p.op("vector", lambda e, sl=sl: e.tensor_scalar(af(c, AF_RS, 8), sl, 512.0, None,
                                                        op0=ALU.is_ge), sem="dve")
        p.op("vector", lambda e, i=i: e.tensor_scalar(
            af(c, AF_RS + 8, 8), af(c, AF_RK + i * 24, 8), 513.0, None, op0=ALU.mult), sem="dve")
        p.op("vector", lambda e, sl=sl: e.tensor_add(af(c, AF_RS + 8, 8), af(c, AF_RS + 8, 8), sl), sem="dve")
        p.op("vector", lambda e: e.scalar_tensor_tensor(
            out=af(c, AF_RS + 16, 8), in0=af(c, AF_RS, 8), scalar=BIG,
            in1=af(c, AF_RS + 8, 8), op0=ALU.mult, op1=ALU.add), sem="dve")
        p.op("vector", lambda e: e.tensor_copy(c["sidx"][:, :], af(c, AF_RS + 16, 8)), sem="dve")
        p.op("vector", lambda e, i=i: e.tensor_scalar(
            af(c, AF_RS + 24, 8), af(c, AF_RK + i * 24, 8), 512.0, None, op0=ALU.mult), sem="dve")
        p.op("vector", lambda e, sl=sl: e.tensor_add(af(c, AF_RS + 24, 8), af(c, AF_RS + 24, 8), sl), sem="dve")
        p.op("vector", lambda e: e.scalar_tensor_tensor(
            out=af(c, AF_RS + 32, 8), in0=af(c, AF_RS, 8), scalar=BIG,
            in1=af(c, AF_RS + 24, 8), op0=ALU.mult, op1=ALU.add), sem="dve")
        p.op("vector", lambda e, i=i: e.tensor_copy(c["widx"][:, i * 8:(i + 1) * 8],
                                                    af(c, AF_RS + 32, 8)), sem="dve")
        tok = c["tokidt"][:, i:i + 1].rearrange("p (o z) -> p o z", z=1).to_broadcast([128, 8, 1])
        rec = af(c, AF_RS + 40, 16).rearrange("p (j two) -> p j two", two=2)
        p.op("vector", lambda e, tok=tok, rec=rec: e.tensor_copy(rec[:, :, 0:1], tok), sem="dve")
        wk = af(c, AF_RK + i * 24 + 8, 8).rearrange("p (j z) -> p j z", z=1)
        p.op("vector", lambda e, wk=wk, rec=rec: e.tensor_copy(rec[:, :, 1:2], wk), sem="dve")
        p.wait("gpsimd", "dve")
        for j in range(TOPK):
            p.op("gpsimd", lambda e, j=j: e.indirect_dma_start(
                out=c["recm"][:, :],
                out_offset=bass.IndirectOffsetOnAxis(ap=c["sidx"][:, j:j + 1], axis=0),
                in_=af(c, AF_RS + 40 + 2 * j, 2),
                in_offset=None, bounds_check=c["bregs"][NE * 513 - 1], oob_is_err=False), sem="gd", n=16)
    p.wait("gpsimd", "gd")
    p.op("gpsimd", lambda e: e.collective_compute(
        "AllReduce", ALU.max, replica_groups=[list(range(NCORES))],
        ins=[c["recm"].opt()], outs=[c["recmr"].opt()]), sem="cc")


def emit_experts(c):
    p = c["p"]
    psA, psB, psC, psD = c["psA"], c["psB"], c["psC"], c["psD"]
    p.wait("gpsimd", "cc")
    pst = c["psS"][:, 0:64].bitcast(BF16)
    for e_i in range(EPC):
        p.wait("gpsimd", "dve")
        for gg in range(4):
            p.op("gpsimd", lambda e, e_i=e_i, gg=gg: e.indirect_dma_start(
                out=af(c, AF_S1 + gg * 2, 2), out_offset=None,
                in_=c["recmr"][:, :],
                in_offset=bass.IndirectOffsetOnAxis(
                    ap=c["recidxt"][:, e_i * 4 + gg:e_i * 4 + gg + 1], axis=0),
                bounds_check=c["bregs"][NE * 513 - 1], oob_is_err=False), sem="gd", n=16)
        p.wait("vector", "gd")
        tv = af(c, AF_S1, 8).rearrange("p (g two) -> p g two", two=2)[:, :, 0:1]
        wv = af(c, AF_S1, 8).rearrange("p (g two) -> p g two", two=2)[:, :, 1:2]
        p.op("vector", lambda e, tv=tv: e.tensor_copy(
            af(c, AF_S2, 4).rearrange("p (g z) -> p g z", z=1), tv), sem="dve")
        p.op("vector", lambda e, wv=wv: e.tensor_copy(
            af(c, AF_S2 + 8, 4).rearrange("p (g z) -> p g z", z=1), wv), sem="dve")
        p.op("vector", lambda e: e.tensor_scalar(af(c, AF_S2 + 4, 4), af(c, AF_S2, 4),
                                                 0.0, None, op0=ALU.is_lt), sem="dve")
        p.op("vector", lambda e: e.scalar_tensor_tensor(
            out=af(c, AF_S2 + 16, 4), in0=af(c, AF_S2 + 4, 4), scalar=BIG,
            in1=af(c, AF_S2, 4), op0=ALU.mult, op1=ALU.add), sem="dve")
        p.op("vector", lambda e: e.tensor_copy(c["sidx"][:, 0:4], af(c, AF_S2 + 16, 4)), sem="dve")
        p.wait("gpsimd", "dve")
        p.wait("gpsimd", "pe")
        for gg in range(4):
            p.op("gpsimd", lambda e, gg=gg: e.memset(ab(c, AB_BUF_T + gg * 1024, 1024), 0.0), sem="gd", n=1)
        for gg in range(4):
            p.op("gpsimd", lambda e, gg=gg: e.indirect_dma_start(
                out=ab(c, AB_BUF_T + gg * 1024, 1024), out_offset=None,
                in_=c["xgo"][:, :],
                in_offset=bass.IndirectOffsetOnAxis(ap=c["sidx"][:, gg:gg + 1], axis=0),
                bounds_check=c["bregs"][NCORES * T - 1], oob_is_err=False), sem="gd", n=16)
        for hc in range(FC):
            p.dma("gpsimd", ab(c, AB_EW + hc * 1024, 1024),
                  c["egu"][e_i, hc * 128:(hc + 1) * 128, :], "wld")
        p.wait("tensor", "gd")
        for gg in range(4):
            for hc in range(FC):
                p.wait("tensor", "dve")
                p.op("tensor", lambda e, gg=gg, hc=hc, pst=pst: e.transpose(
                    pst, ab(c, AB_BUF_T + gg * 1024 + hc * 128, 128), c["idbt"][:, :]), sem="pe")
                p.wait("vector", "pe")
                p.op("vector", lambda e, gg=gg, hc=hc, pst=pst: e.tensor_copy(
                    ab(c, AB_BUF_FM + hc * 512 + gg * 128, 128), pst), sem="dve")
        p.wait("tensor", "wld")
        for guc in range(8):
            ps = psA if guc % 2 == 0 else psB
            p.wait("tensor", "act")
            p.wait("tensor", "dve")
            for hc in range(FC):
                p.op("tensor", lambda e, ps=ps, guc=guc, hc=hc: e.matmul(
                    ps[:, 0:512],
                    ab(c, AB_EW + hc * 1024 + guc * 128, 128),
                    ab(c, AB_BUF_FM + hc * 512, 512),
                    start=(hc == 0), stop=(hc == FC - 1)), sem="pe")
            if guc < 4:
                p.wait("scalar", "pe")
                p.op("scalar", lambda e, ps=ps, guc=guc: e.activation(
                    out=ab(c, AB_ACT + guc * 512, 512), in_=ps[:, 0:512],
                    func=AF_.Silu), sem="act")
            else:
                p.wait("vector", "pe")
                p.op("vector", lambda e, ps=ps, guc=guc: e.tensor_copy(
                    ab(c, AB_ACT4 + (guc - 4) * 512, 512), ps[:, 0:512]), sem="dve")
        p.wait("vector", "act")
        for ic in range(4):
            p.op("vector", lambda e, ic=ic: e.tensor_mul(
                ab(c, AB_ACT + ic * 512, 512), ab(c, AB_ACT + ic * 512, 512),
                ab(c, AB_ACT4 + ic * 512, 512)), sem="dve")
        p.wait("gpsimd", "pe")
        for icw in range(4):
            p.dma("gpsimd", ab(c, AB_EW + icw * 1024, 1024),
                  c["edw"][e_i, icw * 128:(icw + 1) * 128, :], "wld")
        p.wait("tensor", "wld")
        for sg in range(4):
            for n in range(2):
                ps = psC if (sg * 2 + n) % 2 == 0 else psD
                p.wait("tensor", "dve")
                p.wait("tensor", "gd")
                for icm in range(4):
                    p.op("tensor", lambda e, ps=ps, sg=sg, n=n, icm=icm: e.matmul(
                        ps[:, 0:512],
                        ab(c, AB_ACT + icm * 512 + sg * 128, 128),
                        ab(c, AB_EW + icm * 1024 + n * 512, 512),
                        start=(icm == 0), stop=(icm == 3)), sem="pe")
                p.wait("vector", "pe")
                p.wait("vector", "gd")
                p.op("vector", lambda e, ps=ps, sg=sg: e.tensor_scalar(
                    ab(c, AB_WOUT, 512), ps[:, 0:512],
                    af(c, AF_S2 + 8 + sg, 1), None, op0=ALU.mult), sem="dve")
                p.wait("sync", "dve")
                p.dma("sync", c["woutl"][e_i * 512 + sg * 128: e_i * 512 + (sg + 1) * 128,
                                         n * 512:(n + 1) * 512], ab(c, AB_WOUT, 512), "gd")
    p.wait("gpsimd", "gd")
    p.op("gpsimd", lambda e: e.collective_compute(
        "AllGather", ALU.bypass, replica_groups=[list(range(NCORES))],
        ins=[c["woutl"].opt()], outs=[c["woutg"].opt()]), sem="cc")


def emit_shared(c):
    p = c["p"]
    psA, psB, psC, psD = c["psA"], c["psB"], c["psC"], c["psD"]
    for fc in range(FC):
        p.dma("gpsimd", c["sgwb"][:, fc:fc + 1], c["sgw"][fc * 128:(fc + 1) * 128, :], "wld")
    p.wait("tensor", "wld")
    for i in range(TT):
        p.wait("tensor", "act")
        p.wait("tensor", "dve")
        for fc in range(FC):
            p.op("tensor", lambda e, i=i, fc=fc: e.matmul(
                psC[:, 0:1],
                ab(c, AB_XN + fc * T + i * 128, 128),
                c["sgwb"][:, fc:fc + 1],
                start=(fc == 0), stop=(fc == FC - 1)), sem="pe")
        p.wait("scalar", "pe")
        p.op("scalar", lambda e, i=i: e.activation(
            out=af(c, AF_RK + 400 + i, 1), in_=psC[:, 0:1], func=AF_.Sigmoid), sem="act")
    p.wait("sync", "act")
    for i in range(TT):
        p.dma("sync", c["sgd"][i * 128:(i + 1) * 128].rearrange("(p o) -> p o", o=1),
              af(c, AF_RK + 400 + i, 1), "gd")
    p.wait("sync", "gd")
    p.dma("sync", af(c, AF_LNR, 2048)[0:1, :], c["sgd"].rearrange("(o a) -> o a", o=1), "ld")
    for i4 in range(4):
        for icl in range(4):
            ic4 = i4 * 4 + icl
            p.wait("gpsimd", "pe")
            for hc in range(FC):
                p.dma("gpsimd", ab(c, AB_EW + hc * 256, 128),
                      c["sgu"][hc * 128:(hc + 1) * 128, ic4 * 128:(ic4 + 1) * 128], "wld")
                p.dma("gpsimd", ab(c, AB_EW + hc * 256 + 128, 128),
                      c["sgu"][hc * 128:(hc + 1) * 128, ISH + ic4 * 128:ISH + (ic4 + 1) * 128], "wld")
            p.wait("tensor", "wld")
            for tc in range(TC):
                p.wait("tensor", "act")
                p.wait("tensor", "dve")
                for hc in range(FC):
                    p.op("tensor", lambda e, tc=tc, hc=hc: e.matmul(
                        psA[:, 0:512],
                        ab(c, AB_EW + hc * 256, 128),
                        ab(c, AB_XN + hc * T + tc * 512, 512),
                        start=(hc == 0), stop=(hc == FC - 1)), sem="pe")
                for hc in range(FC):
                    p.op("tensor", lambda e, tc=tc, hc=hc: e.matmul(
                        psB[:, 0:512],
                        ab(c, AB_EW + hc * 256 + 128, 128),
                        ab(c, AB_XN + hc * T + tc * 512, 512),
                        start=(hc == 0), stop=(hc == FC - 1)), sem="pe")
                p.wait("scalar", "pe")
                p.op("scalar", lambda e: e.activation(
                    out=ab(c, AB_WOUT, 512), in_=psA[:, 0:512], func=AF_.Silu), sem="act")
                p.wait("vector", "act")
                p.op("vector", lambda e, icl=icl, tc=tc: e.tensor_mul(
                    ab(c, AB_ACT4 + icl * 2048 + tc * 512, 512),
                    ab(c, AB_WOUT, 512), psB[:, 0:512]), sem="dve")
        p.wait("gpsimd", "pe")
        for icl in range(4):
            ic4 = i4 * 4 + icl
            p.dma("gpsimd", ab(c, AB_EW + 2048 + icl * 1024, 1024),
                  c["sdw"][ic4 * 128:(ic4 + 1) * 128, :], "wld")
        p.wait("tensor", "wld")
        p.wait("tensor", "dve")
        for hc in range(FC):
            for tc in range(TC):
                ps = psC if (hc * TC + tc) % 2 == 0 else psD
                p.wait("tensor", "dve")
                for icl in range(4):
                    p.op("tensor", lambda e, ps=ps, hc=hc, tc=tc, icl=icl: e.matmul(
                        ps[:, 0:512],
                        ab(c, AB_EW + 2048 + icl * 1024 + hc * 128, 128),
                        ab(c, AB_ACT4 + icl * 2048 + tc * 512, 512),
                        start=(icl == 0), stop=(icl == 3)), sem="pe")
                p.wait("vector", "pe")
                dst = ab(c, AB_SHACC + hc * T + tc * 512, 512)
                if i4 == 0:
                    p.op("vector", lambda e, ps=ps, dst=dst: e.tensor_copy(dst, ps[:, 0:512]), sem="dve")
                else:
                    p.op("vector", lambda e, ps=ps, dst=dst: e.tensor_add(dst, dst, ps[:, 0:512]), sem="dve")


COMB_PARTS = 3


def emit_combine(c):
    p = c["p"]
    psD, psS = c["psD"], c["psS"]
    p.wait("gpsimd", "cc")
    for tc in range(TC):
        p.wait("tensor", "dve")
        p.wait("tensor", "ld")
        p.op("tensor", lambda e, tc=tc: e.matmul(
            psD[:, 0:512], c["ones1"][0:1, 0:128],
            af(c, AF_LNR + tc * 512, 512)[0:1, :], start=True, stop=True), sem="pe")
        for ii in range(4):
            i = tc * 4 + ii
            slot_a = AF_S1 + (i % 2) * 1024
            p.wait("gpsimd", "dve")
            p.op("gpsimd", lambda e: e.memset(ab(c, AB_QF, 8192), 0.0), sem="gd", n=1)
            for j in range(TOPK):
                p.op("gpsimd", lambda e, i=i, j=j: e.indirect_dma_start(
                    out=ab(c, AB_QF + j * 1024, 1024), out_offset=None,
                    in_=c["woutg"][:, :],
                    in_offset=bass.IndirectOffsetOnAxis(
                        ap=c["widx"][:, i * 8 + j:i * 8 + j + 1], axis=0),
                    bounds_check=c["bregs"][NE * CAP - 1], oob_is_err=False), sem="gd", n=16)
            p.wait("vector", "gd")
            p.wait("vector", "pe")
            p.op("vector", lambda e, slot_a=slot_a: e.tensor_add(
                af(c, slot_a, 1024), ab(c, AB_QF, 1024), ab(c, AB_QF + 1024, 1024)), sem="dve")
            for j in range(2, TOPK):
                p.op("vector", lambda e, j=j, slot_a=slot_a: e.tensor_add(
                    af(c, slot_a, 1024), af(c, slot_a, 1024),
                    ab(c, AB_QF + j * 1024, 1024)), sem="dve")
            for fc in range(FC):
                p.wait("tensor", "ld")
                p.wait("tensor", "dve")
                p.op("tensor", lambda e, fc=fc, slot_a=slot_a: e.transpose(
                    psS[:, 0:128], af(c, slot_a + fc * 128, 128), c["idft"][:, :]), sem="pe")
                slot = fc % 2
                p.wait("sync", "dve")
                p.dma("sync", af(c, AF_S2 + slot * 512 + 128, 128),
                      c["x1d"][fc * 128:(fc + 1) * 128, i * 128:(i + 1) * 128], "ld")
                p.wait("vector", "pe")
                p.wait("vector", "ld")
                p.wait("vector", "st")
                p.op("vector", lambda e, fc=fc, i=i, slot=slot: e.tensor_mul(
                    af(c, AF_S2 + slot * 512, 128),
                    ab(c, AB_SHACC + fc * T + i * 128, 128),
                    psD[:, (i % 4) * 128:(i % 4) * 128 + 128]), sem="dve")
                p.op("vector", lambda e, slot=slot: e.tensor_add(
                    af(c, AF_S2 + slot * 512 + 256, 128),
                    af(c, AF_S2 + slot * 512, 128), psS[:, 0:128]), sem="dve")
                p.op("vector", lambda e, slot=slot: e.tensor_add(
                    af(c, AF_S2 + slot * 512 + 256, 128),
                    af(c, AF_S2 + slot * 512 + 256, 128),
                    af(c, AF_S2 + slot * 512 + 128, 128)), sem="dve")
                p.wait("sync", "dve")
                p.dma("sync", c["out_fm"][fc * 128:(fc + 1) * 128, i * 128:(i + 1) * 128],
                      af(c, AF_S2 + slot * 512 + 256, 128), "st")


def emit_dump(c, regions):
    p = c["p"]
    p.wait("gpsimd", "dve")
    p.wait("gpsimd", "pe")
    p.wait("gpsimd", "act")
    p.wait("gpsimd", "gd")
    p.wait("gpsimd", "wld")
    p.wait("sync", "dve")
    p.wait("sync", "pe")
    p.wait("sync", "gd")
    p.wait("sync", "act")
    p.wait("sync", "wld")
    p.wait("sync", "cc")
    if "ab" in regions:
        CHW = 8192
        for o in range(0, AB_N, CHW):
            w = min(CHW, AB_N - o)
            p.dma("gpsimd", c["dbg_ab"][:, o:o + w], c["AB"][:, o:o + w], "st")
    if "af" in regions:
        p.dma("sync", c["dbg_af"][:, :], c["AF"][:, :], "st")
    if "sag" in regions:
        p.dma("sync", c["dbg_sag"][0:128, :], af(c, AF_SSB, 1040), "st")
        p.dma("sync", c["dbg_sag"][128:160, :], c["ssbB"][:, :], "st")
    if "rec" in regions:
        p.dma("sync", c["dbg_rec"][:, :], c["recmr"][:, :], "st")
    if "cumd" in regions:
        p.dma("sync", c["dbg_cum"][:, :], c["cum"][:, :], "st")
    if "widxd" in regions:
        p.dma("sync", c["dbg_widx"][:, :], c["widx"][:, :], "st")
    if "woutd" in regions:
        for o in range(0, EPC * CAP, 512):
            p.dma("gpsimd", c["dbg_wout"][o:o + 512, :], c["woutl"][o:o + 512, :], "st")
    p.wait("gpsimd", "st")
    p.wait("sync", "st")



def make_inputs(inputs):
    x = np.asarray(inputs["L0_attn_hidden_states"])
    Wq = np.asarray(inputs["L0_attn_Wq"])
    Wk = np.asarray(inputs["L0_attn_Wk"])
    Wv = np.asarray(inputs["L0_attn_Wv"])
    Wo = np.asarray(inputs["L0_attn_Wo"])
    gate = np.asarray(inputs["L0_ffn_gate_w"])
    sgw = np.asarray(inputs["L0_ffn_shared_gate_w"])
    sgu = np.asarray(inputs["L0_ffn_shared_gate_up_w"])
    sdw = np.asarray(inputs["L0_ffn_shared_down_w"])
    egu = np.asarray(inputs["L0_ffn_expert_gate_up_w"])
    edw = np.asarray(inputs["L0_ffn_expert_down_w"])

    wqk = np.concatenate([_pad_qk_weight(Wq), _pad_qk_weight(Wk)], axis=1)
    wvT = np.ascontiguousarray(Wv.T)
    woT = np.ascontiguousarray(Wo.T)
    Ls, Rs = _build_selection()
    msk = _causal_masks()
    iota64 = np.broadcast_to(np.arange(NE, dtype=np.float32)[None, :], (128, NE)).copy()

    in_maps = []
    for core in range(NCORES):
        b, half = core // 2, core % 2
        shard = x[b, half * T:(half + 1) * T, :]
        x_fm = np.ascontiguousarray(shard.T)
        tokid = (core * T + np.arange(T, dtype=np.float32)).reshape(TT, 128).T.copy()
        rankm = (np.arange(8) < core).astype(np.float32).reshape(8, 1)
        selm_v = np.zeros(8, np.float32)
        if half == 1:
            selm_v[core - 1] = 1.0
        selm = np.broadcast_to(selm_v[None, :], (128, 8)).copy()
        ridx = np.zeros((128, EPC * 4), np.int32)
        pcol = np.arange(128, dtype=np.int32)
        for e_i in range(EPC):
            for g in range(4):
                ridx[:, e_i * 4 + g] = (core * EPC + e_i) * 513 + g * 128 + pcol
        in_maps.append({
            "x_fm": x_fm,
            "wqk": wqk, "wv": wvT, "wo": woT,
            "gate": gate, "sgw": sgw, "sgu": sgu, "sdw": sdw,
            "egu": np.ascontiguousarray(egu[core * EPC:(core + 1) * EPC]),
            "edw": np.ascontiguousarray(edw[core * EPC:(core + 1) * EPC]),
            "lsel": Ls.astype(ml_dtypes.bfloat16),
            "rsel": Rs.astype(ml_dtypes.bfloat16),
            "msk": msk.astype(ml_dtypes.bfloat16),
            "idf": np.eye(128, dtype=np.float32),
            "idb": np.eye(128, dtype=ml_dtypes.bfloat16),
            "iota64": iota64,
            "tokid": tokid,
            "rankm": rankm,
            "selm": selm,
            "recidx": ridx,
            "cvals": np.broadcast_to(
                np.array([FSC, FB, LNEPS, 0, 0, 0, 0, 0], np.float32)[None, :],
                (128, 8)).copy(),
        })
    return in_maps


def kernel(**inputs):
    in_maps = make_inputs(inputs)
    nc = build_program()
    res = run_bass_kernel_spmd(nc, in_maps, core_ids=list(range(NCORES)))
    out = np.empty((B, S, HID), np.float32)
    for core in range(NCORES):
        b, half = core // 2, core % 2
        out[b, half * T:(half + 1) * T, :] = res.results[core]["out_fm"].T
    return out


# version 02
# speedup vs baseline: 1.0237x; 1.0237x over previous
import math
from collections import defaultdict

import numpy as np
import ml_dtypes

import concourse.bass as bass
import concourse.mybir as mybir
from concourse.bass_utils import run_bass_kernel_spmd

F32 = mybir.dt.float32
BF16 = mybir.dt.bfloat16
I32 = mybir.dt.int32
U32 = mybir.dt.uint32
AF_ = mybir.ActivationFunctionType
ALU = mybir.AluOpType
AX = mybir.AxisListType

B, S, HID = 4, 4096, 1024
NCORES = 8
T = 2048
TT = 16
TC = 4
FC = 8
NH = 16
DPHI = 153
CH = 8
NE = 64
EPC = 8
CAP = 512
TOPK = 8
ISH = 2048
IEX = 512

RDs = 4.0
RRD = 2.0
R2 = math.sqrt(2.0)
FSC = 1.0 / math.sqrt(32.0)
FB = 3.0 * FSC
BIG = 1.0e9
LNEPS = 1e-5

AB_XN = 0
AB_QF = 16384
AB_KF = 24576
AB_VT = 32768
AB_YF = 49408
AB_N = 65792
AB_PHIQA = AB_XN + 0
AB_PHIQB = AB_XN + 2048
AB_PHIK = AB_XN + 4096
AB_A4 = AB_XN + 5120
AB_STG = AB_XN + 7168
AB_BUF_FM = AB_QF + 0
AB_BUF_T = AB_QF + 4096
AB_EW = AB_KF + 0
AB_ACT = AB_VT + 0
AB_WOUT = AB_VT + 2048
AB_ACT4 = AB_VT + 4096
AB_SHACC = AB_YF

AF_S1 = 0
AF_S2 = 2048
AF_DEN = 4096
AF_SSB = 4352
AF_CAR = 5392
AF_RS = 5408
AF_RK = 6944
AF_ROWS = 7456
AF_LNR = 8480
AF_N = 12608


def _build_selection():
    L = np.zeros((32, DPHI), np.float32)
    R = np.zeros((32, DPHI), np.float32)
    L[16, 0] = 1.0
    R[16, 0] = 1.0
    for i in range(16):
        L[i, 1 + i] = 1.0 / RRD
        R[16, 1 + i] = 1.0
    for i in range(16):
        L[i, 17 + i] = 1.0 / (RDs * R2)
        R[i, 17 + i] = 1.0
    cc = 33
    for i in range(16):
        for j in range(i + 1, 16):
            L[i, cc] = 1.0 / RDs
            R[j, cc] = 1.0
            cc += 1
    assert cc == DPHI
    Ls = np.zeros((128, DPHI), np.float32)
    Rs = np.zeros((128, DPHI), np.float32)
    for g in range(4):
        Ls[g * 32:(g + 1) * 32] = L
        Rs[g * 32:(g + 1) * 32] = R
    return Ls, Rs


def _pad_qk_weight(W):
    Wp = np.zeros((HID, 512), np.float32)
    for h in range(NH):
        Wp[:, h * 32: h * 32 + 16] = W[h * 16:(h + 1) * 16, :].T
    return Wp


def _causal_masks():
    j = np.arange(256)[None, :]
    p = np.arange(128)[:, None]
    m0 = (p <= j).astype(np.float32)
    m1 = (p + 128 <= j).astype(np.float32)
    return np.concatenate([m0, m1], axis=1)


ENGINES = ("sync", "scalar", "vector", "tensor", "gpsimd")


class P:
    def __init__(self, nc, sems):
        self.nc = nc
        self.sems = sems
        self.streams = {e: [] for e in ENGINES}
        self.cnt = defaultdict(int)

    def op(self, eng, fn, sem=None, n=1):
        if eng == "vector":
            assert sem == "dve", "vector ops must increment dve"
            self.wait(eng, "dve")
        elif eng == "scalar" and sem == "act":
            self.wait(eng, "act")
        if sem is None:
            self.streams[eng].append(fn)
        else:
            self.cnt[sem] += n
            s = self.sems[sem]
            self.streams[eng].append(lambda e, fn=fn, s=s, n=n: fn(e).then_inc(s, n))

    def wait(self, eng, sem, val=None):
        v = self.cnt[sem] if val is None else val
        if v <= 0:
            return
        s = self.sems[sem]
        self.streams[eng].append(lambda e, s=s, v=v: e.wait_ge(s, v))

    def dma(self, eng, out, in_, sem, **kw):
        self.op(eng, lambda e, out=out, in_=in_, kw=kw: e.dma_start(out=out, in_=in_, **kw),
                sem=sem, n=16)

    def replay(self, block):
        def runner(name):
            def run(e):
                for f in self.streams[name]:
                    f(e)
            return run
        block.sync(runner("sync"))
        block.scalar(runner("scalar"))
        block.vector(runner("vector"))
        block.tensor(runner("tensor"))
        block.gpsimd(runner("gpsimd"))


def build_program(debug_stage=None):
    nc = bass.Bass("TRN2", target_bir_lowering=False)
    dt = nc.dram_tensor

    io = {}
    io["x_fm"] = dt("x_fm", [HID, T], F32, kind="ExternalInput").ap()
    io["wqk"] = dt("wqk", [HID, 1024], F32, kind="ExternalInput").ap()
    io["wv"] = dt("wv", [HID, HID], F32, kind="ExternalInput").ap()
    io["wo"] = dt("wo", [HID, HID], F32, kind="ExternalInput").ap()
    io["gate"] = dt("gate", [HID, NE], F32, kind="ExternalInput").ap()
    io["sgw"] = dt("sgw", [HID, 1], F32, kind="ExternalInput").ap()
    io["sgu"] = dt("sgu", [HID, 2 * ISH], F32, kind="ExternalInput").ap()
    io["sdw"] = dt("sdw", [ISH, HID], F32, kind="ExternalInput").ap()
    io["egu"] = dt("egu", [EPC, HID, 2 * IEX], F32, kind="ExternalInput").ap()
    io["edw"] = dt("edw", [EPC, IEX, HID], F32, kind="ExternalInput").ap()
    io["lsel"] = dt("lsel", [128, DPHI], BF16, kind="ExternalInput").ap()
    io["rsel"] = dt("rsel", [128, DPHI], BF16, kind="ExternalInput").ap()
    io["msk"] = dt("msk", [128, 512], BF16, kind="ExternalInput").ap()
    io["idf"] = dt("idf", [128, 128], F32, kind="ExternalInput").ap()
    io["idb"] = dt("idb", [128, 128], BF16, kind="ExternalInput").ap()
    io["iota64"] = dt("iota64", [128, NE], F32, kind="ExternalInput").ap()
    io["tokid"] = dt("tokid", [128, TT], F32, kind="ExternalInput").ap()
    io["rankm"] = dt("rankm", [8, 1], F32, kind="ExternalInput").ap()
    io["selm"] = dt("selm", [128, 8], F32, kind="ExternalInput").ap()
    io["recidx"] = dt("recidx", [128, EPC * 4], I32, kind="ExternalInput").ap()
    io["cvals"] = dt("cvals", [128, 8], F32, kind="ExternalInput").ap()
    io["out_fm"] = dt("out_fm", [HID, T], F32, kind="ExternalOutput").ap()

    io["x1d"] = dt("x1d", [HID, T], F32).ap()
    io["onesd"] = dt("onesd", [2048], BF16).ap()
    io["lnd"] = dt("lnd", [4096], F32).ap()
    io["sgd"] = dt("sgd", [T], F32).ap()
    io["sagi"] = dt("sagi", [160, NH * 65], F32).ap()
    io["sago"] = dt("sago", [8 * 160, NH * 65], F32).ap()
    io["cnti"] = dt("cnti", [NE, 1], F32).ap()
    io["cnto"] = dt("cnto", [8 * NE, 1], F32).ap()
    io["recm"] = dt("recm", [NE * 513, 2], F32).ap()
    io["recmr"] = dt("recmr", [NE * 513, 2], F32).ap()
    io["xgi"] = dt("xgi", [T, HID], BF16).ap()
    io["xgo"] = dt("xgo", [NCORES * T, HID], BF16).ap()
    io["woutl"] = dt("woutl", [EPC * CAP, HID], BF16).ap()
    io["ymd"] = dt("ymd", [T, HID], F32).ap()
    io["woutg"] = dt("woutg", [NE * CAP, HID], BF16).ap()

    if debug_stage is not None:
        io["dbg_ab"] = dt("dbg_ab", [128, AB_N], F32, kind="ExternalOutput").ap()
        io["dbg_af"] = dt("dbg_af", [128, AF_N], F32, kind="ExternalOutput").ap()
        io["dbg_rec"] = dt("dbg_rec", [NE * 513, 2], F32, kind="ExternalOutput").ap()
        io["dbg_sag"] = dt("dbg_sag", [160, NH * 65], F32, kind="ExternalOutput").ap()
        io["dbg_cum"] = dt("dbg_cum", [64, 2048], F32, kind="ExternalOutput").ap()
        io["dbg_widx"] = dt("dbg_widx", [128, 128], I32, kind="ExternalOutput").ap()
        io["dbg_wout"] = dt("dbg_wout", [EPC * CAP, HID], F32, kind="ExternalOutput").ap()

    from contextlib import ExitStack
    with ExitStack() as ctx:
        en = ctx.enter_context
        AB = en(nc.sbuf_tensor("AB", [128, AB_N], BF16))
        AFr = en(nc.sbuf_tensor("AFr", [128, AF_N], F32))
        cumt = en(nc.sbuf_tensor("cum", [64, 2048], F32))
        ssbB = en(nc.sbuf_tensor("ssbB", [32, NH * 65], F32))
        sbf = en(nc.sbuf_tensor("sbf", [128, NH * 65], BF16))
        sbfB = en(nc.sbuf_tensor("sbfB", [32, NH * 65], BF16))
        spv = en(nc.sbuf_tensor("spv", [128, NH * 65], BF16))
        spvB = en(nc.sbuf_tensor("spvB", [32, NH * 65], BF16))
        lselt = en(nc.sbuf_tensor("lsel_sb", [128, DPHI], BF16))
        rselt = en(nc.sbuf_tensor("rsel_sb", [128, DPHI], BF16))
        mskt = en(nc.sbuf_tensor("mskt", [128, 512], BF16))
        idft = en(nc.sbuf_tensor("idf_sb", [128, 128], F32))
        idbt = en(nc.sbuf_tensor("idb_sb", [128, 128], BF16))
        iota64t = en(nc.sbuf_tensor("iota64_sb", [128, NE], F32))
        tokidt = en(nc.sbuf_tensor("tokid_sb", [128, TT], F32))
        selmt = en(nc.sbuf_tensor("selm_sb", [128, 8], F32))
        rankmt = en(nc.sbuf_tensor("rankm_sb", [8, 1], F32))
        recidxt = en(nc.sbuf_tensor("recidxt", [128, EPC * 4], I32))
        ones1 = en(nc.sbuf_tensor("ones1", [1, 512], F32))
        onesb = en(nc.sbuf_tensor("onesb", [1, 2048], BF16))
        onesc = en(nc.sbuf_tensor("onesc", [128, 1], F32))
        sgwb = en(nc.sbuf_tensor("sgwb", [128, 8], BF16))
        cvalt = en(nc.sbuf_tensor("cvalt", [128, 8], F32))
        idx8 = en(nc.sbuf_tensor("idx8", [128, 8], U32))
        sidx = en(nc.sbuf_tensor("sidx", [128, 8], I32))
        psA = en(nc.psum_tensor("psA", [128, 512], F32))
        psB = en(nc.psum_tensor("psB", [128, 512], F32))
        psC = en(nc.psum_tensor("psC", [128, 512], F32))
        psD = en(nc.psum_tensor("psD", [128, 512], F32))
        psY = en(nc.psum_tensor("psY", [65, 1024], F32))
        psS = en(nc.psum_tensor("psS", [128, 260], F32))
        psSB = en(nc.psum_tensor("psSB", [32, 260], F32))
        s_ld = en(nc.semaphore("ld"))
        s_wld = en(nc.semaphore("wld"))
        s_pe = en(nc.semaphore("pe"))
        s_act = en(nc.semaphore("act"))
        s_dve = en(nc.semaphore("dve"))
        s_gd = en(nc.semaphore("gd"))
        s_cc = en(nc.semaphore("cc"))
        s_st = en(nc.semaphore("st"))
        block = en(nc.Block())
        sems = dict(ld=s_ld, wld=s_wld, pe=s_pe, act=s_act, dve=s_dve,
                    gd=s_gd, cc=s_cc, st=s_st)
        p = P(nc, sems)
        c = dict(io)
        c.update(nc=nc, p=p, AB=AB, AF=AFr, cum=cumt, ssbB=ssbB, sbf=sbf,
                 sbfB=sbfB, spv=spv, spvB=spvB, lselt=lselt, rselt=rselt,
                 mskt=mskt, idft=idft, idbt=idbt, iota64t=iota64t,
                 tokidt=tokidt, selmt=selmt, rankmt=rankmt, recidxt=recidxt,
                 ones1=ones1, onesb=onesb, onesc=onesc,
                 onesf=AFr[0:64, AF_LNR + 2048:AF_LNR + 2048 + 128],
                 widx=AFr[:, AF_LNR + 2304:AF_LNR + 2304 + 128].bitcast(I32),
                 sgwb=sgwb, idx8=idx8, sidx=sidx,
                 psA=psA, psB=psB, psC=psC, psD=psD, psY=psY, psS=psS,
                 psSB=psSB, cvalt=cvalt, debug_stage=debug_stage)
        nc.const_aps.register(F32, FSC, cvalt[:, 0:1])
        nc.const_aps.register(F32, FB, cvalt[:, 1:2])
        nc.const_aps.register(F32, LNEPS, cvalt[:, 2:3])
        emit_all(c)
        p.replay(block)
    return nc


def ab(c, off, n):
    return c["AB"][:, off:off + n]


def abr(c, r0, r1, off, n):
    return c["AB"][r0:r1, off:off + n]


def af(c, off, n):
    return c["AF"][:, off:off + n]


def afr(c, r0, r1, off, n):
    return c["AF"][r0:r1, off:off + n]


def emit_all(c):
    p = c["p"]
    emit_setup(c)
    emit_ln(c, src="x0")
    emit_projections(c)
    if c["debug_stage"] == "proj":
        emit_dump(c, regions=["ab", "af"])
        return
    emit_scan(c)
    if c["debug_stage"] == "scan":
        emit_dump(c, regions=["ab", "af", "sag"])
        return
    emit_handoff(c)
    emit_correction(c)
    if c["debug_stage"] == "attn":
        emit_dump(c, regions=["ab", "af", "sag"])
        return
    emit_norm_wo_resid(c)
    emit_ln(c, src="x1")
    if c["debug_stage"] == "ln2":
        emit_dump(c, regions=["ab"])
        return
    emit_xg(c)
    emit_routing(c)
    if c["debug_stage"] == "routing":
        emit_dump(c, regions=["af", "rec", "cumd", "widxd"])
        return
    emit_experts(c)
    if c["debug_stage"] == "experts":
        emit_dump(c, regions=["af", "rec", "widxd", "woutd"])
        return
    emit_shared(c)
    if c["debug_stage"] == "shared":
        emit_dump(c, regions=["ab", "af"])
        return
    emit_combine(c)
    if c["debug_stage"] == "full":
        emit_dump(c, regions=["ab", "af"])
        return
    p.wait("gpsimd", "st")


def emit_setup(c):
    p = c["p"]
    c["bregs"] = {}

    def _alloc_bounds(e):
        for v in (NE * 513 - 1, NCORES * T - 1, NE * CAP - 1):
            r = e.alloc_register()
            e.reg_mov(r, v)
            c["bregs"][v] = r
        return None
    p.streams["gpsimd"].append(lambda e: _alloc_bounds(e))
    for name, tile in [("lsel", c["lselt"]), ("rsel", c["rselt"]),
                       ("msk", c["mskt"]), ("idf", c["idft"]), ("idb", c["idbt"]),
                       ("iota64", c["iota64t"]), ("tokid", c["tokidt"]),
                       ("selm", c["selmt"]), ("rankm", c["rankmt"]),
                       ("recidx", c["recidxt"]), ("cvals", c["cvalt"])]:
        p.dma("sync", tile[:], c[name][:], "ld")
    p.op("vector", lambda e: e.memset(c["ones1"][:], 1.0), sem="dve")
    p.op("vector", lambda e: e.memset(c["onesb"][:], 1.0), sem="dve")
    p.op("vector", lambda e: e.memset(c["onesc"][:], 1.0), sem="dve")
    p.op("vector", lambda e: e.memset(af(c, AF_SSB, 1040), 0.0), sem="dve")
    p.op("vector", lambda e: e.memset(c["ssbB"][:], 0.0), sem="dve")
    p.op("vector", lambda e: e.memset(c["sbf"][:], 0.0), sem="dve")
    p.op("vector", lambda e: e.memset(c["sbfB"][:], 0.0), sem="dve")
    p.op("vector", lambda e: e.memset(af(c, AF_DEN, 256), 0.0), sem="dve")
    p.wait("sync", "dve")
    p.dma("sync", c["onesd"].rearrange("(o a) -> o a", o=1), c["onesb"][0:1, :], "ld")


def emit_ln(c, src):
    p = c["p"]
    xsrc = c["x_fm"] if src == "x0" else c["x1d"]
    psA, psB, psC, psD = c["psA"], c["psB"], c["psC"], c["psD"]
    p.wait("sync", "gd")

    for tc in range(TC):
        for fc in range(FC):
            slot = fc % 2
            xs = af(c, AF_S1 + slot * 512, 512)
            sq = af(c, AF_S2 + slot * 512, 512)
            p.wait("sync", "pe")
            p.wait("sync", "act")
            p.dma("sync", xs, xsrc[fc * 128:(fc + 1) * 128, tc * 512:(tc + 1) * 512], "ld")
            p.wait("scalar", "ld")
            p.wait("scalar", "pe")
            p.op("scalar", lambda e, xs=xs, sq=sq: e.activation(
                out=sq, in_=xs, func=AF_.Square), sem="act")
            if fc == 0:
                p.wait("tensor", "dve")
            p.wait("tensor", "ld")
            p.wait("tensor", "act")
            p.op("tensor", lambda e, xs=xs, fc=fc: e.matmul(
                psA[0:1, 0:512], c["onesc"][:, 0:1], xs,
                start=(fc == 0), stop=(fc == FC - 1)), sem="pe")
            p.op("tensor", lambda e, sq=sq, fc=fc: e.matmul(
                psB[0:1, 0:512], c["onesc"][:, 0:1], sq,
                start=(fc == 0), stop=(fc == FC - 1)), sem="pe")
        p.wait("vector", "pe")
        p.wait("vector", "ld")
        p.op("vector", lambda e: e.tensor_copy(af(c, AF_ROWS, 512)[0:1, :], psA[0:1, 0:512]), sem="dve")
        p.wait("sync", "dve")
        p.dma("sync", c["lnd"][tc * 512:(tc + 1) * 512].rearrange("(o a) -> o a", o=1),
              af(c, AF_ROWS, 512)[0:1, :], "ld")
        p.wait("vector", "ld")
        p.op("vector", lambda e: e.tensor_copy(af(c, AF_ROWS, 512)[0:1, :], psB[0:1, 0:512]), sem="dve")
        p.wait("sync", "dve")
        p.dma("sync", c["lnd"][2048 + tc * 512:2048 + (tc + 1) * 512].rearrange("(o a) -> o a", o=1),
              af(c, AF_ROWS, 512)[0:1, :], "ld")

    p.wait("sync", "ld")
    p.wait("sync", "pe")
    p.wait("sync", "act")
    p.dma("sync", af(c, AF_S1, 32), c["lnd"].rearrange("(p a) -> p a", p=128), "ld")
    p.wait("vector", "ld")
    inv = 1.0 / HID
    p.op("vector", lambda e: e.tensor_copy(afr(c, 0, 64, AF_S1 + 32, 32),
                                           afr(c, 64, 128, AF_S1, 32)), sem="dve")
    p.op("vector", lambda e: e.tensor_scalar(afr(c, 0, 64, AF_S2, 32), afr(c, 0, 64, AF_S1, 32),
                                             inv, None, op0=ALU.mult), sem="dve")
    p.op("vector", lambda e: e.tensor_scalar(afr(c, 0, 64, AF_S2 + 32, 32),
                                             afr(c, 0, 64, AF_S1 + 32, 32),
                                             inv, None, op0=ALU.mult), sem="dve")
    p.op("vector", lambda e: e.tensor_mul(afr(c, 0, 64, AF_S1 + 64, 32),
                                          afr(c, 0, 64, AF_S2, 32),
                                          afr(c, 0, 64, AF_S2, 32)), sem="dve")
    p.op("vector", lambda e: e.tensor_sub(afr(c, 0, 64, AF_S1 + 96, 32),
                                          afr(c, 0, 64, AF_S2 + 32, 32),
                                          afr(c, 0, 64, AF_S1 + 64, 32)), sem="dve")
    p.wait("scalar", "dve")
    p.op("scalar", lambda e: e.activation(out=afr(c, 0, 64, AF_S2 + 64, 32),
                                          in_=afr(c, 0, 64, AF_S1 + 96, 32),
                                          func=AF_.Sqrt, bias=LNEPS), sem="act")
    p.wait("vector", "act")
    p.op("vector", lambda e: e.reciprocal(afr(c, 0, 64, AF_S2 + 96, 32),
                                          afr(c, 0, 64, AF_S2 + 64, 32)), sem="dve")
    p.op("vector", lambda e: e.tensor_mul(afr(c, 0, 64, AF_S1 + 64, 32),
                                          afr(c, 0, 64, AF_S2, 32),
                                          afr(c, 0, 64, AF_S2 + 96, 32)), sem="dve")
    p.wait("sync", "dve")
    p.dma("sync", c["lnd"][0:2048].rearrange("(p a) -> p a", p=64),
          afr(c, 0, 64, AF_S2 + 96, 32), "ld")
    p.dma("sync", c["lnd"][2048:4096].rearrange("(p a) -> p a", p=64),
          afr(c, 0, 64, AF_S1 + 64, 32), "ld")
    p.wait("sync", "ld")
    p.dma("sync", af(c, AF_LNR, 2048)[0:1, :],
          c["lnd"][0:2048].rearrange("(o a) -> o a", o=1), "ld")
    p.dma("sync", af(c, AF_LNR + 2048, 2048)[0:1, :],
          c["lnd"][2048:4096].rearrange("(o a) -> o a", o=1), "ld")

    for tc in range(TC):
        p.wait("tensor", "ld")
        p.wait("tensor", "dve")
        p.op("tensor", lambda e, tc=tc: e.matmul(
            psC[:, 0:512], c["ones1"][0:1, 0:128],
            af(c, AF_LNR + tc * 512, 512)[0:1, :], start=True, stop=True), sem="pe")
        p.op("tensor", lambda e, tc=tc: e.matmul(
            psD[:, 0:512], c["ones1"][0:1, 0:128],
            af(c, AF_LNR + 2048 + tc * 512, 512)[0:1, :], start=True, stop=True), sem="pe")
        for fc in range(FC):
            slot = fc % 2
            xs = af(c, AF_S1 + slot * 512, 512)
            tmp = af(c, AF_S2 + slot * 512, 512)
            p.wait("sync", "dve")
            p.dma("sync", xs, xsrc[fc * 128:(fc + 1) * 128, tc * 512:(tc + 1) * 512], "ld")
            p.wait("vector", "ld")
            p.wait("vector", "pe")
            p.op("vector", lambda e, xs=xs, tmp=tmp: e.tensor_mul(tmp, xs, psC[:, 0:512]), sem="dve")
            p.op("vector", lambda e, tmp=tmp, fc=fc, tc=tc: e.tensor_sub(
                ab(c, AB_XN + fc * T + tc * 512, 512), tmp, psD[:, 0:512]), sem="dve")


def emit_projections(c):
    p = c["p"]
    psA, psB = c["psA"], c["psB"]
    WQK = AB_YF
    for fc in range(FC):
        p.dma("gpsimd", ab(c, WQK + fc * 1024, 1024),
              c["wqk"][fc * 128:(fc + 1) * 128, :], "wld")
    p.wait("tensor", "wld")
    for mj in range(8):
        for tc in range(TC):
            ps = psA if (mj * TC + tc) % 2 == 0 else psB
            p.wait("tensor", "dve")
            for fc in range(FC):
                p.op("tensor", lambda e, ps=ps, mj=mj, tc=tc, fc=fc: e.matmul(
                    ps[:, 0:512],
                    ab(c, WQK + fc * 1024 + mj * 128, 128),
                    ab(c, AB_XN + fc * T + tc * 512, 512),
                    start=(fc == 0), stop=(fc == FC - 1)),
                    sem=("pe" if fc == FC - 1 else None))
            dst_off = (AB_QF + mj * T if mj < 4 else AB_KF + (mj - 4) * T) + tc * 512
            p.wait("vector", "pe")
            p.op("vector", lambda e, ps=ps, dst_off=dst_off: e.tensor_copy(
                ab(c, dst_off, 512), ps[:, 0:512]), sem="dve")
    p.wait("gpsimd", "dve")
    for base in (AB_QF, AB_KF):
        for g in range(4):
            row = 16 + 32 * g
            src = c["onesd"].rearrange("(o b a) -> o b a", o=1, b=1).to_broadcast([1, 4, 2048])
            p.dma("gpsimd",
                  c["AB"][row:row + 1, base:base + 8192].rearrange("p (a b) -> p a b", b=2048),
                  src, "wld")
    p.wait("gpsimd", "pe")
    for fc in range(FC):
        p.dma("gpsimd", ab(c, WQK + fc * 1024, 1024),
              c["wv"][fc * 128:(fc + 1) * 128, :], "wld")
    p.wait("tensor", "wld")
    for i in range(TT):
        for n in range(2):
            ps = psA if (i * 2 + n) % 2 == 0 else psB
            p.wait("tensor", "dve")
            for fc in range(FC):
                p.op("tensor", lambda e, ps=ps, i=i, n=n, fc=fc: e.matmul(
                    ps[:, 0:512],
                    ab(c, AB_XN + fc * T + i * 128, 128),
                    ab(c, WQK + fc * 1024 + n * 512, 512),
                    start=(fc == 0), stop=(fc == FC - 1)),
                    sem=("pe" if fc == FC - 1 else None))
            p.wait("vector", "pe")
            dst = c["AB"][:, AB_VT + i * 1040 + n * 8 * 65: AB_VT + i * 1040 + (n * 8 + 8) * 65]
            dstv = dst.rearrange("p (h d) -> p h d", d=65)[:, :, 0:64]
            srcv = ps[:, 0:512].rearrange("p (h d) -> p h d", d=64)
            p.op("vector", lambda e, dstv=dstv, srcv=srcv: e.tensor_copy(dstv, srcv), sem="dve")
        dstones = c["AB"][:, AB_VT + i * 1040: AB_VT + (i + 1) * 1040]
        dstones = dstones.rearrange("p (h d) -> p h d", d=65)[:, :, 64:65]
        p.op("vector", lambda e, dstones=dstones: e.memset(dstones, 1.0), sem="dve")


SCAN_G = 4
SCAN_CH = CH
SCAN_PARTS = 5
SCAN_MASK = True
SCAN_F = True


def emit_scan(c):
    p = c["p"]
    psA, psB, psC, psD = c["psA"], c["psB"], c["psC"], c["psD"]
    psY, psS, psSB = c["psY"], c["psS"], c["psSB"]
    p.wait("tensor", "wld")

    for g in range(SCAN_G):
        for ch in range(SCAN_CH):
            tq0 = ch * 256
            pss = (psA, psB, psC, psD)
            p.wait("tensor", "dve")
            for hp in range(4):
                p.op("tensor", lambda e, hp=hp, g=g, tq0=tq0: e.matmul(
                    pss[hp][:, 0:256],
                    c["lselt"][32 * hp:32 * hp + 32, 0:128],
                    ab(c, AB_QF + g * T + tq0, 256)[32 * hp:32 * hp + 32, :],
                    start=True, stop=True, tile_position=(32 * hp, 0)), sem="pe")
                p.op("tensor", lambda e, hp=hp, g=g, tq0=tq0: e.matmul(
                    pss[hp][0:25, 256:512],
                    c["lselt"][32 * hp:32 * hp + 32, 128:153],
                    ab(c, AB_QF + g * T + tq0, 256)[32 * hp:32 * hp + 32, :],
                    start=True, stop=True, tile_position=(32 * hp, 0)), sem="pe")
            p.wait("vector", "pe")
            for hp in range(4):
                p.op("vector", lambda e, hp=hp: e.tensor_copy(
                    ab(c, AB_STG + hp * 256, 256), pss[hp][:, 0:256]), sem="dve")
                p.op("vector", lambda e, hp=hp: e.tensor_copy(
                    abr(c, 0, 25, AB_STG + 1024 + hp * 256, 256),
                    pss[hp][0:25, 256:512]), sem="dve")
            p.wait("tensor", "dve")
            for hp in range(4):
                p.op("tensor", lambda e, hp=hp, g=g, tq0=tq0: e.matmul(
                    pss[hp][:, 0:256],
                    c["rselt"][32 * hp:32 * hp + 32, 0:128],
                    ab(c, AB_QF + g * T + tq0, 256)[32 * hp:32 * hp + 32, :],
                    start=True, stop=True, tile_position=(32 * hp, 0)), sem="pe")
                p.op("tensor", lambda e, hp=hp, g=g, tq0=tq0: e.matmul(
                    pss[hp][0:25, 256:512],
                    c["rselt"][32 * hp:32 * hp + 32, 128:153],
                    ab(c, AB_QF + g * T + tq0, 256)[32 * hp:32 * hp + 32, :],
                    start=True, stop=True, tile_position=(32 * hp, 0)), sem="pe")
            p.wait("vector", "pe")
            for hp in range(4):
                p.op("vector", lambda e, hp=hp: e.tensor_mul(
                    ab(c, AB_PHIQA + hp * 256, 256),
                    ab(c, AB_STG + hp * 256, 256), pss[hp][:, 0:256]), sem="dve")
                p.op("vector", lambda e, hp=hp: e.tensor_mul(
                    abr(c, 0, 25, AB_PHIQB + hp * 256, 256),
                    abr(c, 0, 25, AB_STG + 1024 + hp * 256, 256),
                    pss[hp][0:25, 256:512]), sem="dve")

            for tk in range(2 if SCAN_PARTS >= 2 else 0):
                p.wait("tensor", "dve")
                if tk == 1:
                    p.wait("tensor", "act")
                for hp in range(4):
                    p.op("tensor", lambda e, hp=hp, g=g, tq0=tq0, tk=tk: e.matmul(
                        pss[hp][:, 0:256],
                        ab(c, AB_KF + g * T + tq0 + tk * 128, 128)[32 * hp:32 * hp + 32, :],
                        ab(c, AB_QF + g * T + tq0, 256)[32 * hp:32 * hp + 32, :],
                        start=True, stop=True, tile_position=(32 * hp, 0)), sem="pe")
                a4o = AB_A4 + tk * 1024
                p.wait("scalar", "pe")
                for hp in range(4):
                    p.op("scalar", lambda e, hp=hp, a4o=a4o: e.activation(
                        out=ab(c, a4o + hp * 256, 256), in_=pss[hp][:, 0:256],
                        func=AF_.Square, scale=FSC, bias=FB), sem="act")
                p.wait("vector", "act")
                a4v = ab(c, a4o, 1024).rearrange("p (h n) -> p h n", n=256)
                mskv = c["mskt"][:, tk * 256:(tk + 1) * 256].rearrange(
                    "p (o n) -> p o n", o=1).to_broadcast([128, 4, 256])
                if SCAN_MASK:
                    p.op("vector", lambda e, a4v=a4v, mskv=mskv: e.scalar_tensor_tensor(
                        out=a4v, in0=a4v, scalar=0.5, in1=mskv,
                        op0=ALU.add, op1=ALU.mult), sem="dve")

            p.wait("tensor", "dve")
            for hp in range(4 if SCAN_PARTS >= 3 else 0):
                h = 4 * g + hp
                col = hp * 256
                for tk in range(2):
                    p.op("tensor", lambda e, h=h, hp=hp, col=col, tk=tk, ch=ch: e.matmul(
                        psY[0:65, col:col + 256],
                        ab(c, AB_VT + (2 * ch + tk) * 1040 + h * 65, 65),
                        ab(c, AB_A4 + tk * 1024 + hp * 256, 256),
                        start=(tk == 0), stop=False), sem="pe")
                p.op("tensor", lambda e, h=h, hp=hp, col=col: e.matmul(
                    psY[0:65, col:col + 256],
                    c["sbf"][:, h * 65:h * 65 + 65],
                    ab(c, AB_PHIQA + hp * 256, 256),
                    start=False, stop=False), sem="pe")
                p.op("tensor", lambda e, h=h, hp=hp, col=col: e.matmul(
                    psY[0:65, col:col + 256],
                    c["sbfB"][0:25, h * 65:h * 65 + 65],
                    abr(c, 0, 25, AB_PHIQB + hp * 256, 256),
                    start=False, stop=True), sem="pe")
            p.wait("vector", "pe")
            p.wait("vector", "gd")
            for hp in range(4 if SCAN_PARTS >= 3 else 0):
                h = 4 * g + hp
                p.op("vector", lambda e, h=h, hp=hp, tq0=tq0: e.tensor_copy(
                    abr(c, (h % 2) * 64, (h % 2) * 64 + 64, AB_YF + (h // 2) * T + tq0, 256),
                    psY[0:64, hp * 256:hp * 256 + 256]), sem="dve")
            if SCAN_PARTS >= 3:
                p.op("vector", lambda e: e.tensor_copy(
                    af(c, AF_ROWS, 1024)[0:1, :], psY[64:65, 0:1024]), sem="dve")
                p.wait("sync", "dve")
                dstden = afr(c, ch * 16 + g * 4, ch * 16 + g * 4 + 4, AF_DEN, 256)
                srcden = af(c, AF_ROWS, 1024)[0:1, :].rearrange("p (a b) -> p a b", b=256)
                p.op("sync", lambda e, dstden=dstden, srcden=srcden: e.dma_start(
                    out=dstden, in_=srcden), sem="gd", n=16)

            first_state = True
            for tk in range(2 if SCAN_PARTS >= 4 else 0):
                p.wait("tensor", "dve")
                for hp in range(4):
                    p.op("tensor", lambda e, hp=hp, g=g, tq0=tq0, tk=tk: e.matmul(
                        pss[hp][:, 0:153],
                        ab(c, AB_KF + g * T + tq0 + tk * 128, 128)[32 * hp:32 * hp + 32, :],
                        c["lselt"][32 * hp:32 * hp + 32, 0:153],
                        start=True, stop=True, tile_position=(32 * hp, 0)), sem="pe")
                p.wait("vector", "pe")
                for hp in range(4):
                    p.op("vector", lambda e, hp=hp: e.tensor_copy(
                        ab(c, AB_STG + hp * 153, 153), pss[hp][:, 0:153]), sem="dve")
                p.wait("tensor", "dve")
                for hp in range(4):
                    p.op("tensor", lambda e, hp=hp, g=g, tq0=tq0, tk=tk: e.matmul(
                        pss[hp][:, 0:153],
                        ab(c, AB_KF + g * T + tq0 + tk * 128, 128)[32 * hp:32 * hp + 32, :],
                        c["rselt"][32 * hp:32 * hp + 32, 0:153],
                        start=True, stop=True, tile_position=(32 * hp, 0)), sem="pe")
                p.wait("vector", "pe")
                for hp in range(4):
                    p.op("vector", lambda e, hp=hp: e.tensor_mul(
                        ab(c, AB_PHIK + hp * 153, 153),
                        ab(c, AB_STG + hp * 153, 153), pss[hp][:, 0:153]), sem="dve")
                p.wait("tensor", "dve")
                for hp in range(4):
                    h = 4 * g + hp
                    p.op("tensor", lambda e, hp=hp, h=h, ch=ch, tk=tk, fs=first_state: e.matmul(
                        psS[:, hp * 65:hp * 65 + 65],
                        ab(c, AB_PHIK + hp * 153, 128),
                        ab(c, AB_VT + (2 * ch + tk) * 1040 + h * 65, 65),
                        start=(fs and hp == 0), stop=(tk == 1 and hp == 3)), sem="pe")
                    p.op("tensor", lambda e, hp=hp, h=h, ch=ch, tk=tk, fs=first_state: e.matmul(
                        psSB[0:25, hp * 65:hp * 65 + 65],
                        ab(c, AB_PHIK + hp * 153 + 128, 25),
                        ab(c, AB_VT + (2 * ch + tk) * 1040 + h * 65, 65),
                        start=(fs and hp == 0), stop=(tk == 1 and hp == 3)), sem="pe")
                first_state = False
            if SCAN_PARTS < 5:
                continue
            p.wait("vector", "pe")
            p.op("vector", lambda e, g=g: e.tensor_add(
                af(c, AF_SSB + g * 260, 260), af(c, AF_SSB + g * 260, 260),
                psS[:, 0:260]), sem="dve")
            p.op("vector", lambda e, g=g: e.tensor_add(
                c["ssbB"][0:25, g * 260:g * 260 + 260],
                c["ssbB"][0:25, g * 260:g * 260 + 260],
                psSB[0:25, 0:260]), sem="dve")
            p.op("vector", lambda e, g=g: e.tensor_copy(
                c["sbf"][:, g * 260:g * 260 + 260], af(c, AF_SSB + g * 260, 260)), sem="dve")
            p.op("vector", lambda e, g=g: e.tensor_copy(
                c["sbfB"][0:25, g * 260:g * 260 + 260],
                c["ssbB"][0:25, g * 260:g * 260 + 260]), sem="dve")


def emit_handoff(c):
    p = c["p"]
    p.wait("sync", "dve")
    p.dma("sync", c["sagi"][0:128, :], af(c, AF_SSB, 1040), "gd")
    p.dma("sync", c["sagi"][128:160, :], c["ssbB"][:, :], "gd")
    p.wait("gpsimd", "gd")
    p.op("gpsimd", lambda e: e.collective_compute(
        "AllGather", ALU.bypass, replica_groups=[list(range(NCORES))],
        ins=[c["sagi"].opt()], outs=[c["sago"].opt()]), sem="cc")
    p.wait("sync", "cc")
    p.op("vector", lambda e: e.memset(af(c, AF_S1, 1040), 0.0), sem="dve")
    p.op("vector", lambda e: e.memset(afr(c, 32, 64, AF_S2, 1040), 0.0), sem="dve")
    for r in range(NCORES):
        p.wait("sync", "dve")
        p.dma("sync", af(c, AF_RS, 1040), c["sago"][r * 160:r * 160 + 128, :], "ld")
        p.dma("sync", afr(c, 32, 64, AF_LNR, 1040), c["sago"][r * 160 + 128:(r + 1) * 160, :], "ld")
        p.wait("vector", "ld")
        p.op("vector", lambda e, r=r: e.scalar_tensor_tensor(
            out=af(c, AF_S1, 1040), in0=af(c, AF_RS, 1040),
            scalar=c["selmt"][:, r:r + 1], in1=af(c, AF_S1, 1040),
            op0=ALU.mult, op1=ALU.add), sem="dve")
        p.op("vector", lambda e, r=r: e.scalar_tensor_tensor(
            out=afr(c, 32, 64, AF_S2, 1040), in0=afr(c, 32, 64, AF_LNR, 1040),
            scalar=c["selmt"][32:64, r:r + 1], in1=afr(c, 32, 64, AF_S2, 1040),
            op0=ALU.mult, op1=ALU.add), sem="dve")
    p.op("vector", lambda e: e.tensor_copy(c["spv"][:, :], af(c, AF_S1, 1040)), sem="dve")
    p.op("vector", lambda e: e.tensor_copy(c["spvB"][:, :], afr(c, 32, 64, AF_S2, 1040)), sem="dve")


def emit_correction(c):
    p = c["p"]
    psA, psB, psC, psD, psY = c["psA"], c["psB"], c["psC"], c["psD"], c["psY"]
    pss = (psA, psB, psC, psD)
    for g in range(4):
        for c2 in range(TC):
            tq0 = c2 * 512
            p.wait("tensor", "dve")
            for hp in range(4):
                p.op("tensor", lambda e, hp=hp, g=g, tq0=tq0: e.matmul(
                    pss[hp][:, 0:512],
                    c["lselt"][32 * hp:32 * hp + 32, 0:128],
                    ab(c, AB_QF + g * T + tq0, 512)[32 * hp:32 * hp + 32, :],
                    start=True, stop=True, tile_position=(32 * hp, 0)), sem="pe")
            p.wait("vector", "pe")
            for hp in range(4):
                p.op("vector", lambda e, hp=hp: e.tensor_copy(
                    ab(c, AB_STG + hp * 512, 512), pss[hp][:, 0:512]), sem="dve")
            p.wait("tensor", "dve")
            for hp in range(4):
                p.op("tensor", lambda e, hp=hp, g=g, tq0=tq0: e.matmul(
                    pss[hp][:, 0:512],
                    c["rselt"][32 * hp:32 * hp + 32, 0:128],
                    ab(c, AB_QF + g * T + tq0, 512)[32 * hp:32 * hp + 32, :],
                    start=True, stop=True, tile_position=(32 * hp, 0)), sem="pe")
            p.wait("vector", "pe")
            for hp in range(4):
                p.op("vector", lambda e, hp=hp: e.tensor_mul(
                    ab(c, AB_PHIQA + hp * 512, 512),
                    ab(c, AB_STG + hp * 512, 512), pss[hp][:, 0:512]), sem="dve")
            p.wait("tensor", "dve")
            for hp in range(4):
                p.op("tensor", lambda e, hp=hp, g=g, tq0=tq0: e.matmul(
                    pss[hp][0:25, 0:512],
                    c["lselt"][32 * hp:32 * hp + 32, 128:153],
                    ab(c, AB_QF + g * T + tq0, 512)[32 * hp:32 * hp + 32, :],
                    start=True, stop=True, tile_position=(32 * hp, 0)), sem="pe")
            p.wait("vector", "pe")
            for hp in range(4):
                p.op("vector", lambda e, hp=hp: e.tensor_copy(
                    abr(c, 0, 25, AB_STG + hp * 512, 512), pss[hp][0:25, 0:512]), sem="dve")
            p.wait("tensor", "dve")
            for hp in range(4):
                p.op("tensor", lambda e, hp=hp, g=g, tq0=tq0: e.matmul(
                    pss[hp][0:25, 0:512],
                    c["rselt"][32 * hp:32 * hp + 32, 128:153],
                    ab(c, AB_QF + g * T + tq0, 512)[32 * hp:32 * hp + 32, :],
                    start=True, stop=True, tile_position=(32 * hp, 0)), sem="pe")
            p.wait("vector", "pe")
            for hp in range(4):
                p.op("vector", lambda e, hp=hp: e.tensor_mul(
                    abr(c, 0, 25, AB_PHIQB + hp * 512, 512),
                    abr(c, 0, 25, AB_STG + hp * 512, 512), pss[hp][0:25, 0:512]), sem="dve")
            for pr in range(2):
                p.wait("tensor", "dve")
                if pr == 1:
                    p.wait("tensor", "gd")
                for hh in range(2):
                    hp = pr * 2 + hh
                    h = 4 * g + hp
                    p.op("tensor", lambda e, h=h, hp=hp, hh=hh: e.matmul(
                        psY[0:65, hh * 512:hh * 512 + 512],
                        c["spv"][:, h * 65:h * 65 + 65],
                        ab(c, AB_PHIQA + hp * 512, 512),
                        start=True, stop=False), sem="pe")
                    p.op("tensor", lambda e, h=h, hp=hp, hh=hh: e.matmul(
                        psY[0:65, hh * 512:hh * 512 + 512],
                        c["spvB"][0:25, h * 65:h * 65 + 65],
                        abr(c, 0, 25, AB_PHIQB + hp * 512, 512),
                        start=False, stop=True), sem="pe")
                p.wait("vector", "pe")
                p.wait("vector", "gd")
                for hh in range(2):
                    hp = pr * 2 + hh
                    h = 4 * g + hp
                    ysl = abr(c, (h % 2) * 64, (h % 2) * 64 + 64, AB_YF + (h // 2) * T + tq0, 512)
                    p.op("vector", lambda e, ysl=ysl, hh=hh: e.tensor_add(
                        ysl, ysl, psY[0:64, hh * 512:hh * 512 + 512]), sem="dve")
                p.op("vector", lambda e: e.tensor_copy(
                    af(c, AF_ROWS, 1024)[0:1, :], psY[64:65, 0:1024]), sem="dve")
                p.wait("gpsimd", "dve")
                for hh in range(2):
                    hp = pr * 2 + hh
                    for cc2 in range(2):
                        row = (2 * c2 + cc2) * 16 + g * 4 + hp
                        p.op("gpsimd", lambda e, row=row, hh=hh, cc2=cc2: e.dma_start(
                            out=afr(c, row, row + 1, AF_DEN, 256),
                            in_=af(c, AF_ROWS + hh * 512 + cc2 * 256, 256)[0:1, :],
                            accum_op=ALU.add), sem="gd", n=16)


def emit_norm_wo_resid(c):
    p = c["p"]
    psA, psB, psC = c["psA"], c["psB"], c["psC"]
    p.wait("vector", "gd")
    p.op("vector", lambda e: e.tensor_scalar(af(c, AF_DEN, 256), af(c, AF_DEN, 256),
                                             1e-12, None, op0=ALU.add), sem="dve")
    p.op("vector", lambda e: e.reciprocal(af(c, AF_DEN, 256), af(c, AF_DEN, 256)), sem="dve")
    p.wait("sync", "gd")
    for fc in range(FC):
        for tc in range(TC):
            p.wait("sync", "dve")
            p.wait("sync", "pe")
            for hh in range(2):
                h = 2 * fc + hh
                for cc2 in range(2):
                    chv = 2 * tc + cc2
                    row = chv * 16 + (h // 4) * 4 + (h % 4)
                    p.dma("sync", af(c, AF_ROWS + hh * 512 + cc2 * 256, 256)[0:1, :],
                          afr(c, row, row + 1, AF_DEN, 256), "ld")
            p.wait("tensor", "ld")
            p.wait("tensor", "dve")
            p.op("tensor", lambda e: e.matmul(
                psC[0:64, 0:512], c["ones1"][0:1, 0:64],
                af(c, AF_ROWS, 512)[0:1, :], start=True, stop=True), sem="pe")
            p.op("tensor", lambda e: e.matmul(
                psC[64:128, 0:512], c["ones1"][0:1, 0:64],
                af(c, AF_ROWS + 512, 512)[0:1, :], start=True, stop=True), sem="pe")
            p.wait("vector", "pe")
            ysl = ab(c, AB_YF + fc * T + tc * 512, 512)
            p.op("vector", lambda e, ysl=ysl: e.tensor_mul(ysl, ysl, psC[:, 0:512]), sem="dve")
    WO = AB_QF
    p.wait("gpsimd", "dve")
    for fcw in range(FC):
        p.dma("gpsimd", ab(c, WO + fcw * 1024, 1024),
              c["wo"][fcw * 128:(fcw + 1) * 128, :], "wld")
    p.wait("tensor", "wld")
    for oc in range(FC):
        for tc in range(TC):
            ps = psA if (oc * TC + tc) % 2 == 0 else psB
            p.wait("tensor", "dve")
            for hd in range(FC):
                p.op("tensor", lambda e, ps=ps, oc=oc, tc=tc, hd=hd: e.matmul(
                    ps[:, 0:512],
                    ab(c, WO + hd * 1024 + oc * 128, 128),
                    ab(c, AB_YF + hd * T + tc * 512, 512),
                    start=(hd == 0), stop=(hd == FC - 1)),
                    sem=("pe" if hd == FC - 1 else None))
            slot = (oc * TC + tc) % 2
            xs = af(c, AF_S1 + slot * 512, 512)
            res = af(c, AF_S2 + slot * 512, 512)
            p.wait("sync", "dve")
            p.dma("sync", xs, c["x_fm"][oc * 128:(oc + 1) * 128, tc * 512:(tc + 1) * 512], "ld")
            p.wait("vector", "ld")
            p.wait("vector", "pe")
            p.wait("vector", "gd")
            p.op("vector", lambda e, ps=ps, xs=xs, res=res: e.tensor_add(res, xs, ps[:, 0:512]), sem="dve")
            p.wait("sync", "dve")
            p.dma("sync", c["x1d"][oc * 128:(oc + 1) * 128, tc * 512:(tc + 1) * 512], res, "gd")


def emit_xg(c):
    p = c["p"]
    pst = c["psS"][:, 0:64].bitcast(BF16)
    for i in range(TT):
        for fc in range(FC):
            p.wait("tensor", "dve")
            if fc == 0:
                p.wait("tensor", "gd")
            p.op("tensor", lambda e, i=i, fc=fc, pst=pst: e.transpose(
                pst, ab(c, AB_XN + fc * T + i * 128, 128), c["idbt"][:, :]), sem="pe")
            p.wait("vector", "pe")
            p.op("vector", lambda e, fc=fc, pst=pst: e.tensor_copy(
                ab(c, AB_BUF_T + fc * 128, 128), pst), sem="dve")
        p.wait("sync", "dve")
        p.dma("sync", c["xgi"][i * 128:(i + 1) * 128, :], ab(c, AB_BUF_T, 1024), "gd")
    p.wait("gpsimd", "gd")
    p.op("gpsimd", lambda e: e.collective_compute(
        "AllGather", ALU.bypass, replica_groups=[list(range(NCORES))],
        ins=[c["xgi"].opt()], outs=[c["xgo"].opt()]), sem="cc")


def emit_routing(c):
    p = c["p"]
    psA, psS = c["psA"], c["psS"]
    GW = AB_EW
    p.wait("gpsimd", "pe")
    for fc in range(FC):
        p.dma("gpsimd", ab(c, GW + fc * 64, 64),
              c["gate"][fc * 128:(fc + 1) * 128, :], "wld")
    p.op("vector", lambda e: e.memset(afr(c, 0, 64, AF_CAR, 1), 0.0), sem="dve")
    p.op("vector", lambda e: e.memset(c["onesf"][:, :], 1.0), sem="dve")
    p.wait("tensor", "wld")
    for i in range(TT):
        p.wait("tensor", "dve")
        for fc in range(FC):
            p.op("tensor", lambda e, i=i, fc=fc: e.matmul(
                psA[:, 0:64],
                ab(c, AB_XN + fc * T + i * 128, 128),
                ab(c, GW + fc * 64, 64),
                start=(fc == 0), stop=(fc == FC - 1)),
                sem=("pe" if fc == FC - 1 else None))
        p.wait("vector", "pe")
        p.op("vector", lambda e: e.tensor_copy(af(c, AF_RS, 64), psA[:, 0:64]), sem="dve")
        p.op("vector", lambda e: e.max(out=af(c, AF_RS + 64, 8), in_=af(c, AF_RS, 64)), sem="dve")
        p.op("vector", lambda e: e.max_index(out=c["idx8"][:, :], in_max=af(c, AF_RS + 64, 8),
                                             in_values=af(c, AF_RS, 64)), sem="dve")
        p.op("vector", lambda e: e.tensor_copy(af(c, AF_RS + 96, 8), c["idx8"][:, :]), sem="dve")
        p.op("vector", lambda e: e.tensor_scalar(af(c, AF_RS + 72, 1), af(c, AF_RS + 64, 1),
                                                 -1.0, None, op0=ALU.mult), sem="dve")
        p.wait("scalar", "dve")
        p.op("scalar", lambda e: e.activation(out=af(c, AF_RS + 80, 8), in_=af(c, AF_RS + 64, 8),
                                              func=AF_.Exp, bias=af(c, AF_RS + 72, 1)), sem="act")
        p.wait("vector", "act")
        p.op("vector", lambda e: e.reduce_sum(af(c, AF_RS + 73, 1), af(c, AF_RS + 80, 8), axis=AX.X), sem="dve")
        p.op("vector", lambda e: e.reciprocal(af(c, AF_RS + 74, 1), af(c, AF_RS + 73, 1)), sem="dve")
        p.op("vector", lambda e: e.tensor_scalar(af(c, AF_RS + 88, 8), af(c, AF_RS + 80, 8),
                                                 af(c, AF_RS + 74, 1), None, op0=ALU.mult), sem="dve")
        p.op("vector", lambda e, i=i: e.tensor_copy(af(c, AF_RK + i * 24, 8), af(c, AF_RS + 96, 8)), sem="dve")
        p.op("vector", lambda e, i=i: e.tensor_copy(af(c, AF_RK + i * 24 + 8, 8), af(c, AF_RS + 88, 8)), sem="dve")
        io64 = c["iota64t"][:, :].rearrange("p (e z) -> p e z", z=1).to_broadcast([128, 64, 8])
        idv = af(c, AF_RS + 96, 8).rearrange("p (z j) -> p z j", z=1).to_broadcast([128, 64, 8])
        p.op("vector", lambda e, io64=io64, idv=idv: e.tensor_tensor(
            af(c, AF_RS + 512, 512).rearrange("p (ee j) -> p ee j", j=8), io64, idv,
            op=ALU.is_equal), sem="dve")
        wvv = af(c, AF_RS + 88, 8).rearrange("p (z j) -> p z j", z=1).to_broadcast([128, 64, 8])
        p.op("vector", lambda e, wvv=wvv: e.tensor_tensor(
            af(c, AF_RS + 1024, 512).rearrange("p (ee j) -> p ee j", j=8),
            af(c, AF_RS + 512, 512).rearrange("p (ee j) -> p ee j", j=8), wvv,
            op=ALU.mult), sem="dve")
        p.op("vector", lambda e: e.reduce_sum(
            af(c, AF_RS + 128, 64), af(c, AF_RS + 1024, 512).rearrange("p (ee j) -> p ee j", j=8),
            axis=AX.X), sem="dve")
        p.op("vector", lambda e: e.tensor_scalar(af(c, AF_RS + 192, 64), af(c, AF_RS + 128, 64),
                                                 0.0, None, op0=ALU.is_gt), sem="dve")
        p.wait("tensor", "dve")
        p.op("tensor", lambda e: e.transpose(psS[0:64, 0:128], af(c, AF_RS + 192, 64),
                                             c["idft"][:, :]), sem="pe")
        p.wait("vector", "pe")
        p.op("vector", lambda e: e.tensor_copy(afr(c, 0, 64, AF_RS + 256, 128), psS[0:64, 0:128]), sem="dve")
        p.op("vector", lambda e, i=i: e.tensor_tensor_scan(
            out=c["cum"][:, i * 128:(i + 1) * 128],
            data0=c["onesf"][:, :], data1=afr(c, 0, 64, AF_RS + 256, 128),
            initial=afr(c, 0, 64, AF_CAR, 1), op0=ALU.mult, op1=ALU.add), sem="dve")
        p.op("vector", lambda e, i=i: e.tensor_copy(
            afr(c, 0, 64, AF_CAR, 1), c["cum"][:, i * 128 + 127:(i + 1) * 128]), sem="dve")
    p.wait("sync", "dve")
    p.dma("sync", c["cnti"][:, :], afr(c, 0, 64, AF_CAR, 1), "gd")
    p.wait("gpsimd", "gd")
    p.op("gpsimd", lambda e: e.collective_compute(
        "AllGather", ALU.bypass, replica_groups=[list(range(NCORES))],
        ins=[c["cnti"].opt()], outs=[c["cnto"].opt()]), sem="cc")
    p.wait("sync", "cc")
    p.dma("sync", afr(c, 0, 8, AF_RS, 64),
          c["cnto"].rearrange("(r e) o -> r (e o)", e=NE), "ld")
    p.wait("tensor", "ld")
    p.wait("tensor", "dve")
    p.op("tensor", lambda e: e.matmul(psS[0:64, 128:129], afr(c, 0, 8, AF_RS, 64),
                                      c["rankmt"][:, :], start=True, stop=True), sem="pe")
    p.wait("vector", "pe")
    p.op("vector", lambda e: e.tensor_copy(afr(c, 0, 64, AF_CAR + 1, 1), psS[0:64, 128:129]), sem="dve")
    p.op("vector", lambda e: e.tensor_scalar(afr(c, 0, 64, AF_CAR + 2, 1),
                                             afr(c, 0, 64, AF_CAR + 1, 1),
                                             -1.0, None, op0=ALU.add), sem="dve")
    p.op("vector", lambda e: e.memset(af(c, AF_S1, 513), -1.0), sem="dve")
    p.wait("gpsimd", "dve")
    p.dma("gpsimd", c["recm"].flatten().rearrange("(p a) -> p a", p=128),
          af(c, AF_S1, 513), "gd")
    p.wait("gpsimd", "gd")
    for i in range(TT):
        p.wait("vector", "dve")
        p.wait("vector", "gd")
        p.op("vector", lambda e, i=i: e.tensor_scalar(
            afr(c, 0, 64, AF_S2, 128), c["cum"][:, i * 128:(i + 1) * 128],
            afr(c, 0, 64, AF_CAR + 2, 1), None, op0=ALU.add), sem="dve")
        p.wait("tensor", "dve")
        p.op("tensor", lambda e: e.transpose(psS[:, 0:64], afr(c, 0, 64, AF_S2, 128),
                                             c["idft"][0:64, 0:64]), sem="pe")
        p.wait("vector", "pe")
        p.op("vector", lambda e: e.tensor_copy(af(c, AF_S2 + 128, 64), psS[:, 0:64]), sem="dve")
        io64 = c["iota64t"][:, :].rearrange("p (z e) -> p z e", z=1).to_broadcast([128, 8, 64])
        idv = af(c, AF_RK + i * 24, 8).rearrange("p (j z) -> p j z", z=1).to_broadcast([128, 8, 64])
        p.op("vector", lambda e, io64=io64, idv=idv: e.tensor_tensor(
            af(c, AF_RS + 512, 512).rearrange("p (j ee) -> p j ee", ee=64), io64, idv,
            op=ALU.is_equal), sem="dve")
        pos = af(c, AF_S2 + 128, 64).rearrange("p (z e) -> p z e", z=1).to_broadcast([128, 8, 64])
        p.op("vector", lambda e, pos=pos: e.tensor_tensor(
            af(c, AF_RS + 1024, 512).rearrange("p (j ee) -> p j ee", ee=64),
            af(c, AF_RS + 512, 512).rearrange("p (j ee) -> p j ee", ee=64), pos,
            op=ALU.mult), sem="dve")
        p.op("vector", lambda e, i=i: e.reduce_sum(
            af(c, AF_RK + i * 24 + 16, 8),
            af(c, AF_RS + 1024, 512).rearrange("p (j ee) -> p j ee", ee=64), axis=AX.X), sem="dve")
        sl = af(c, AF_RK + i * 24 + 16, 8)
        p.op("vector", lambda e, sl=sl: e.tensor_scalar(af(c, AF_RS, 8), sl, 512.0, None,
                                                        op0=ALU.is_ge), sem="dve")
        p.op("vector", lambda e, i=i: e.tensor_scalar(
            af(c, AF_RS + 8, 8), af(c, AF_RK + i * 24, 8), 513.0, None, op0=ALU.mult), sem="dve")
        p.op("vector", lambda e, sl=sl: e.tensor_add(af(c, AF_RS + 8, 8), af(c, AF_RS + 8, 8), sl), sem="dve")
        p.op("vector", lambda e: e.scalar_tensor_tensor(
            out=af(c, AF_RS + 16, 8), in0=af(c, AF_RS, 8), scalar=BIG,
            in1=af(c, AF_RS + 8, 8), op0=ALU.mult, op1=ALU.add), sem="dve")
        p.op("vector", lambda e: e.tensor_copy(c["sidx"][:, :], af(c, AF_RS + 16, 8)), sem="dve")
        p.op("vector", lambda e, i=i: e.tensor_scalar(
            af(c, AF_RS + 24, 8), af(c, AF_RK + i * 24, 8), 512.0, None, op0=ALU.mult), sem="dve")
        p.op("vector", lambda e, sl=sl: e.tensor_add(af(c, AF_RS + 24, 8), af(c, AF_RS + 24, 8), sl), sem="dve")
        p.op("vector", lambda e: e.scalar_tensor_tensor(
            out=af(c, AF_RS + 32, 8), in0=af(c, AF_RS, 8), scalar=BIG,
            in1=af(c, AF_RS + 24, 8), op0=ALU.mult, op1=ALU.add), sem="dve")
        p.op("vector", lambda e, i=i: e.tensor_copy(c["widx"][:, i * 8:(i + 1) * 8],
                                                    af(c, AF_RS + 32, 8)), sem="dve")
        tok = c["tokidt"][:, i:i + 1].rearrange("p (o z) -> p o z", z=1).to_broadcast([128, 8, 1])
        rec = af(c, AF_RS + 40, 16).rearrange("p (j two) -> p j two", two=2)
        p.op("vector", lambda e, tok=tok, rec=rec: e.tensor_copy(rec[:, :, 0:1], tok), sem="dve")
        wk = af(c, AF_RK + i * 24 + 8, 8).rearrange("p (j z) -> p j z", z=1)
        p.op("vector", lambda e, wk=wk, rec=rec: e.tensor_copy(rec[:, :, 1:2], wk), sem="dve")
        p.wait("gpsimd", "dve")
        for j in range(TOPK):
            p.op("gpsimd", lambda e, j=j: e.indirect_dma_start(
                out=c["recm"][:, :],
                out_offset=bass.IndirectOffsetOnAxis(ap=c["sidx"][:, j:j + 1], axis=0),
                in_=af(c, AF_RS + 40 + 2 * j, 2),
                in_offset=None, bounds_check=c["bregs"][NE * 513 - 1], oob_is_err=False), sem="gd", n=16)
    p.wait("gpsimd", "gd")
    p.op("gpsimd", lambda e: e.collective_compute(
        "AllReduce", ALU.max, replica_groups=[list(range(NCORES))],
        ins=[c["recm"].opt()], outs=[c["recmr"].opt()]), sem="cc")


def emit_experts(c):
    p = c["p"]
    psA, psB, psC, psD = c["psA"], c["psB"], c["psC"], c["psD"]
    p.wait("gpsimd", "cc")
    pst = c["psS"][:, 0:64].bitcast(BF16)
    for e_i in range(EPC):
        p.wait("gpsimd", "dve")
        for gg in range(4):
            p.op("gpsimd", lambda e, e_i=e_i, gg=gg: e.indirect_dma_start(
                out=af(c, AF_S1 + gg * 2, 2), out_offset=None,
                in_=c["recmr"][:, :],
                in_offset=bass.IndirectOffsetOnAxis(
                    ap=c["recidxt"][:, e_i * 4 + gg:e_i * 4 + gg + 1], axis=0),
                bounds_check=c["bregs"][NE * 513 - 1], oob_is_err=False), sem="gd", n=16)
        p.wait("vector", "gd")
        tv = af(c, AF_S1, 8).rearrange("p (g two) -> p g two", two=2)[:, :, 0:1]
        wv = af(c, AF_S1, 8).rearrange("p (g two) -> p g two", two=2)[:, :, 1:2]
        p.op("vector", lambda e, tv=tv: e.tensor_copy(
            af(c, AF_S2, 4).rearrange("p (g z) -> p g z", z=1), tv), sem="dve")
        p.op("vector", lambda e, wv=wv: e.tensor_copy(
            af(c, AF_S2 + 8, 4).rearrange("p (g z) -> p g z", z=1), wv), sem="dve")
        p.op("vector", lambda e: e.tensor_scalar(af(c, AF_S2 + 4, 4), af(c, AF_S2, 4),
                                                 0.0, None, op0=ALU.is_lt), sem="dve")
        p.op("vector", lambda e: e.scalar_tensor_tensor(
            out=af(c, AF_S2 + 16, 4), in0=af(c, AF_S2 + 4, 4), scalar=BIG,
            in1=af(c, AF_S2, 4), op0=ALU.mult, op1=ALU.add), sem="dve")
        p.op("vector", lambda e: e.tensor_copy(c["sidx"][:, 0:4], af(c, AF_S2 + 16, 4)), sem="dve")
        p.wait("gpsimd", "dve")
        p.wait("gpsimd", "pe")
        for gg in range(4):
            p.op("gpsimd", lambda e, gg=gg: e.memset(ab(c, AB_BUF_T + gg * 1024, 1024), 0.0), sem="gd", n=1)
        for gg in range(4):
            p.op("gpsimd", lambda e, gg=gg: e.indirect_dma_start(
                out=ab(c, AB_BUF_T + gg * 1024, 1024), out_offset=None,
                in_=c["xgo"][:, :],
                in_offset=bass.IndirectOffsetOnAxis(ap=c["sidx"][:, gg:gg + 1], axis=0),
                bounds_check=c["bregs"][NCORES * T - 1], oob_is_err=False), sem="gd", n=16)
        for hc in range(FC):
            p.dma("gpsimd", ab(c, AB_EW + hc * 1024, 1024),
                  c["egu"][e_i, hc * 128:(hc + 1) * 128, :], "wld")
        p.wait("tensor", "gd")
        for gg in range(4):
            for hc in range(FC):
                p.wait("tensor", "dve")
                p.op("tensor", lambda e, gg=gg, hc=hc, pst=pst: e.transpose(
                    pst, ab(c, AB_BUF_T + gg * 1024 + hc * 128, 128), c["idbt"][:, :]), sem="pe")
                p.wait("vector", "pe")
                p.op("vector", lambda e, gg=gg, hc=hc, pst=pst: e.tensor_copy(
                    ab(c, AB_BUF_FM + hc * 512 + gg * 128, 128), pst), sem="dve")
        p.wait("tensor", "wld")
        for guc in range(8):
            ps = psA if guc % 2 == 0 else psB
            p.wait("tensor", "act")
            p.wait("tensor", "dve")
            for hc in range(FC):
                p.op("tensor", lambda e, ps=ps, guc=guc, hc=hc: e.matmul(
                    ps[:, 0:512],
                    ab(c, AB_EW + hc * 1024 + guc * 128, 128),
                    ab(c, AB_BUF_FM + hc * 512, 512),
                    start=(hc == 0), stop=(hc == FC - 1)),
                    sem=("pe" if hc == FC - 1 else None))
            if guc < 4:
                p.wait("scalar", "pe")
                p.op("scalar", lambda e, ps=ps, guc=guc: e.activation(
                    out=ab(c, AB_ACT + guc * 512, 512), in_=ps[:, 0:512],
                    func=AF_.Silu), sem="act")
            else:
                p.wait("vector", "pe")
                p.op("vector", lambda e, ps=ps, guc=guc: e.tensor_copy(
                    ab(c, AB_ACT4 + (guc - 4) * 512, 512), ps[:, 0:512]), sem="dve")
        p.wait("vector", "act")
        for ic in range(4):
            p.op("vector", lambda e, ic=ic: e.tensor_mul(
                ab(c, AB_ACT + ic * 512, 512), ab(c, AB_ACT + ic * 512, 512),
                ab(c, AB_ACT4 + ic * 512, 512)), sem="dve")
        p.wait("gpsimd", "pe")
        for icw in range(4):
            p.dma("gpsimd", ab(c, AB_EW + icw * 1024, 1024),
                  c["edw"][e_i, icw * 128:(icw + 1) * 128, :], "wld")
        p.wait("tensor", "wld")
        for sg in range(4):
            for n in range(2):
                ps = psC if (sg * 2 + n) % 2 == 0 else psD
                p.wait("tensor", "dve")
                p.wait("tensor", "gd")
                for icm in range(4):
                    p.op("tensor", lambda e, ps=ps, sg=sg, n=n, icm=icm: e.matmul(
                        ps[:, 0:512],
                        ab(c, AB_ACT + icm * 512 + sg * 128, 128),
                        ab(c, AB_EW + icm * 1024 + n * 512, 512),
                        start=(icm == 0), stop=(icm == 3)),
                        sem=("pe" if icm == 3 else None))
                p.wait("vector", "pe")
                p.wait("vector", "gd")
                p.op("vector", lambda e, ps=ps, sg=sg: e.tensor_scalar(
                    ab(c, AB_WOUT, 512), ps[:, 0:512],
                    af(c, AF_S2 + 8 + sg, 1), None, op0=ALU.mult), sem="dve")
                p.wait("sync", "dve")
                p.dma("sync", c["woutl"][e_i * 512 + sg * 128: e_i * 512 + (sg + 1) * 128,
                                         n * 512:(n + 1) * 512], ab(c, AB_WOUT, 512), "gd")
    p.wait("gpsimd", "gd")
    p.op("gpsimd", lambda e: e.collective_compute(
        "AllGather", ALU.bypass, replica_groups=[list(range(NCORES))],
        ins=[c["woutl"].opt()], outs=[c["woutg"].opt()]), sem="cc")


def emit_shared(c):
    p = c["p"]
    psA, psB, psC, psD = c["psA"], c["psB"], c["psC"], c["psD"]
    for fc in range(FC):
        p.dma("gpsimd", c["sgwb"][:, fc:fc + 1], c["sgw"][fc * 128:(fc + 1) * 128, :], "wld")
    p.wait("tensor", "wld")
    for i in range(TT):
        p.wait("tensor", "act")
        p.wait("tensor", "dve")
        for fc in range(FC):
            p.op("tensor", lambda e, i=i, fc=fc: e.matmul(
                psC[:, 0:1],
                ab(c, AB_XN + fc * T + i * 128, 128),
                c["sgwb"][:, fc:fc + 1],
                start=(fc == 0), stop=(fc == FC - 1)),
                sem=("pe" if fc == FC - 1 else None))
        p.wait("scalar", "pe")
        p.op("scalar", lambda e, i=i: e.activation(
            out=af(c, AF_RK + 400 + i, 1), in_=psC[:, 0:1], func=AF_.Sigmoid), sem="act")
    p.wait("sync", "act")
    for i in range(TT):
        p.dma("sync", c["sgd"][i * 128:(i + 1) * 128].rearrange("(p o) -> p o", o=1),
              af(c, AF_RK + 400 + i, 1), "gd")
    p.wait("sync", "gd")
    p.dma("sync", af(c, AF_LNR, 2048)[0:1, :], c["sgd"].rearrange("(o a) -> o a", o=1), "ld")
    for i4 in range(4):
        for icl in range(4):
            ic4 = i4 * 4 + icl
            p.wait("gpsimd", "pe")
            gsrc = c["sgu"][:, ic4 * 128:(ic4 + 1) * 128].rearrange("(h p) n -> p h n", p=128)
            usrc = c["sgu"][:, ISH + ic4 * 128:ISH + (ic4 + 1) * 128].rearrange("(h p) n -> p h n", p=128)
            gdst = ab(c, AB_EW, 2048).rearrange("p (h n) -> p h n", n=256)[:, :, 0:128]
            udst = ab(c, AB_EW, 2048).rearrange("p (h n) -> p h n", n=256)[:, :, 128:256]
            p.dma("gpsimd", gdst, gsrc, "wld")
            p.dma("gpsimd", udst, usrc, "wld")
            p.wait("tensor", "wld")
            for tc in range(TC):
                p.wait("tensor", "act")
                p.wait("tensor", "dve")
                for hc in range(FC):
                    p.op("tensor", lambda e, tc=tc, hc=hc: e.matmul(
                        psA[:, 0:512],
                        ab(c, AB_EW + hc * 256, 128),
                        ab(c, AB_XN + hc * T + tc * 512, 512),
                        start=(hc == 0), stop=(hc == FC - 1)),
                        sem=("pe" if hc == FC - 1 else None))
                for hc in range(FC):
                    p.op("tensor", lambda e, tc=tc, hc=hc: e.matmul(
                        psB[:, 0:512],
                        ab(c, AB_EW + hc * 256 + 128, 128),
                        ab(c, AB_XN + hc * T + tc * 512, 512),
                        start=(hc == 0), stop=(hc == FC - 1)),
                        sem=("pe" if hc == FC - 1 else None))
                p.wait("scalar", "pe")
                p.op("scalar", lambda e: e.activation(
                    out=ab(c, AB_WOUT, 512), in_=psA[:, 0:512], func=AF_.Silu), sem="act")
                p.wait("vector", "act")
                p.op("vector", lambda e, icl=icl, tc=tc: e.tensor_mul(
                    ab(c, AB_ACT4 + icl * 2048 + tc * 512, 512),
                    ab(c, AB_WOUT, 512), psB[:, 0:512]), sem="dve")
        p.wait("gpsimd", "pe")
        for icl in range(4):
            ic4 = i4 * 4 + icl
            p.dma("gpsimd", ab(c, AB_EW + 2048 + icl * 1024, 1024),
                  c["sdw"][ic4 * 128:(ic4 + 1) * 128, :], "wld")
        p.wait("tensor", "wld")
        p.wait("tensor", "dve")
        for hc in range(FC):
            for tc in range(TC):
                ps = psC if (hc * TC + tc) % 2 == 0 else psD
                p.wait("tensor", "dve")
                for icl in range(4):
                    p.op("tensor", lambda e, ps=ps, hc=hc, tc=tc, icl=icl: e.matmul(
                        ps[:, 0:512],
                        ab(c, AB_EW + 2048 + icl * 1024 + hc * 128, 128),
                        ab(c, AB_ACT4 + icl * 2048 + tc * 512, 512),
                        start=(icl == 0), stop=(icl == 3)),
                        sem=("pe" if icl == 3 else None))
                p.wait("vector", "pe")
                dst = ab(c, AB_SHACC + hc * T + tc * 512, 512)
                if i4 == 0:
                    p.op("vector", lambda e, ps=ps, dst=dst: e.tensor_copy(dst, ps[:, 0:512]), sem="dve")
                else:
                    p.op("vector", lambda e, ps=ps, dst=dst: e.tensor_add(dst, dst, ps[:, 0:512]), sem="dve")


COMB_PARTS = 3


def emit_combine(c):
    p = c["p"]
    psD, psS = c["psD"], c["psS"]
    p.wait("gpsimd", "cc")
    for tc in range(TC):
        p.wait("tensor", "dve")
        p.wait("tensor", "ld")
        p.op("tensor", lambda e, tc=tc: e.matmul(
            psD[:, 0:512], c["ones1"][0:1, 0:128],
            af(c, AF_LNR + tc * 512, 512)[0:1, :], start=True, stop=True), sem="pe")
        for ii in range(4):
            i = tc * 4 + ii
            slot_a = AF_S1 + (i % 2) * 1024
            slot_x = AF_S2 + (i % 2) * 1024
            p.wait("sync", "dve")
            x1src = c["x1d"][:, i * 128:(i + 1) * 128].rearrange("(h p) n -> p h n", p=128)
            x1dst = af(c, slot_x, 1024).rearrange("p (h n) -> p h n", n=128)
            p.dma("sync", x1dst, x1src, "ld")
            p.wait("gpsimd", "dve")
            p.op("gpsimd", lambda e: e.memset(ab(c, AB_QF, 8192), 0.0), sem="gd", n=1)
            for j in range(TOPK):
                p.op("gpsimd", lambda e, i=i, j=j: e.indirect_dma_start(
                    out=ab(c, AB_QF + j * 1024, 1024), out_offset=None,
                    in_=c["woutg"][:, :],
                    in_offset=bass.IndirectOffsetOnAxis(
                        ap=c["widx"][:, i * 8 + j:i * 8 + j + 1], axis=0),
                    bounds_check=c["bregs"][NE * CAP - 1], oob_is_err=False), sem="gd", n=16)
            p.wait("vector", "gd")
            p.wait("vector", "pe")
            p.op("vector", lambda e, slot_a=slot_a: e.tensor_add(
                af(c, slot_a, 1024), ab(c, AB_QF, 1024), ab(c, AB_QF + 1024, 1024)), sem="dve")
            for j in range(2, TOPK):
                p.op("vector", lambda e, j=j, slot_a=slot_a: e.tensor_add(
                    af(c, slot_a, 1024), af(c, slot_a, 1024),
                    ab(c, AB_QF + j * 1024, 1024)), sem="dve")
            for fc in range(FC):
                p.wait("tensor", "ld")
                p.wait("tensor", "dve")
                p.op("tensor", lambda e, fc=fc, slot_a=slot_a: e.transpose(
                    psS[:, 0:128], af(c, slot_a + fc * 128, 128), c["idft"][:, :]), sem="pe")
                slot = fc % 2
                p.wait("vector", "pe")
                p.wait("vector", "ld")
                p.wait("vector", "st")
                p.op("vector", lambda e, fc=fc, i=i, slot=slot: e.tensor_mul(
                    af(c, AF_RS + slot * 256, 128),
                    ab(c, AB_SHACC + fc * T + i * 128, 128),
                    psD[:, (i % 4) * 128:(i % 4) * 128 + 128]), sem="dve")
                p.op("vector", lambda e, slot=slot: e.tensor_add(
                    af(c, AF_RS + slot * 256 + 128, 128),
                    af(c, AF_RS + slot * 256, 128), psS[:, 0:128]), sem="dve")
                p.op("vector", lambda e, fc=fc, slot=slot, slot_x=slot_x: e.tensor_add(
                    af(c, AF_RS + slot * 256 + 128, 128),
                    af(c, AF_RS + slot * 256 + 128, 128),
                    af(c, slot_x + fc * 128, 128)), sem="dve")
                p.wait("sync", "dve")
                p.dma("sync", c["out_fm"][fc * 128:(fc + 1) * 128, i * 128:(i + 1) * 128],
                      af(c, AF_RS + slot * 256 + 128, 128), "st")


def emit_dump(c, regions):
    p = c["p"]
    p.wait("gpsimd", "dve")
    p.wait("gpsimd", "pe")
    p.wait("gpsimd", "act")
    p.wait("gpsimd", "gd")
    p.wait("gpsimd", "wld")
    p.wait("sync", "dve")
    p.wait("sync", "pe")
    p.wait("sync", "gd")
    p.wait("sync", "act")
    p.wait("sync", "wld")
    p.wait("sync", "cc")
    if "ab" in regions:
        CHW = 8192
        for o in range(0, AB_N, CHW):
            w = min(CHW, AB_N - o)
            p.dma("gpsimd", c["dbg_ab"][:, o:o + w], c["AB"][:, o:o + w], "st")
    if "af" in regions:
        p.dma("sync", c["dbg_af"][:, :], c["AF"][:, :], "st")
    if "sag" in regions:
        p.dma("sync", c["dbg_sag"][0:128, :], af(c, AF_SSB, 1040), "st")
        p.dma("sync", c["dbg_sag"][128:160, :], c["ssbB"][:, :], "st")
    if "rec" in regions:
        p.dma("sync", c["dbg_rec"][:, :], c["recmr"][:, :], "st")
    if "cumd" in regions:
        p.dma("sync", c["dbg_cum"][:, :], c["cum"][:, :], "st")
    if "widxd" in regions:
        p.dma("sync", c["dbg_widx"][:, :], c["widx"][:, :], "st")
    if "woutd" in regions:
        for o in range(0, EPC * CAP, 512):
            p.dma("gpsimd", c["dbg_wout"][o:o + 512, :], c["woutl"][o:o + 512, :], "st")
    p.wait("gpsimd", "st")
    p.wait("sync", "st")



def make_inputs(inputs):
    x = np.asarray(inputs["L0_attn_hidden_states"])
    Wq = np.asarray(inputs["L0_attn_Wq"])
    Wk = np.asarray(inputs["L0_attn_Wk"])
    Wv = np.asarray(inputs["L0_attn_Wv"])
    Wo = np.asarray(inputs["L0_attn_Wo"])
    gate = np.asarray(inputs["L0_ffn_gate_w"])
    sgw = np.asarray(inputs["L0_ffn_shared_gate_w"])
    sgu = np.asarray(inputs["L0_ffn_shared_gate_up_w"])
    sdw = np.asarray(inputs["L0_ffn_shared_down_w"])
    egu = np.asarray(inputs["L0_ffn_expert_gate_up_w"])
    edw = np.asarray(inputs["L0_ffn_expert_down_w"])

    wqk = np.concatenate([_pad_qk_weight(Wq), _pad_qk_weight(Wk)], axis=1)
    wvT = np.ascontiguousarray(Wv.T)
    woT = np.ascontiguousarray(Wo.T)
    Ls, Rs = _build_selection()
    msk = _causal_masks()
    iota64 = np.broadcast_to(np.arange(NE, dtype=np.float32)[None, :], (128, NE)).copy()

    in_maps = []
    for core in range(NCORES):
        b, half = core // 2, core % 2
        shard = x[b, half * T:(half + 1) * T, :]
        x_fm = np.ascontiguousarray(shard.T)
        tokid = (core * T + np.arange(T, dtype=np.float32)).reshape(TT, 128).T.copy()
        rankm = (np.arange(8) < core).astype(np.float32).reshape(8, 1)
        selm_v = np.zeros(8, np.float32)
        if half == 1:
            selm_v[core - 1] = 1.0
        selm = np.broadcast_to(selm_v[None, :], (128, 8)).copy()
        ridx = np.zeros((128, EPC * 4), np.int32)
        pcol = np.arange(128, dtype=np.int32)
        for e_i in range(EPC):
            for g in range(4):
                ridx[:, e_i * 4 + g] = (core * EPC + e_i) * 513 + g * 128 + pcol
        in_maps.append({
            "x_fm": x_fm,
            "wqk": wqk, "wv": wvT, "wo": woT,
            "gate": gate, "sgw": sgw, "sgu": sgu, "sdw": sdw,
            "egu": np.ascontiguousarray(egu[core * EPC:(core + 1) * EPC]),
            "edw": np.ascontiguousarray(edw[core * EPC:(core + 1) * EPC]),
            "lsel": Ls.astype(ml_dtypes.bfloat16),
            "rsel": Rs.astype(ml_dtypes.bfloat16),
            "msk": msk.astype(ml_dtypes.bfloat16),
            "idf": np.eye(128, dtype=np.float32),
            "idb": np.eye(128, dtype=ml_dtypes.bfloat16),
            "iota64": iota64,
            "tokid": tokid,
            "rankm": rankm,
            "selm": selm,
            "recidx": ridx,
            "cvals": np.broadcast_to(
                np.array([FSC, FB, LNEPS, 0, 0, 0, 0, 0], np.float32)[None, :],
                (128, 8)).copy(),
        })
    return in_maps


def kernel(**inputs):
    in_maps = make_inputs(inputs)
    nc = build_program()
    res = run_bass_kernel_spmd(nc, in_maps, core_ids=list(range(NCORES)))
    out = np.empty((B, S, HID), np.float32)
    for core in range(NCORES):
        b, half = core // 2, core % 2
        out[b, half * T:(half + 1) * T, :] = res.results[core]["out_fm"].T
    return out


# version 03
# speedup vs baseline: 1.0332x; 1.0093x over previous
import math
from collections import defaultdict

import numpy as np
import ml_dtypes

import concourse.bass as bass
import concourse.mybir as mybir
from concourse.bass_utils import run_bass_kernel_spmd

F32 = mybir.dt.float32
BF16 = mybir.dt.bfloat16
I32 = mybir.dt.int32
U32 = mybir.dt.uint32
AF_ = mybir.ActivationFunctionType
ALU = mybir.AluOpType
AX = mybir.AxisListType

B, S, HID = 4, 4096, 1024
NCORES = 8
T = 2048
TT = 16
TC = 4
FC = 8
NH = 16
DPHI = 153
CH = 8
NE = 64
EPC = 8
CAP = 512
TOPK = 8
ISH = 2048
IEX = 512

RDs = 4.0
RRD = 2.0
R2 = math.sqrt(2.0)
FSC = 1.0 / math.sqrt(32.0)
FB = 3.0 * FSC
BIG = 1.0e9
LNEPS = 1e-5

AB_XN = 0
AB_QF = 16384
AB_KF = 24576
AB_VT = 32768
AB_YF = 49408
AB_N = 65792
AB_PHIQA = AB_XN + 0
AB_PHIQB = AB_XN + 2048
AB_PHIK = AB_XN + 4096
AB_A4 = AB_XN + 5120
AB_STG = AB_XN + 7168
AB_BUF_FM = AB_QF + 0
AB_BUF_T = AB_QF + 4096
AB_EW = AB_KF + 0
AB_ACT = AB_VT + 0
AB_WOUT = AB_VT + 2048
AB_ACT4 = AB_VT + 4096
AB_SHACC = AB_YF

AF_S1 = 0
AF_S2 = 2048
AF_DEN = 4096
AF_SSB = 4352
AF_CAR = 5392
AF_RS = 5408
AF_RK = 6944
AF_ROWS = 7456
AF_LNR = 8480
AF_N = 12608


def _build_selection():
    L = np.zeros((32, DPHI), np.float32)
    R = np.zeros((32, DPHI), np.float32)
    L[16, 0] = 1.0
    R[16, 0] = 1.0
    for i in range(16):
        L[i, 1 + i] = 1.0 / RRD
        R[16, 1 + i] = 1.0
    for i in range(16):
        L[i, 17 + i] = 1.0 / (RDs * R2)
        R[i, 17 + i] = 1.0
    cc = 33
    for i in range(16):
        for j in range(i + 1, 16):
            L[i, cc] = 1.0 / RDs
            R[j, cc] = 1.0
            cc += 1
    assert cc == DPHI
    Ls = np.zeros((128, DPHI), np.float32)
    Rs = np.zeros((128, DPHI), np.float32)
    for g in range(4):
        Ls[g * 32:(g + 1) * 32] = L
        Rs[g * 32:(g + 1) * 32] = R
    return Ls, Rs


def _pad_qk_weight(W):
    Wp = np.zeros((HID, 512), np.float32)
    for h in range(NH):
        Wp[:, h * 32: h * 32 + 16] = W[h * 16:(h + 1) * 16, :].T
    return Wp


def _causal_masks():
    j = np.arange(256)[None, :]
    p = np.arange(128)[:, None]
    m0 = (p <= j).astype(np.float32)
    m1 = (p + 128 <= j).astype(np.float32)
    return np.concatenate([m0, m1], axis=1)


ENGINES = ("sync", "scalar", "vector", "tensor", "gpsimd")


class P:
    def __init__(self, nc, sems):
        self.nc = nc
        self.sems = sems
        self.streams = {e: [] for e in ENGINES}
        self.cnt = defaultdict(int)
        self.lastw = {}

    def op(self, eng, fn, sem=None, n=1):
        if eng == "vector":
            assert sem == "dve", "vector ops must increment dve"
            self.wait(eng, "dve")
        elif eng == "scalar" and sem == "act":
            self.wait(eng, "act")
        if sem is None:
            self.streams[eng].append(fn)
        else:
            self.cnt[sem] += n
            s = self.sems[sem]
            self.streams[eng].append(lambda e, fn=fn, s=s, n=n: fn(e).then_inc(s, n))

    def wait(self, eng, sem, val=None):
        v = self.cnt[sem] if val is None else val
        if v <= 0:
            return
        key = (eng, sem)
        if self.lastw.get(key, -1) >= v:
            return
        self.lastw[key] = v
        s = self.sems[sem]
        self.streams[eng].append(lambda e, s=s, v=v: e.wait_ge(s, v))

    def dma(self, eng, out, in_, sem, **kw):
        self.op(eng, lambda e, out=out, in_=in_, kw=kw: e.dma_start(out=out, in_=in_, **kw),
                sem=sem, n=16)

    def replay(self, block):
        def runner(name):
            def run(e):
                for f in self.streams[name]:
                    f(e)
            return run
        block.sync(runner("sync"))
        block.scalar(runner("scalar"))
        block.vector(runner("vector"))
        block.tensor(runner("tensor"))
        block.gpsimd(runner("gpsimd"))


def build_program(debug_stage=None):
    nc = bass.Bass("TRN2", target_bir_lowering=False)
    dt = nc.dram_tensor

    io = {}
    io["x_fm"] = dt("x_fm", [HID, T], F32, kind="ExternalInput").ap()
    io["wqk"] = dt("wqk", [HID, 1024], F32, kind="ExternalInput").ap()
    io["wv"] = dt("wv", [HID, HID], F32, kind="ExternalInput").ap()
    io["wo"] = dt("wo", [HID, HID], F32, kind="ExternalInput").ap()
    io["gate"] = dt("gate", [HID, NE], F32, kind="ExternalInput").ap()
    io["sgw"] = dt("sgw", [HID, 1], F32, kind="ExternalInput").ap()
    io["sgu"] = dt("sgu", [HID, 2 * ISH], F32, kind="ExternalInput").ap()
    io["sdw"] = dt("sdw", [ISH, HID], F32, kind="ExternalInput").ap()
    io["egu"] = dt("egu", [EPC, HID, 2 * IEX], F32, kind="ExternalInput").ap()
    io["edw"] = dt("edw", [EPC, IEX, HID], F32, kind="ExternalInput").ap()
    io["lsel"] = dt("lsel", [128, DPHI], BF16, kind="ExternalInput").ap()
    io["rsel"] = dt("rsel", [128, DPHI], BF16, kind="ExternalInput").ap()
    io["msk"] = dt("msk", [128, 512], BF16, kind="ExternalInput").ap()
    io["idf"] = dt("idf", [128, 128], F32, kind="ExternalInput").ap()
    io["idb"] = dt("idb", [128, 128], BF16, kind="ExternalInput").ap()
    io["iota64"] = dt("iota64", [128, NE], F32, kind="ExternalInput").ap()
    io["tokid"] = dt("tokid", [128, TT], F32, kind="ExternalInput").ap()
    io["rankm"] = dt("rankm", [8, 1], F32, kind="ExternalInput").ap()
    io["selm"] = dt("selm", [128, 8], F32, kind="ExternalInput").ap()
    io["recidx"] = dt("recidx", [128, EPC * 4], I32, kind="ExternalInput").ap()
    io["cvals"] = dt("cvals", [128, 8], F32, kind="ExternalInput").ap()
    io["out_fm"] = dt("out_fm", [HID, T], F32, kind="ExternalOutput").ap()

    io["x1d"] = dt("x1d", [HID, T], F32).ap()
    io["onesd"] = dt("onesd", [2048], BF16).ap()
    io["lnd"] = dt("lnd", [4096], F32).ap()
    io["sgd"] = dt("sgd", [T], F32).ap()
    io["sagi"] = dt("sagi", [160, NH * 65], F32).ap()
    io["sago"] = dt("sago", [8 * 160, NH * 65], F32).ap()
    io["cnti"] = dt("cnti", [NE, 1], F32).ap()
    io["cnto"] = dt("cnto", [8 * NE, 1], F32).ap()
    io["recm"] = dt("recm", [NE * 513, 2], F32).ap()
    io["recmr"] = dt("recmr", [NE * 513, 2], F32).ap()
    io["xgi"] = dt("xgi", [T, HID], BF16).ap()
    io["xgo"] = dt("xgo", [NCORES * T, HID], BF16).ap()
    io["woutl"] = dt("woutl", [EPC * CAP, HID], BF16).ap()
    io["ymd"] = dt("ymd", [T, HID], F32).ap()
    io["woutg"] = dt("woutg", [NE * CAP, HID], BF16).ap()

    if debug_stage is not None:
        io["dbg_ab"] = dt("dbg_ab", [128, AB_N], F32, kind="ExternalOutput").ap()
        io["dbg_af"] = dt("dbg_af", [128, AF_N], F32, kind="ExternalOutput").ap()
        io["dbg_rec"] = dt("dbg_rec", [NE * 513, 2], F32, kind="ExternalOutput").ap()
        io["dbg_sag"] = dt("dbg_sag", [160, NH * 65], F32, kind="ExternalOutput").ap()
        io["dbg_cum"] = dt("dbg_cum", [64, 2048], F32, kind="ExternalOutput").ap()
        io["dbg_widx"] = dt("dbg_widx", [128, 128], I32, kind="ExternalOutput").ap()
        io["dbg_wout"] = dt("dbg_wout", [EPC * CAP, HID], F32, kind="ExternalOutput").ap()

    from contextlib import ExitStack
    with ExitStack() as ctx:
        en = ctx.enter_context
        AB = en(nc.sbuf_tensor("AB", [128, AB_N], BF16))
        AFr = en(nc.sbuf_tensor("AFr", [128, AF_N], F32))
        cumt = en(nc.sbuf_tensor("cum", [64, 2048], F32))
        ssbB = en(nc.sbuf_tensor("ssbB", [32, NH * 65], F32))
        sbf = en(nc.sbuf_tensor("sbf", [128, NH * 65], BF16))
        sbfB = en(nc.sbuf_tensor("sbfB", [32, NH * 65], BF16))
        spv = en(nc.sbuf_tensor("spv", [128, NH * 65], BF16))
        spvB = en(nc.sbuf_tensor("spvB", [32, NH * 65], BF16))
        lselt = en(nc.sbuf_tensor("lsel_sb", [128, DPHI], BF16))
        rselt = en(nc.sbuf_tensor("rsel_sb", [128, DPHI], BF16))
        mskt = en(nc.sbuf_tensor("mskt", [128, 512], BF16))
        idft = en(nc.sbuf_tensor("idf_sb", [128, 128], F32))
        idbt = en(nc.sbuf_tensor("idb_sb", [128, 128], BF16))
        iota64t = en(nc.sbuf_tensor("iota64_sb", [128, NE], F32))
        tokidt = en(nc.sbuf_tensor("tokid_sb", [128, TT], F32))
        selmt = en(nc.sbuf_tensor("selm_sb", [128, 8], F32))
        rankmt = en(nc.sbuf_tensor("rankm_sb", [8, 1], F32))
        recidxt = en(nc.sbuf_tensor("recidxt", [128, EPC * 4], I32))
        ones1 = en(nc.sbuf_tensor("ones1", [1, 512], F32))
        onesb = en(nc.sbuf_tensor("onesb", [1, 2048], BF16))
        onesc = en(nc.sbuf_tensor("onesc", [128, 1], F32))
        sgwb = en(nc.sbuf_tensor("sgwb", [128, 8], BF16))
        cvalt = en(nc.sbuf_tensor("cvalt", [128, 8], F32))
        idx8 = en(nc.sbuf_tensor("idx8", [128, 8], U32))
        sidx = en(nc.sbuf_tensor("sidx", [128, 8], I32))
        psA = en(nc.psum_tensor("psA", [128, 512], F32))
        psB = en(nc.psum_tensor("psB", [128, 512], F32))
        psC = en(nc.psum_tensor("psC", [128, 512], F32))
        psD = en(nc.psum_tensor("psD", [128, 512], F32))
        psY = en(nc.psum_tensor("psY", [65, 1024], F32))
        psS = en(nc.psum_tensor("psS", [128, 260], F32))
        psSB = en(nc.psum_tensor("psSB", [32, 260], F32))
        s_ld = en(nc.semaphore("ld"))
        s_wld = en(nc.semaphore("wld"))
        s_pe = en(nc.semaphore("pe"))
        s_act = en(nc.semaphore("act"))
        s_dve = en(nc.semaphore("dve"))
        s_gd = en(nc.semaphore("gd"))
        s_cc = en(nc.semaphore("cc"))
        s_st = en(nc.semaphore("st"))
        block = en(nc.Block())
        sems = dict(ld=s_ld, wld=s_wld, pe=s_pe, act=s_act, dve=s_dve,
                    gd=s_gd, cc=s_cc, st=s_st)
        p = P(nc, sems)
        c = dict(io)
        c.update(nc=nc, p=p, AB=AB, AF=AFr, cum=cumt, ssbB=ssbB, sbf=sbf,
                 sbfB=sbfB, spv=spv, spvB=spvB, lselt=lselt, rselt=rselt,
                 mskt=mskt, idft=idft, idbt=idbt, iota64t=iota64t,
                 tokidt=tokidt, selmt=selmt, rankmt=rankmt, recidxt=recidxt,
                 ones1=ones1, onesb=onesb, onesc=onesc,
                 onesf=AFr[0:64, AF_LNR + 2048:AF_LNR + 2048 + 128],
                 widx=AFr[:, AF_LNR + 2304:AF_LNR + 2304 + 128].bitcast(I32),
                 sgwb=sgwb, idx8=idx8, sidx=sidx,
                 psA=psA, psB=psB, psC=psC, psD=psD, psY=psY, psS=psS,
                 psSB=psSB, cvalt=cvalt, debug_stage=debug_stage)
        nc.const_aps.register(F32, FSC, cvalt[:, 0:1])
        nc.const_aps.register(F32, FB, cvalt[:, 1:2])
        nc.const_aps.register(F32, LNEPS, cvalt[:, 2:3])
        emit_all(c)
        p.replay(block)
    return nc


def ab(c, off, n):
    return c["AB"][:, off:off + n]


def abr(c, r0, r1, off, n):
    return c["AB"][r0:r1, off:off + n]


def af(c, off, n):
    return c["AF"][:, off:off + n]


def afr(c, r0, r1, off, n):
    return c["AF"][r0:r1, off:off + n]


def emit_all(c):
    p = c["p"]
    emit_setup(c)
    emit_ln(c, src="x0")
    emit_projections(c)
    if c["debug_stage"] == "proj":
        emit_dump(c, regions=["ab", "af"])
        return
    emit_scan(c)
    if c["debug_stage"] == "scan":
        emit_dump(c, regions=["ab", "af", "sag"])
        return
    emit_handoff(c)
    emit_correction(c)
    if c["debug_stage"] == "attn":
        emit_dump(c, regions=["ab", "af", "sag"])
        return
    emit_norm_wo_resid(c)
    emit_ln(c, src="x1")
    if c["debug_stage"] == "ln2":
        emit_dump(c, regions=["ab"])
        return
    emit_xg(c)
    emit_routing(c)
    if c["debug_stage"] == "routing":
        emit_dump(c, regions=["af", "rec", "cumd", "widxd"])
        return
    emit_experts(c)
    if c["debug_stage"] == "experts":
        emit_dump(c, regions=["af", "rec", "widxd", "woutd"])
        return
    emit_shared(c)
    if c["debug_stage"] == "shared":
        emit_dump(c, regions=["ab", "af"])
        return
    emit_combine(c)
    if c["debug_stage"] == "full":
        emit_dump(c, regions=["ab", "af"])
        return
    p.wait("gpsimd", "st")


def emit_setup(c):
    p = c["p"]
    c["bregs"] = {}

    def _alloc_bounds(e):
        for v in (NE * 513 - 1, NCORES * T - 1, NE * CAP - 1):
            r = e.alloc_register()
            e.reg_mov(r, v)
            c["bregs"][v] = r
        return None
    p.streams["gpsimd"].append(lambda e: _alloc_bounds(e))
    for name, tile in [("lsel", c["lselt"]), ("rsel", c["rselt"]),
                       ("msk", c["mskt"]), ("idf", c["idft"]), ("idb", c["idbt"]),
                       ("iota64", c["iota64t"]), ("tokid", c["tokidt"]),
                       ("selm", c["selmt"]), ("rankm", c["rankmt"]),
                       ("recidx", c["recidxt"]), ("cvals", c["cvalt"])]:
        p.dma("sync", tile[:], c[name][:], "ld")
    p.op("vector", lambda e: e.memset(c["ones1"][:], 1.0), sem="dve")
    p.op("vector", lambda e: e.memset(c["onesb"][:], 1.0), sem="dve")
    p.op("vector", lambda e: e.memset(c["onesc"][:], 1.0), sem="dve")
    p.op("vector", lambda e: e.memset(af(c, AF_SSB, 1040), 0.0), sem="dve")
    p.op("vector", lambda e: e.memset(c["ssbB"][:], 0.0), sem="dve")
    p.op("vector", lambda e: e.memset(c["sbf"][:], 0.0), sem="dve")
    p.op("vector", lambda e: e.memset(c["sbfB"][:], 0.0), sem="dve")
    p.op("vector", lambda e: e.memset(af(c, AF_DEN, 256), 0.0), sem="dve")
    p.wait("sync", "dve")
    p.dma("sync", c["onesd"].rearrange("(o a) -> o a", o=1), c["onesb"][0:1, :], "ld")


def emit_ln(c, src):
    p = c["p"]
    xsrc = c["x_fm"] if src == "x0" else c["x1d"]
    psA, psB, psC, psD = c["psA"], c["psB"], c["psC"], c["psD"]
    p.wait("sync", "gd")

    for tc in range(TC):
        for fc in range(FC):
            slot = fc % 2
            xs = af(c, AF_S1 + slot * 512, 512)
            sq = af(c, AF_S2 + slot * 512, 512)
            p.wait("sync", "pe")
            p.wait("sync", "act")
            p.dma("sync", xs, xsrc[fc * 128:(fc + 1) * 128, tc * 512:(tc + 1) * 512], "ld")
            p.wait("scalar", "ld")
            p.wait("scalar", "pe")
            p.op("scalar", lambda e, xs=xs, sq=sq: e.activation(
                out=sq, in_=xs, func=AF_.Square), sem="act")
            if fc == 0:
                p.wait("tensor", "dve")
            p.wait("tensor", "ld")
            p.wait("tensor", "act")
            p.op("tensor", lambda e, xs=xs, fc=fc: e.matmul(
                psA[0:1, 0:512], c["onesc"][:, 0:1], xs,
                start=(fc == 0), stop=(fc == FC - 1)), sem="pe")
            p.op("tensor", lambda e, sq=sq, fc=fc: e.matmul(
                psB[0:1, 0:512], c["onesc"][:, 0:1], sq,
                start=(fc == 0), stop=(fc == FC - 1)), sem="pe")
        p.wait("vector", "pe")
        p.wait("vector", "ld")
        p.op("vector", lambda e: e.tensor_copy(af(c, AF_ROWS, 512)[0:1, :], psA[0:1, 0:512]), sem="dve")
        p.wait("sync", "dve")
        p.dma("sync", c["lnd"][tc * 512:(tc + 1) * 512].rearrange("(o a) -> o a", o=1),
              af(c, AF_ROWS, 512)[0:1, :], "ld")
        p.wait("vector", "ld")
        p.op("vector", lambda e: e.tensor_copy(af(c, AF_ROWS, 512)[0:1, :], psB[0:1, 0:512]), sem="dve")
        p.wait("sync", "dve")
        p.dma("sync", c["lnd"][2048 + tc * 512:2048 + (tc + 1) * 512].rearrange("(o a) -> o a", o=1),
              af(c, AF_ROWS, 512)[0:1, :], "ld")

    p.wait("sync", "ld")
    p.wait("sync", "pe")
    p.wait("sync", "act")
    p.dma("sync", af(c, AF_S1, 32), c["lnd"].rearrange("(p a) -> p a", p=128), "ld")
    p.wait("vector", "ld")
    inv = 1.0 / HID
    p.op("vector", lambda e: e.tensor_copy(afr(c, 0, 64, AF_S1 + 32, 32),
                                           afr(c, 64, 128, AF_S1, 32)), sem="dve")
    p.op("vector", lambda e: e.tensor_scalar(afr(c, 0, 64, AF_S2, 32), afr(c, 0, 64, AF_S1, 32),
                                             inv, None, op0=ALU.mult), sem="dve")
    p.op("vector", lambda e: e.tensor_scalar(afr(c, 0, 64, AF_S2 + 32, 32),
                                             afr(c, 0, 64, AF_S1 + 32, 32),
                                             inv, None, op0=ALU.mult), sem="dve")
    p.op("vector", lambda e: e.tensor_mul(afr(c, 0, 64, AF_S1 + 64, 32),
                                          afr(c, 0, 64, AF_S2, 32),
                                          afr(c, 0, 64, AF_S2, 32)), sem="dve")
    p.op("vector", lambda e: e.tensor_sub(afr(c, 0, 64, AF_S1 + 96, 32),
                                          afr(c, 0, 64, AF_S2 + 32, 32),
                                          afr(c, 0, 64, AF_S1 + 64, 32)), sem="dve")
    p.wait("scalar", "dve")
    p.op("scalar", lambda e: e.activation(out=afr(c, 0, 64, AF_S2 + 64, 32),
                                          in_=afr(c, 0, 64, AF_S1 + 96, 32),
                                          func=AF_.Sqrt, bias=LNEPS), sem="act")
    p.wait("vector", "act")
    p.op("vector", lambda e: e.reciprocal(afr(c, 0, 64, AF_S2 + 96, 32),
                                          afr(c, 0, 64, AF_S2 + 64, 32)), sem="dve")
    p.op("vector", lambda e: e.tensor_mul(afr(c, 0, 64, AF_S1 + 64, 32),
                                          afr(c, 0, 64, AF_S2, 32),
                                          afr(c, 0, 64, AF_S2 + 96, 32)), sem="dve")
    p.wait("sync", "dve")
    p.dma("sync", c["lnd"][0:2048].rearrange("(p a) -> p a", p=64),
          afr(c, 0, 64, AF_S2 + 96, 32), "ld")
    p.dma("sync", c["lnd"][2048:4096].rearrange("(p a) -> p a", p=64),
          afr(c, 0, 64, AF_S1 + 64, 32), "ld")
    p.wait("sync", "ld")
    p.dma("sync", af(c, AF_LNR, 2048)[0:1, :],
          c["lnd"][0:2048].rearrange("(o a) -> o a", o=1), "ld")
    p.dma("sync", af(c, AF_LNR + 2048, 2048)[0:1, :],
          c["lnd"][2048:4096].rearrange("(o a) -> o a", o=1), "ld")

    for tc in range(TC):
        p.wait("tensor", "ld")
        p.wait("tensor", "dve")
        p.op("tensor", lambda e, tc=tc: e.matmul(
            psC[:, 0:512], c["ones1"][0:1, 0:128],
            af(c, AF_LNR + tc * 512, 512)[0:1, :], start=True, stop=True), sem="pe")
        p.op("tensor", lambda e, tc=tc: e.matmul(
            psD[:, 0:512], c["ones1"][0:1, 0:128],
            af(c, AF_LNR + 2048 + tc * 512, 512)[0:1, :], start=True, stop=True), sem="pe")
        for fc in range(FC):
            slot = fc % 2
            xs = af(c, AF_S1 + slot * 512, 512)
            tmp = af(c, AF_S2 + slot * 512, 512)
            p.wait("sync", "dve")
            p.dma("sync", xs, xsrc[fc * 128:(fc + 1) * 128, tc * 512:(tc + 1) * 512], "ld")
            p.wait("vector", "ld")
            p.wait("vector", "pe")
            p.op("vector", lambda e, xs=xs, tmp=tmp: e.tensor_mul(tmp, xs, psC[:, 0:512]), sem="dve")
            p.op("vector", lambda e, tmp=tmp, fc=fc, tc=tc: e.tensor_sub(
                ab(c, AB_XN + fc * T + tc * 512, 512), tmp, psD[:, 0:512]), sem="dve")


def emit_projections(c):
    p = c["p"]
    psA, psB = c["psA"], c["psB"]
    WQK = AB_YF
    for fc in range(FC):
        p.dma("gpsimd", ab(c, WQK + fc * 1024, 1024),
              c["wqk"][fc * 128:(fc + 1) * 128, :], "wld")
    p.wait("tensor", "wld")
    for mj in range(8):
        for tc in range(TC):
            ps = psA if (mj * TC + tc) % 2 == 0 else psB
            p.wait("tensor", "dve")
            for fc in range(FC):
                p.op("tensor", lambda e, ps=ps, mj=mj, tc=tc, fc=fc: e.matmul(
                    ps[:, 0:512],
                    ab(c, WQK + fc * 1024 + mj * 128, 128),
                    ab(c, AB_XN + fc * T + tc * 512, 512),
                    start=(fc == 0), stop=(fc == FC - 1)),
                    sem=("pe" if fc == FC - 1 else None))
            dst_off = (AB_QF + mj * T if mj < 4 else AB_KF + (mj - 4) * T) + tc * 512
            p.wait("vector", "pe")
            p.op("vector", lambda e, ps=ps, dst_off=dst_off: e.tensor_copy(
                ab(c, dst_off, 512), ps[:, 0:512]), sem="dve")
    p.wait("gpsimd", "dve")
    for base in (AB_QF, AB_KF):
        for g in range(4):
            row = 16 + 32 * g
            src = c["onesd"].rearrange("(o b a) -> o b a", o=1, b=1).to_broadcast([1, 4, 2048])
            p.dma("gpsimd",
                  c["AB"][row:row + 1, base:base + 8192].rearrange("p (a b) -> p a b", b=2048),
                  src, "wld")
    p.wait("gpsimd", "pe")
    for fc in range(FC):
        p.dma("gpsimd", ab(c, WQK + fc * 1024, 1024),
              c["wv"][fc * 128:(fc + 1) * 128, :], "wld")
    p.wait("tensor", "wld")
    for i in range(TT):
        for n in range(2):
            ps = psA if (i * 2 + n) % 2 == 0 else psB
            p.wait("tensor", "dve")
            for fc in range(FC):
                p.op("tensor", lambda e, ps=ps, i=i, n=n, fc=fc: e.matmul(
                    ps[:, 0:512],
                    ab(c, AB_XN + fc * T + i * 128, 128),
                    ab(c, WQK + fc * 1024 + n * 512, 512),
                    start=(fc == 0), stop=(fc == FC - 1)),
                    sem=("pe" if fc == FC - 1 else None))
            p.wait("vector", "pe")
            dst = c["AB"][:, AB_VT + i * 1040 + n * 8 * 65: AB_VT + i * 1040 + (n * 8 + 8) * 65]
            dstv = dst.rearrange("p (h d) -> p h d", d=65)[:, :, 0:64]
            srcv = ps[:, 0:512].rearrange("p (h d) -> p h d", d=64)
            p.op("vector", lambda e, dstv=dstv, srcv=srcv: e.tensor_copy(dstv, srcv), sem="dve")
        dstones = c["AB"][:, AB_VT + i * 1040: AB_VT + (i + 1) * 1040]
        dstones = dstones.rearrange("p (h d) -> p h d", d=65)[:, :, 64:65]
        p.op("vector", lambda e, dstones=dstones: e.memset(dstones, 1.0), sem="dve")


SCAN_G = 4
SCAN_CH = CH
SCAN_PARTS = 5
SCAN_MASK = True
SCAN_F = True


def emit_scan(c):
    p = c["p"]
    psA, psB, psC, psD = c["psA"], c["psB"], c["psC"], c["psD"]
    psY, psS, psSB = c["psY"], c["psS"], c["psSB"]
    p.wait("tensor", "wld")

    for g in range(SCAN_G):
        for ch in range(SCAN_CH):
            tq0 = ch * 256
            pss = (psA, psB, psC, psD)
            p.wait("tensor", "dve")
            for hp in range(4):
                p.op("tensor", lambda e, hp=hp, g=g, tq0=tq0: e.matmul(
                    pss[hp][:, 0:256],
                    c["lselt"][32 * hp:32 * hp + 32, 0:128],
                    ab(c, AB_QF + g * T + tq0, 256)[32 * hp:32 * hp + 32, :],
                    start=True, stop=True, tile_position=(32 * hp, 0)), sem="pe")
                p.op("tensor", lambda e, hp=hp, g=g, tq0=tq0: e.matmul(
                    pss[hp][0:25, 256:512],
                    c["lselt"][32 * hp:32 * hp + 32, 128:153],
                    ab(c, AB_QF + g * T + tq0, 256)[32 * hp:32 * hp + 32, :],
                    start=True, stop=True, tile_position=(32 * hp, 0)), sem="pe")
            p.wait("vector", "pe")
            for hp in range(4):
                p.op("vector", lambda e, hp=hp: e.tensor_copy(
                    ab(c, AB_STG + hp * 256, 256), pss[hp][:, 0:256]), sem="dve")
                p.op("vector", lambda e, hp=hp: e.tensor_copy(
                    abr(c, 0, 25, AB_STG + 1024 + hp * 256, 256),
                    pss[hp][0:25, 256:512]), sem="dve")
            p.wait("tensor", "dve")
            for hp in range(4):
                p.op("tensor", lambda e, hp=hp, g=g, tq0=tq0: e.matmul(
                    pss[hp][:, 0:256],
                    c["rselt"][32 * hp:32 * hp + 32, 0:128],
                    ab(c, AB_QF + g * T + tq0, 256)[32 * hp:32 * hp + 32, :],
                    start=True, stop=True, tile_position=(32 * hp, 0)), sem="pe")
                p.op("tensor", lambda e, hp=hp, g=g, tq0=tq0: e.matmul(
                    pss[hp][0:25, 256:512],
                    c["rselt"][32 * hp:32 * hp + 32, 128:153],
                    ab(c, AB_QF + g * T + tq0, 256)[32 * hp:32 * hp + 32, :],
                    start=True, stop=True, tile_position=(32 * hp, 0)), sem="pe")
            p.wait("vector", "pe")
            for hp in range(4):
                p.op("vector", lambda e, hp=hp: e.tensor_mul(
                    ab(c, AB_PHIQA + hp * 256, 256),
                    ab(c, AB_STG + hp * 256, 256), pss[hp][:, 0:256]), sem="dve")
                p.op("vector", lambda e, hp=hp: e.tensor_mul(
                    abr(c, 0, 25, AB_PHIQB + hp * 256, 256),
                    abr(c, 0, 25, AB_STG + 1024 + hp * 256, 256),
                    pss[hp][0:25, 256:512]), sem="dve")

            for tk in range(2 if SCAN_PARTS >= 2 else 0):
                p.wait("tensor", "dve")
                if tk == 1:
                    p.wait("tensor", "act")
                for hp in range(4):
                    p.op("tensor", lambda e, hp=hp, g=g, tq0=tq0, tk=tk: e.matmul(
                        pss[hp][:, 0:256],
                        ab(c, AB_KF + g * T + tq0 + tk * 128, 128)[32 * hp:32 * hp + 32, :],
                        ab(c, AB_QF + g * T + tq0, 256)[32 * hp:32 * hp + 32, :],
                        start=True, stop=True, tile_position=(32 * hp, 0)), sem="pe")
                a4o = AB_A4 + tk * 1024
                p.wait("scalar", "pe")
                for hp in range(4):
                    p.op("scalar", lambda e, hp=hp, a4o=a4o: e.activation(
                        out=ab(c, a4o + hp * 256, 256), in_=pss[hp][:, 0:256],
                        func=AF_.Square, scale=FSC, bias=FB), sem="act")
                p.wait("vector", "act")
                a4v = ab(c, a4o, 1024).rearrange("p (h n) -> p h n", n=256)
                mskv = c["mskt"][:, tk * 256:(tk + 1) * 256].rearrange(
                    "p (o n) -> p o n", o=1).to_broadcast([128, 4, 256])
                if SCAN_MASK:
                    p.op("vector", lambda e, a4v=a4v, mskv=mskv: e.scalar_tensor_tensor(
                        out=a4v, in0=a4v, scalar=0.5, in1=mskv,
                        op0=ALU.add, op1=ALU.mult), sem="dve")

            p.wait("tensor", "dve")
            for hp in range(4 if SCAN_PARTS >= 3 else 0):
                h = 4 * g + hp
                col = hp * 256
                for tk in range(2):
                    p.op("tensor", lambda e, h=h, hp=hp, col=col, tk=tk, ch=ch: e.matmul(
                        psY[0:65, col:col + 256],
                        ab(c, AB_VT + (2 * ch + tk) * 1040 + h * 65, 65),
                        ab(c, AB_A4 + tk * 1024 + hp * 256, 256),
                        start=(tk == 0), stop=False), sem="pe")
                p.op("tensor", lambda e, h=h, hp=hp, col=col: e.matmul(
                    psY[0:65, col:col + 256],
                    c["sbf"][:, h * 65:h * 65 + 65],
                    ab(c, AB_PHIQA + hp * 256, 256),
                    start=False, stop=False), sem="pe")
                p.op("tensor", lambda e, h=h, hp=hp, col=col: e.matmul(
                    psY[0:65, col:col + 256],
                    c["sbfB"][0:25, h * 65:h * 65 + 65],
                    abr(c, 0, 25, AB_PHIQB + hp * 256, 256),
                    start=False, stop=True), sem="pe")
            p.wait("vector", "pe")
            p.wait("vector", "gd")
            for hp in range(4 if SCAN_PARTS >= 3 else 0):
                h = 4 * g + hp
                p.op("vector", lambda e, h=h, hp=hp, tq0=tq0: e.tensor_copy(
                    abr(c, (h % 2) * 64, (h % 2) * 64 + 64, AB_YF + (h // 2) * T + tq0, 256),
                    psY[0:64, hp * 256:hp * 256 + 256]), sem="dve")
            if SCAN_PARTS >= 3:
                p.op("vector", lambda e: e.tensor_copy(
                    af(c, AF_ROWS, 1024)[0:1, :], psY[64:65, 0:1024]), sem="dve")
                p.wait("sync", "dve")
                dstden = afr(c, ch * 16 + g * 4, ch * 16 + g * 4 + 4, AF_DEN, 256)
                srcden = af(c, AF_ROWS, 1024)[0:1, :].rearrange("p (a b) -> p a b", b=256)
                p.op("sync", lambda e, dstden=dstden, srcden=srcden: e.dma_start(
                    out=dstden, in_=srcden), sem="gd", n=16)

            first_state = True
            for tk in range(2 if SCAN_PARTS >= 4 else 0):
                p.wait("tensor", "dve")
                for hp in range(4):
                    p.op("tensor", lambda e, hp=hp, g=g, tq0=tq0, tk=tk: e.matmul(
                        pss[hp][:, 0:153],
                        ab(c, AB_KF + g * T + tq0 + tk * 128, 128)[32 * hp:32 * hp + 32, :],
                        c["lselt"][32 * hp:32 * hp + 32, 0:153],
                        start=True, stop=True, tile_position=(32 * hp, 0)), sem="pe")
                p.wait("vector", "pe")
                for hp in range(4):
                    p.op("vector", lambda e, hp=hp: e.tensor_copy(
                        ab(c, AB_STG + hp * 153, 153), pss[hp][:, 0:153]), sem="dve")
                p.wait("tensor", "dve")
                for hp in range(4):
                    p.op("tensor", lambda e, hp=hp, g=g, tq0=tq0, tk=tk: e.matmul(
                        pss[hp][:, 0:153],
                        ab(c, AB_KF + g * T + tq0 + tk * 128, 128)[32 * hp:32 * hp + 32, :],
                        c["rselt"][32 * hp:32 * hp + 32, 0:153],
                        start=True, stop=True, tile_position=(32 * hp, 0)), sem="pe")
                p.wait("vector", "pe")
                for hp in range(4):
                    p.op("vector", lambda e, hp=hp: e.tensor_mul(
                        ab(c, AB_PHIK + hp * 153, 153),
                        ab(c, AB_STG + hp * 153, 153), pss[hp][:, 0:153]), sem="dve")
                p.wait("tensor", "dve")
                for hp in range(4):
                    h = 4 * g + hp
                    p.op("tensor", lambda e, hp=hp, h=h, ch=ch, tk=tk, fs=first_state: e.matmul(
                        psS[:, hp * 65:hp * 65 + 65],
                        ab(c, AB_PHIK + hp * 153, 128),
                        ab(c, AB_VT + (2 * ch + tk) * 1040 + h * 65, 65),
                        start=(fs and hp == 0), stop=(tk == 1 and hp == 3)), sem="pe")
                    p.op("tensor", lambda e, hp=hp, h=h, ch=ch, tk=tk, fs=first_state: e.matmul(
                        psSB[0:25, hp * 65:hp * 65 + 65],
                        ab(c, AB_PHIK + hp * 153 + 128, 25),
                        ab(c, AB_VT + (2 * ch + tk) * 1040 + h * 65, 65),
                        start=(fs and hp == 0), stop=(tk == 1 and hp == 3)), sem="pe")
                first_state = False
            if SCAN_PARTS < 5:
                continue
            p.wait("vector", "pe")
            p.op("vector", lambda e, g=g: e.tensor_add(
                af(c, AF_SSB + g * 260, 260), af(c, AF_SSB + g * 260, 260),
                psS[:, 0:260]), sem="dve")
            p.op("vector", lambda e, g=g: e.tensor_add(
                c["ssbB"][0:25, g * 260:g * 260 + 260],
                c["ssbB"][0:25, g * 260:g * 260 + 260],
                psSB[0:25, 0:260]), sem="dve")
            p.op("vector", lambda e, g=g: e.tensor_copy(
                c["sbf"][:, g * 260:g * 260 + 260], af(c, AF_SSB + g * 260, 260)), sem="dve")
            p.op("vector", lambda e, g=g: e.tensor_copy(
                c["sbfB"][0:25, g * 260:g * 260 + 260],
                c["ssbB"][0:25, g * 260:g * 260 + 260]), sem="dve")


def emit_handoff(c):
    p = c["p"]
    p.wait("sync", "dve")
    p.dma("sync", c["sagi"][0:128, :], af(c, AF_SSB, 1040), "gd")
    p.dma("sync", c["sagi"][128:160, :], c["ssbB"][:, :], "gd")
    p.wait("gpsimd", "gd")
    p.op("gpsimd", lambda e: e.collective_compute(
        "AllGather", ALU.bypass, replica_groups=[list(range(NCORES))],
        ins=[c["sagi"].opt()], outs=[c["sago"].opt()]), sem="cc")
    p.wait("sync", "cc")
    p.op("vector", lambda e: e.memset(af(c, AF_S1, 1040), 0.0), sem="dve")
    p.op("vector", lambda e: e.memset(afr(c, 32, 64, AF_S2, 1040), 0.0), sem="dve")
    for r in range(NCORES):
        p.wait("sync", "dve")
        p.dma("sync", af(c, AF_RS, 1040), c["sago"][r * 160:r * 160 + 128, :], "ld")
        p.dma("sync", afr(c, 32, 64, AF_LNR, 1040), c["sago"][r * 160 + 128:(r + 1) * 160, :], "ld")
        p.wait("vector", "ld")
        p.op("vector", lambda e, r=r: e.scalar_tensor_tensor(
            out=af(c, AF_S1, 1040), in0=af(c, AF_RS, 1040),
            scalar=c["selmt"][:, r:r + 1], in1=af(c, AF_S1, 1040),
            op0=ALU.mult, op1=ALU.add), sem="dve")
        p.op("vector", lambda e, r=r: e.scalar_tensor_tensor(
            out=afr(c, 32, 64, AF_S2, 1040), in0=afr(c, 32, 64, AF_LNR, 1040),
            scalar=c["selmt"][32:64, r:r + 1], in1=afr(c, 32, 64, AF_S2, 1040),
            op0=ALU.mult, op1=ALU.add), sem="dve")
    p.op("vector", lambda e: e.tensor_copy(c["spv"][:, :], af(c, AF_S1, 1040)), sem="dve")
    p.op("vector", lambda e: e.tensor_copy(c["spvB"][:, :], afr(c, 32, 64, AF_S2, 1040)), sem="dve")


def emit_correction(c):
    p = c["p"]
    psA, psB, psC, psD, psY = c["psA"], c["psB"], c["psC"], c["psD"], c["psY"]
    pss = (psA, psB, psC, psD)
    for g in range(4):
        for c2 in range(TC):
            tq0 = c2 * 512
            p.wait("tensor", "dve")
            for hp in range(4):
                p.op("tensor", lambda e, hp=hp, g=g, tq0=tq0: e.matmul(
                    pss[hp][:, 0:512],
                    c["lselt"][32 * hp:32 * hp + 32, 0:128],
                    ab(c, AB_QF + g * T + tq0, 512)[32 * hp:32 * hp + 32, :],
                    start=True, stop=True, tile_position=(32 * hp, 0)), sem="pe")
            p.wait("vector", "pe")
            for hp in range(4):
                p.op("vector", lambda e, hp=hp: e.tensor_copy(
                    ab(c, AB_STG + hp * 512, 512), pss[hp][:, 0:512]), sem="dve")
            p.wait("tensor", "dve")
            for hp in range(4):
                p.op("tensor", lambda e, hp=hp, g=g, tq0=tq0: e.matmul(
                    pss[hp][:, 0:512],
                    c["rselt"][32 * hp:32 * hp + 32, 0:128],
                    ab(c, AB_QF + g * T + tq0, 512)[32 * hp:32 * hp + 32, :],
                    start=True, stop=True, tile_position=(32 * hp, 0)), sem="pe")
            p.wait("vector", "pe")
            for hp in range(4):
                p.op("vector", lambda e, hp=hp: e.tensor_mul(
                    ab(c, AB_PHIQA + hp * 512, 512),
                    ab(c, AB_STG + hp * 512, 512), pss[hp][:, 0:512]), sem="dve")
            p.wait("tensor", "dve")
            for hp in range(4):
                p.op("tensor", lambda e, hp=hp, g=g, tq0=tq0: e.matmul(
                    pss[hp][0:25, 0:512],
                    c["lselt"][32 * hp:32 * hp + 32, 128:153],
                    ab(c, AB_QF + g * T + tq0, 512)[32 * hp:32 * hp + 32, :],
                    start=True, stop=True, tile_position=(32 * hp, 0)), sem="pe")
            p.wait("vector", "pe")
            for hp in range(4):
                p.op("vector", lambda e, hp=hp: e.tensor_copy(
                    abr(c, 0, 25, AB_STG + hp * 512, 512), pss[hp][0:25, 0:512]), sem="dve")
            p.wait("tensor", "dve")
            for hp in range(4):
                p.op("tensor", lambda e, hp=hp, g=g, tq0=tq0: e.matmul(
                    pss[hp][0:25, 0:512],
                    c["rselt"][32 * hp:32 * hp + 32, 128:153],
                    ab(c, AB_QF + g * T + tq0, 512)[32 * hp:32 * hp + 32, :],
                    start=True, stop=True, tile_position=(32 * hp, 0)), sem="pe")
            p.wait("vector", "pe")
            for hp in range(4):
                p.op("vector", lambda e, hp=hp: e.tensor_mul(
                    abr(c, 0, 25, AB_PHIQB + hp * 512, 512),
                    abr(c, 0, 25, AB_STG + hp * 512, 512), pss[hp][0:25, 0:512]), sem="dve")
            for pr in range(2):
                p.wait("tensor", "dve")
                if pr == 1:
                    p.wait("tensor", "gd")
                for hh in range(2):
                    hp = pr * 2 + hh
                    h = 4 * g + hp
                    p.op("tensor", lambda e, h=h, hp=hp, hh=hh: e.matmul(
                        psY[0:65, hh * 512:hh * 512 + 512],
                        c["spv"][:, h * 65:h * 65 + 65],
                        ab(c, AB_PHIQA + hp * 512, 512),
                        start=True, stop=False), sem="pe")
                    p.op("tensor", lambda e, h=h, hp=hp, hh=hh: e.matmul(
                        psY[0:65, hh * 512:hh * 512 + 512],
                        c["spvB"][0:25, h * 65:h * 65 + 65],
                        abr(c, 0, 25, AB_PHIQB + hp * 512, 512),
                        start=False, stop=True), sem="pe")
                p.wait("vector", "pe")
                p.wait("vector", "gd")
                for hh in range(2):
                    hp = pr * 2 + hh
                    h = 4 * g + hp
                    ysl = abr(c, (h % 2) * 64, (h % 2) * 64 + 64, AB_YF + (h // 2) * T + tq0, 512)
                    p.op("vector", lambda e, ysl=ysl, hh=hh: e.tensor_add(
                        ysl, ysl, psY[0:64, hh * 512:hh * 512 + 512]), sem="dve")
                p.op("vector", lambda e: e.tensor_copy(
                    af(c, AF_ROWS, 1024)[0:1, :], psY[64:65, 0:1024]), sem="dve")
                p.wait("gpsimd", "dve")
                for hh in range(2):
                    hp = pr * 2 + hh
                    for cc2 in range(2):
                        row = (2 * c2 + cc2) * 16 + g * 4 + hp
                        p.op("gpsimd", lambda e, row=row, hh=hh, cc2=cc2: e.dma_start(
                            out=afr(c, row, row + 1, AF_DEN, 256),
                            in_=af(c, AF_ROWS + hh * 512 + cc2 * 256, 256)[0:1, :],
                            accum_op=ALU.add), sem="gd", n=16)


def emit_norm_wo_resid(c):
    p = c["p"]
    psA, psB, psC = c["psA"], c["psB"], c["psC"]
    p.wait("vector", "gd")
    p.op("vector", lambda e: e.tensor_scalar(af(c, AF_DEN, 256), af(c, AF_DEN, 256),
                                             1e-12, None, op0=ALU.add), sem="dve")
    p.op("vector", lambda e: e.reciprocal(af(c, AF_DEN, 256), af(c, AF_DEN, 256)), sem="dve")
    p.wait("sync", "gd")
    for fc in range(FC):
        for tc in range(TC):
            p.wait("sync", "dve")
            p.wait("sync", "pe")
            for hh in range(2):
                h = 2 * fc + hh
                for cc2 in range(2):
                    chv = 2 * tc + cc2
                    row = chv * 16 + (h // 4) * 4 + (h % 4)
                    p.dma("sync", af(c, AF_ROWS + hh * 512 + cc2 * 256, 256)[0:1, :],
                          afr(c, row, row + 1, AF_DEN, 256), "ld")
            p.wait("tensor", "ld")
            p.wait("tensor", "dve")
            p.op("tensor", lambda e: e.matmul(
                psC[0:64, 0:512], c["ones1"][0:1, 0:64],
                af(c, AF_ROWS, 512)[0:1, :], start=True, stop=True), sem="pe")
            p.op("tensor", lambda e: e.matmul(
                psC[64:128, 0:512], c["ones1"][0:1, 0:64],
                af(c, AF_ROWS + 512, 512)[0:1, :], start=True, stop=True), sem="pe")
            p.wait("vector", "pe")
            ysl = ab(c, AB_YF + fc * T + tc * 512, 512)
            p.op("vector", lambda e, ysl=ysl: e.tensor_mul(ysl, ysl, psC[:, 0:512]), sem="dve")
    WO = AB_QF
    p.wait("gpsimd", "dve")
    for fcw in range(FC):
        p.dma("gpsimd", ab(c, WO + fcw * 1024, 1024),
              c["wo"][fcw * 128:(fcw + 1) * 128, :], "wld")
    p.wait("tensor", "wld")
    for oc in range(FC):
        for tc in range(TC):
            ps = psA if (oc * TC + tc) % 2 == 0 else psB
            p.wait("tensor", "dve")
            for hd in range(FC):
                p.op("tensor", lambda e, ps=ps, oc=oc, tc=tc, hd=hd: e.matmul(
                    ps[:, 0:512],
                    ab(c, WO + hd * 1024 + oc * 128, 128),
                    ab(c, AB_YF + hd * T + tc * 512, 512),
                    start=(hd == 0), stop=(hd == FC - 1)),
                    sem=("pe" if hd == FC - 1 else None))
            slot = (oc * TC + tc) % 2
            xs = af(c, AF_S1 + slot * 512, 512)
            res = af(c, AF_S2 + slot * 512, 512)
            p.wait("sync", "dve")
            p.dma("sync", xs, c["x_fm"][oc * 128:(oc + 1) * 128, tc * 512:(tc + 1) * 512], "ld")
            p.wait("vector", "ld")
            p.wait("vector", "pe")
            p.wait("vector", "gd")
            p.op("vector", lambda e, ps=ps, xs=xs, res=res: e.tensor_add(res, xs, ps[:, 0:512]), sem="dve")
            p.wait("sync", "dve")
            p.dma("sync", c["x1d"][oc * 128:(oc + 1) * 128, tc * 512:(tc + 1) * 512], res, "gd")


def emit_xg(c):
    p = c["p"]
    pst = c["psS"][:, 0:64].bitcast(BF16)
    for i in range(TT):
        for fc in range(FC):
            p.wait("tensor", "dve")
            if fc == 0:
                p.wait("tensor", "gd")
            p.op("tensor", lambda e, i=i, fc=fc, pst=pst: e.transpose(
                pst, ab(c, AB_XN + fc * T + i * 128, 128), c["idbt"][:, :]), sem="pe")
            p.wait("vector", "pe")
            p.op("vector", lambda e, fc=fc, pst=pst: e.tensor_copy(
                ab(c, AB_BUF_T + fc * 128, 128), pst), sem="dve")
        p.wait("sync", "dve")
        p.dma("sync", c["xgi"][i * 128:(i + 1) * 128, :], ab(c, AB_BUF_T, 1024), "gd")
    p.wait("gpsimd", "gd")
    p.op("gpsimd", lambda e: e.collective_compute(
        "AllGather", ALU.bypass, replica_groups=[list(range(NCORES))],
        ins=[c["xgi"].opt()], outs=[c["xgo"].opt()]), sem="cc")


def emit_routing(c):
    p = c["p"]
    psA, psS = c["psA"], c["psS"]
    GW = AB_EW
    p.wait("gpsimd", "pe")
    for fc in range(FC):
        p.dma("gpsimd", ab(c, GW + fc * 64, 64),
              c["gate"][fc * 128:(fc + 1) * 128, :], "wld")
    p.op("vector", lambda e: e.memset(afr(c, 0, 64, AF_CAR, 1), 0.0), sem="dve")
    p.op("vector", lambda e: e.memset(c["onesf"][:, :], 1.0), sem="dve")
    p.wait("tensor", "wld")
    for i in range(TT):
        p.wait("tensor", "dve")
        for fc in range(FC):
            p.op("tensor", lambda e, i=i, fc=fc: e.matmul(
                psA[:, 0:64],
                ab(c, AB_XN + fc * T + i * 128, 128),
                ab(c, GW + fc * 64, 64),
                start=(fc == 0), stop=(fc == FC - 1)),
                sem=("pe" if fc == FC - 1 else None))
        p.wait("vector", "pe")
        p.op("vector", lambda e: e.tensor_copy(af(c, AF_RS, 64), psA[:, 0:64]), sem="dve")
        p.op("vector", lambda e: e.max(out=af(c, AF_RS + 64, 8), in_=af(c, AF_RS, 64)), sem="dve")
        p.op("vector", lambda e: e.max_index(out=c["idx8"][:, :], in_max=af(c, AF_RS + 64, 8),
                                             in_values=af(c, AF_RS, 64)), sem="dve")
        p.op("vector", lambda e: e.tensor_copy(af(c, AF_RS + 96, 8), c["idx8"][:, :]), sem="dve")
        p.op("vector", lambda e: e.tensor_scalar(af(c, AF_RS + 72, 1), af(c, AF_RS + 64, 1),
                                                 -1.0, None, op0=ALU.mult), sem="dve")
        p.wait("scalar", "dve")
        p.op("scalar", lambda e: e.activation(out=af(c, AF_RS + 80, 8), in_=af(c, AF_RS + 64, 8),
                                              func=AF_.Exp, bias=af(c, AF_RS + 72, 1)), sem="act")
        p.wait("vector", "act")
        p.op("vector", lambda e: e.reduce_sum(af(c, AF_RS + 73, 1), af(c, AF_RS + 80, 8), axis=AX.X), sem="dve")
        p.op("vector", lambda e: e.reciprocal(af(c, AF_RS + 74, 1), af(c, AF_RS + 73, 1)), sem="dve")
        p.op("vector", lambda e: e.tensor_scalar(af(c, AF_RS + 88, 8), af(c, AF_RS + 80, 8),
                                                 af(c, AF_RS + 74, 1), None, op0=ALU.mult), sem="dve")
        p.op("vector", lambda e, i=i: e.tensor_copy(af(c, AF_RK + i * 24, 8), af(c, AF_RS + 96, 8)), sem="dve")
        p.op("vector", lambda e, i=i: e.tensor_copy(af(c, AF_RK + i * 24 + 8, 8), af(c, AF_RS + 88, 8)), sem="dve")
        io64 = c["iota64t"][:, :].rearrange("p (e z) -> p e z", z=1).to_broadcast([128, 64, 8])
        idv = af(c, AF_RS + 96, 8).rearrange("p (z j) -> p z j", z=1).to_broadcast([128, 64, 8])
        p.op("vector", lambda e, io64=io64, idv=idv: e.tensor_tensor(
            af(c, AF_RS + 512, 512).rearrange("p (ee j) -> p ee j", j=8), io64, idv,
            op=ALU.is_equal), sem="dve")
        wvv = af(c, AF_RS + 88, 8).rearrange("p (z j) -> p z j", z=1).to_broadcast([128, 64, 8])
        p.op("vector", lambda e, wvv=wvv: e.tensor_tensor(
            af(c, AF_RS + 1024, 512).rearrange("p (ee j) -> p ee j", j=8),
            af(c, AF_RS + 512, 512).rearrange("p (ee j) -> p ee j", j=8), wvv,
            op=ALU.mult), sem="dve")
        p.op("vector", lambda e: e.reduce_sum(
            af(c, AF_RS + 128, 64), af(c, AF_RS + 1024, 512).rearrange("p (ee j) -> p ee j", j=8),
            axis=AX.X), sem="dve")
        p.op("vector", lambda e: e.tensor_scalar(af(c, AF_RS + 192, 64), af(c, AF_RS + 128, 64),
                                                 0.0, None, op0=ALU.is_gt), sem="dve")
        p.wait("tensor", "dve")
        p.op("tensor", lambda e: e.transpose(psS[0:64, 0:128], af(c, AF_RS + 192, 64),
                                             c["idft"][:, :]), sem="pe")
        p.wait("vector", "pe")
        p.op("vector", lambda e: e.tensor_copy(afr(c, 0, 64, AF_RS + 256, 128), psS[0:64, 0:128]), sem="dve")
        p.op("vector", lambda e, i=i: e.tensor_tensor_scan(
            out=c["cum"][:, i * 128:(i + 1) * 128],
            data0=c["onesf"][:, :], data1=afr(c, 0, 64, AF_RS + 256, 128),
            initial=afr(c, 0, 64, AF_CAR, 1), op0=ALU.mult, op1=ALU.add), sem="dve")
        p.op("vector", lambda e, i=i: e.tensor_copy(
            afr(c, 0, 64, AF_CAR, 1), c["cum"][:, i * 128 + 127:(i + 1) * 128]), sem="dve")
    p.wait("sync", "dve")
    p.dma("sync", c["cnti"][:, :], afr(c, 0, 64, AF_CAR, 1), "gd")
    p.wait("gpsimd", "gd")
    p.op("gpsimd", lambda e: e.collective_compute(
        "AllGather", ALU.bypass, replica_groups=[list(range(NCORES))],
        ins=[c["cnti"].opt()], outs=[c["cnto"].opt()]), sem="cc")
    p.wait("sync", "cc")
    p.dma("sync", afr(c, 0, 8, AF_RS, 64),
          c["cnto"].rearrange("(r e) o -> r (e o)", e=NE), "ld")
    p.wait("tensor", "ld")
    p.wait("tensor", "dve")
    p.op("tensor", lambda e: e.matmul(psS[0:64, 128:129], afr(c, 0, 8, AF_RS, 64),
                                      c["rankmt"][:, :], start=True, stop=True), sem="pe")
    p.wait("vector", "pe")
    p.op("vector", lambda e: e.tensor_copy(afr(c, 0, 64, AF_CAR + 1, 1), psS[0:64, 128:129]), sem="dve")
    p.op("vector", lambda e: e.tensor_scalar(afr(c, 0, 64, AF_CAR + 2, 1),
                                             afr(c, 0, 64, AF_CAR + 1, 1),
                                             -1.0, None, op0=ALU.add), sem="dve")
    p.op("vector", lambda e: e.memset(af(c, AF_S1, 513), -1.0), sem="dve")
    p.wait("gpsimd", "dve")
    p.dma("gpsimd", c["recm"].flatten().rearrange("(p a) -> p a", p=128),
          af(c, AF_S1, 513), "gd")
    p.wait("gpsimd", "gd")
    for i in range(TT):
        p.wait("vector", "dve")
        p.wait("vector", "gd")
        p.op("vector", lambda e, i=i: e.tensor_scalar(
            afr(c, 0, 64, AF_S2, 128), c["cum"][:, i * 128:(i + 1) * 128],
            afr(c, 0, 64, AF_CAR + 2, 1), None, op0=ALU.add), sem="dve")
        p.wait("tensor", "dve")
        p.op("tensor", lambda e: e.transpose(psS[:, 0:64], afr(c, 0, 64, AF_S2, 128),
                                             c["idft"][0:64, 0:64]), sem="pe")
        p.wait("vector", "pe")
        p.op("vector", lambda e: e.tensor_copy(af(c, AF_S2 + 128, 64), psS[:, 0:64]), sem="dve")
        io64 = c["iota64t"][:, :].rearrange("p (z e) -> p z e", z=1).to_broadcast([128, 8, 64])
        idv = af(c, AF_RK + i * 24, 8).rearrange("p (j z) -> p j z", z=1).to_broadcast([128, 8, 64])
        p.op("vector", lambda e, io64=io64, idv=idv: e.tensor_tensor(
            af(c, AF_RS + 512, 512).rearrange("p (j ee) -> p j ee", ee=64), io64, idv,
            op=ALU.is_equal), sem="dve")
        pos = af(c, AF_S2 + 128, 64).rearrange("p (z e) -> p z e", z=1).to_broadcast([128, 8, 64])
        p.op("vector", lambda e, pos=pos: e.tensor_tensor(
            af(c, AF_RS + 1024, 512).rearrange("p (j ee) -> p j ee", ee=64),
            af(c, AF_RS + 512, 512).rearrange("p (j ee) -> p j ee", ee=64), pos,
            op=ALU.mult), sem="dve")
        p.op("vector", lambda e, i=i: e.reduce_sum(
            af(c, AF_RK + i * 24 + 16, 8),
            af(c, AF_RS + 1024, 512).rearrange("p (j ee) -> p j ee", ee=64), axis=AX.X), sem="dve")
        sl = af(c, AF_RK + i * 24 + 16, 8)
        p.op("vector", lambda e, sl=sl: e.tensor_scalar(af(c, AF_RS, 8), sl, 512.0, None,
                                                        op0=ALU.is_ge), sem="dve")
        p.op("vector", lambda e, i=i: e.tensor_scalar(
            af(c, AF_RS + 8, 8), af(c, AF_RK + i * 24, 8), 513.0, None, op0=ALU.mult), sem="dve")
        p.op("vector", lambda e, sl=sl: e.tensor_add(af(c, AF_RS + 8, 8), af(c, AF_RS + 8, 8), sl), sem="dve")
        p.op("vector", lambda e: e.scalar_tensor_tensor(
            out=af(c, AF_RS + 16, 8), in0=af(c, AF_RS, 8), scalar=BIG,
            in1=af(c, AF_RS + 8, 8), op0=ALU.mult, op1=ALU.add), sem="dve")
        p.op("vector", lambda e: e.tensor_copy(c["sidx"][:, :], af(c, AF_RS + 16, 8)), sem="dve")
        p.op("vector", lambda e, i=i: e.tensor_scalar(
            af(c, AF_RS + 24, 8), af(c, AF_RK + i * 24, 8), 512.0, None, op0=ALU.mult), sem="dve")
        p.op("vector", lambda e, sl=sl: e.tensor_add(af(c, AF_RS + 24, 8), af(c, AF_RS + 24, 8), sl), sem="dve")
        p.op("vector", lambda e: e.scalar_tensor_tensor(
            out=af(c, AF_RS + 32, 8), in0=af(c, AF_RS, 8), scalar=BIG,
            in1=af(c, AF_RS + 24, 8), op0=ALU.mult, op1=ALU.add), sem="dve")
        p.op("vector", lambda e, i=i: e.tensor_copy(c["widx"][:, i * 8:(i + 1) * 8],
                                                    af(c, AF_RS + 32, 8)), sem="dve")
        tok = c["tokidt"][:, i:i + 1].rearrange("p (o z) -> p o z", z=1).to_broadcast([128, 8, 1])
        rec = af(c, AF_RS + 40, 16).rearrange("p (j two) -> p j two", two=2)
        p.op("vector", lambda e, tok=tok, rec=rec: e.tensor_copy(rec[:, :, 0:1], tok), sem="dve")
        wk = af(c, AF_RK + i * 24 + 8, 8).rearrange("p (j z) -> p j z", z=1)
        p.op("vector", lambda e, wk=wk, rec=rec: e.tensor_copy(rec[:, :, 1:2], wk), sem="dve")
        p.wait("gpsimd", "dve")
        for j in range(TOPK):
            p.op("gpsimd", lambda e, j=j: e.indirect_dma_start(
                out=c["recm"][:, :],
                out_offset=bass.IndirectOffsetOnAxis(ap=c["sidx"][:, j:j + 1], axis=0),
                in_=af(c, AF_RS + 40 + 2 * j, 2),
                in_offset=None, bounds_check=c["bregs"][NE * 513 - 1], oob_is_err=False), sem="gd", n=16)
    p.wait("gpsimd", "gd")
    p.op("gpsimd", lambda e: e.collective_compute(
        "AllReduce", ALU.max, replica_groups=[list(range(NCORES))],
        ins=[c["recm"].opt()], outs=[c["recmr"].opt()]), sem="cc")


def emit_experts(c):
    p = c["p"]
    psA, psB, psC, psD = c["psA"], c["psB"], c["psC"], c["psD"]
    p.wait("gpsimd", "cc")
    pst = c["psS"][:, 0:64].bitcast(BF16)
    for e_i in range(EPC):
        p.wait("gpsimd", "dve")
        for gg in range(4):
            p.op("gpsimd", lambda e, e_i=e_i, gg=gg: e.indirect_dma_start(
                out=af(c, AF_S1 + gg * 2, 2), out_offset=None,
                in_=c["recmr"][:, :],
                in_offset=bass.IndirectOffsetOnAxis(
                    ap=c["recidxt"][:, e_i * 4 + gg:e_i * 4 + gg + 1], axis=0),
                bounds_check=c["bregs"][NE * 513 - 1], oob_is_err=False), sem="gd", n=16)
        p.wait("vector", "gd")
        tv = af(c, AF_S1, 8).rearrange("p (g two) -> p g two", two=2)[:, :, 0:1]
        wv = af(c, AF_S1, 8).rearrange("p (g two) -> p g two", two=2)[:, :, 1:2]
        p.op("vector", lambda e, tv=tv: e.tensor_copy(
            af(c, AF_S2, 4).rearrange("p (g z) -> p g z", z=1), tv), sem="dve")
        p.op("vector", lambda e, wv=wv: e.tensor_copy(
            af(c, AF_S2 + 8, 4).rearrange("p (g z) -> p g z", z=1), wv), sem="dve")
        p.op("vector", lambda e: e.tensor_scalar(af(c, AF_S2 + 4, 4), af(c, AF_S2, 4),
                                                 0.0, None, op0=ALU.is_lt), sem="dve")
        p.op("vector", lambda e: e.scalar_tensor_tensor(
            out=af(c, AF_S2 + 16, 4), in0=af(c, AF_S2 + 4, 4), scalar=BIG,
            in1=af(c, AF_S2, 4), op0=ALU.mult, op1=ALU.add), sem="dve")
        p.op("vector", lambda e: e.tensor_copy(c["sidx"][:, 0:4], af(c, AF_S2 + 16, 4)), sem="dve")
        p.wait("gpsimd", "dve")
        p.wait("gpsimd", "pe")
        for gg in range(4):
            p.op("gpsimd", lambda e, gg=gg: e.memset(ab(c, AB_BUF_T + gg * 1024, 1024), 0.0), sem="gd", n=1)
        for gg in range(4):
            p.op("gpsimd", lambda e, gg=gg: e.indirect_dma_start(
                out=ab(c, AB_BUF_T + gg * 1024, 1024), out_offset=None,
                in_=c["xgo"][:, :],
                in_offset=bass.IndirectOffsetOnAxis(ap=c["sidx"][:, gg:gg + 1], axis=0),
                bounds_check=c["bregs"][NCORES * T - 1], oob_is_err=False), sem="gd", n=16)
        for hc in range(FC):
            p.dma("gpsimd", ab(c, AB_EW + hc * 1024, 1024),
                  c["egu"][e_i, hc * 128:(hc + 1) * 128, :], "wld")
        p.wait("tensor", "gd")
        for gg in range(4):
            for hc in range(FC):
                p.wait("tensor", "dve")
                p.op("tensor", lambda e, gg=gg, hc=hc, pst=pst: e.transpose(
                    pst, ab(c, AB_BUF_T + gg * 1024 + hc * 128, 128), c["idbt"][:, :]), sem="pe")
                p.wait("vector", "pe")
                p.op("vector", lambda e, gg=gg, hc=hc, pst=pst: e.tensor_copy(
                    ab(c, AB_BUF_FM + hc * 512 + gg * 128, 128), pst), sem="dve")
        p.wait("tensor", "wld")
        for guc in range(8):
            ps = psA if guc % 2 == 0 else psB
            p.wait("tensor", "act")
            p.wait("tensor", "dve")
            for hc in range(FC):
                p.op("tensor", lambda e, ps=ps, guc=guc, hc=hc: e.matmul(
                    ps[:, 0:512],
                    ab(c, AB_EW + hc * 1024 + guc * 128, 128),
                    ab(c, AB_BUF_FM + hc * 512, 512),
                    start=(hc == 0), stop=(hc == FC - 1)),
                    sem=("pe" if hc == FC - 1 else None))
            if guc < 4:
                p.wait("scalar", "pe")
                p.op("scalar", lambda e, ps=ps, guc=guc: e.activation(
                    out=ab(c, AB_ACT + guc * 512, 512), in_=ps[:, 0:512],
                    func=AF_.Silu), sem="act")
            else:
                p.wait("vector", "pe")
                p.op("vector", lambda e, ps=ps, guc=guc: e.tensor_copy(
                    ab(c, AB_ACT4 + (guc - 4) * 512, 512), ps[:, 0:512]), sem="dve")
        p.wait("vector", "act")
        for ic in range(4):
            p.op("vector", lambda e, ic=ic: e.tensor_mul(
                ab(c, AB_ACT + ic * 512, 512), ab(c, AB_ACT + ic * 512, 512),
                ab(c, AB_ACT4 + ic * 512, 512)), sem="dve")
        p.wait("gpsimd", "pe")
        for icw in range(4):
            p.dma("gpsimd", ab(c, AB_EW + icw * 1024, 1024),
                  c["edw"][e_i, icw * 128:(icw + 1) * 128, :], "wld")
        p.wait("tensor", "wld")
        for sg in range(4):
            for n in range(2):
                ps = psC if (sg * 2 + n) % 2 == 0 else psD
                p.wait("tensor", "dve")
                p.wait("tensor", "gd")
                for icm in range(4):
                    p.op("tensor", lambda e, ps=ps, sg=sg, n=n, icm=icm: e.matmul(
                        ps[:, 0:512],
                        ab(c, AB_ACT + icm * 512 + sg * 128, 128),
                        ab(c, AB_EW + icm * 1024 + n * 512, 512),
                        start=(icm == 0), stop=(icm == 3)),
                        sem=("pe" if icm == 3 else None))
                p.wait("vector", "pe")
                p.wait("vector", "gd")
                p.op("vector", lambda e, ps=ps, sg=sg: e.tensor_scalar(
                    ab(c, AB_WOUT, 512), ps[:, 0:512],
                    af(c, AF_S2 + 8 + sg, 1), None, op0=ALU.mult), sem="dve")
                p.wait("sync", "dve")
                p.dma("sync", c["woutl"][e_i * 512 + sg * 128: e_i * 512 + (sg + 1) * 128,
                                         n * 512:(n + 1) * 512], ab(c, AB_WOUT, 512), "gd")
    p.wait("gpsimd", "gd")
    p.op("gpsimd", lambda e: e.collective_compute(
        "AllGather", ALU.bypass, replica_groups=[list(range(NCORES))],
        ins=[c["woutl"].opt()], outs=[c["woutg"].opt()]), sem="cc")


def emit_shared(c):
    p = c["p"]
    psA, psB, psC, psD = c["psA"], c["psB"], c["psC"], c["psD"]
    for fc in range(FC):
        p.dma("gpsimd", c["sgwb"][:, fc:fc + 1], c["sgw"][fc * 128:(fc + 1) * 128, :], "wld")
    p.wait("tensor", "wld")
    for i in range(TT):
        p.wait("tensor", "act")
        p.wait("tensor", "dve")
        for fc in range(FC):
            p.op("tensor", lambda e, i=i, fc=fc: e.matmul(
                psC[:, 0:1],
                ab(c, AB_XN + fc * T + i * 128, 128),
                c["sgwb"][:, fc:fc + 1],
                start=(fc == 0), stop=(fc == FC - 1)),
                sem=("pe" if fc == FC - 1 else None))
        p.wait("scalar", "pe")
        p.op("scalar", lambda e, i=i: e.activation(
            out=af(c, AF_RK + 400 + i, 1), in_=psC[:, 0:1], func=AF_.Sigmoid), sem="act")
    p.wait("sync", "act")
    for i in range(TT):
        p.dma("sync", c["sgd"][i * 128:(i + 1) * 128].rearrange("(p o) -> p o", o=1),
              af(c, AF_RK + 400 + i, 1), "gd")
    p.wait("sync", "gd")
    p.dma("sync", af(c, AF_LNR, 2048)[0:1, :], c["sgd"].rearrange("(o a) -> o a", o=1), "ld")
    for i4 in range(4):
        for icl in range(4):
            ic4 = i4 * 4 + icl
            p.wait("gpsimd", "pe")
            gsrc = c["sgu"][:, ic4 * 128:(ic4 + 1) * 128].rearrange("(h p) n -> p h n", p=128)
            usrc = c["sgu"][:, ISH + ic4 * 128:ISH + (ic4 + 1) * 128].rearrange("(h p) n -> p h n", p=128)
            gdst = ab(c, AB_EW, 2048).rearrange("p (h n) -> p h n", n=256)[:, :, 0:128]
            udst = ab(c, AB_EW, 2048).rearrange("p (h n) -> p h n", n=256)[:, :, 128:256]
            p.dma("gpsimd", gdst, gsrc, "wld")
            p.dma("gpsimd", udst, usrc, "wld")
            p.wait("tensor", "wld")
            for tc in range(TC):
                p.wait("tensor", "act")
                p.wait("tensor", "dve")
                for hc in range(FC):
                    p.op("tensor", lambda e, tc=tc, hc=hc: e.matmul(
                        psA[:, 0:512],
                        ab(c, AB_EW + hc * 256, 128),
                        ab(c, AB_XN + hc * T + tc * 512, 512),
                        start=(hc == 0), stop=(hc == FC - 1)),
                        sem=("pe" if hc == FC - 1 else None))
                for hc in range(FC):
                    p.op("tensor", lambda e, tc=tc, hc=hc: e.matmul(
                        psB[:, 0:512],
                        ab(c, AB_EW + hc * 256 + 128, 128),
                        ab(c, AB_XN + hc * T + tc * 512, 512),
                        start=(hc == 0), stop=(hc == FC - 1)),
                        sem=("pe" if hc == FC - 1 else None))
                p.wait("scalar", "pe")
                p.op("scalar", lambda e: e.activation(
                    out=ab(c, AB_WOUT, 512), in_=psA[:, 0:512], func=AF_.Silu), sem="act")
                p.wait("vector", "act")
                p.op("vector", lambda e, icl=icl, tc=tc: e.tensor_mul(
                    ab(c, AB_ACT4 + icl * 2048 + tc * 512, 512),
                    ab(c, AB_WOUT, 512), psB[:, 0:512]), sem="dve")
        p.wait("gpsimd", "pe")
        for icl in range(4):
            ic4 = i4 * 4 + icl
            p.dma("gpsimd", ab(c, AB_EW + 2048 + icl * 1024, 1024),
                  c["sdw"][ic4 * 128:(ic4 + 1) * 128, :], "wld")
        p.wait("tensor", "wld")
        p.wait("tensor", "dve")
        for hc in range(FC):
            for tc in range(TC):
                ps = psC if (hc * TC + tc) % 2 == 0 else psD
                p.wait("tensor", "dve")
                for icl in range(4):
                    p.op("tensor", lambda e, ps=ps, hc=hc, tc=tc, icl=icl: e.matmul(
                        ps[:, 0:512],
                        ab(c, AB_EW + 2048 + icl * 1024 + hc * 128, 128),
                        ab(c, AB_ACT4 + icl * 2048 + tc * 512, 512),
                        start=(icl == 0), stop=(icl == 3)),
                        sem=("pe" if icl == 3 else None))
                p.wait("vector", "pe")
                dst = ab(c, AB_SHACC + hc * T + tc * 512, 512)
                if i4 == 0:
                    p.op("vector", lambda e, ps=ps, dst=dst: e.tensor_copy(dst, ps[:, 0:512]), sem="dve")
                else:
                    p.op("vector", lambda e, ps=ps, dst=dst: e.tensor_add(dst, dst, ps[:, 0:512]), sem="dve")


COMB_PARTS = 3


def emit_combine(c):
    p = c["p"]
    psD, psS = c["psD"], c["psS"]
    p.wait("gpsimd", "cc")
    for tc in range(TC):
        p.wait("tensor", "dve")
        p.wait("tensor", "ld")
        p.op("tensor", lambda e, tc=tc: e.matmul(
            psD[:, 0:512], c["ones1"][0:1, 0:128],
            af(c, AF_LNR + tc * 512, 512)[0:1, :], start=True, stop=True), sem="pe")
        for ii in range(4):
            i = tc * 4 + ii
            slot_a = AF_S1 + (i % 2) * 1024
            slot_x = AF_S2 + (i % 2) * 1024
            p.wait("sync", "dve")
            x1src = c["x1d"][:, i * 128:(i + 1) * 128].rearrange("(h p) n -> p h n", p=128)
            x1dst = af(c, slot_x, 1024).rearrange("p (h n) -> p h n", n=128)
            p.dma("sync", x1dst, x1src, "ld")
            p.wait("gpsimd", "dve")
            p.op("gpsimd", lambda e: e.memset(ab(c, AB_QF, 8192), 0.0), sem="gd", n=1)
            for j in range(TOPK):
                p.op("gpsimd", lambda e, i=i, j=j: e.indirect_dma_start(
                    out=ab(c, AB_QF + j * 1024, 1024), out_offset=None,
                    in_=c["woutg"][:, :],
                    in_offset=bass.IndirectOffsetOnAxis(
                        ap=c["widx"][:, i * 8 + j:i * 8 + j + 1], axis=0),
                    bounds_check=c["bregs"][NE * CAP - 1], oob_is_err=False), sem="gd", n=16)
            p.wait("vector", "gd")
            p.wait("vector", "pe")
            p.op("vector", lambda e, slot_a=slot_a: e.tensor_add(
                af(c, slot_a, 1024), ab(c, AB_QF, 1024), ab(c, AB_QF + 1024, 1024)), sem="dve")
            for j in range(2, TOPK):
                p.op("vector", lambda e, j=j, slot_a=slot_a: e.tensor_add(
                    af(c, slot_a, 1024), af(c, slot_a, 1024),
                    ab(c, AB_QF + j * 1024, 1024)), sem="dve")
            for fc in range(FC):
                p.wait("tensor", "ld")
                p.wait("tensor", "dve")
                p.op("tensor", lambda e, fc=fc, slot_a=slot_a: e.transpose(
                    psS[:, 0:128], af(c, slot_a + fc * 128, 128), c["idft"][:, :]), sem="pe")
                slot = fc % 2
                p.wait("vector", "pe")
                p.wait("vector", "ld")
                p.wait("vector", "st")
                p.op("vector", lambda e, fc=fc, i=i, slot=slot: e.tensor_mul(
                    af(c, AF_RS + slot * 256, 128),
                    ab(c, AB_SHACC + fc * T + i * 128, 128),
                    psD[:, (i % 4) * 128:(i % 4) * 128 + 128]), sem="dve")
                p.op("vector", lambda e, slot=slot: e.tensor_add(
                    af(c, AF_RS + slot * 256 + 128, 128),
                    af(c, AF_RS + slot * 256, 128), psS[:, 0:128]), sem="dve")
                p.op("vector", lambda e, fc=fc, slot=slot, slot_x=slot_x: e.tensor_add(
                    af(c, AF_RS + slot * 256 + 128, 128),
                    af(c, AF_RS + slot * 256 + 128, 128),
                    af(c, slot_x + fc * 128, 128)), sem="dve")
                p.wait("sync", "dve")
                p.dma("sync", c["out_fm"][fc * 128:(fc + 1) * 128, i * 128:(i + 1) * 128],
                      af(c, AF_RS + slot * 256 + 128, 128), "st")


def emit_dump(c, regions):
    p = c["p"]
    p.wait("gpsimd", "dve")
    p.wait("gpsimd", "pe")
    p.wait("gpsimd", "act")
    p.wait("gpsimd", "gd")
    p.wait("gpsimd", "wld")
    p.wait("sync", "dve")
    p.wait("sync", "pe")
    p.wait("sync", "gd")
    p.wait("sync", "act")
    p.wait("sync", "wld")
    p.wait("sync", "cc")
    if "ab" in regions:
        CHW = 8192
        for o in range(0, AB_N, CHW):
            w = min(CHW, AB_N - o)
            p.dma("gpsimd", c["dbg_ab"][:, o:o + w], c["AB"][:, o:o + w], "st")
    if "af" in regions:
        p.dma("sync", c["dbg_af"][:, :], c["AF"][:, :], "st")
    if "sag" in regions:
        p.dma("sync", c["dbg_sag"][0:128, :], af(c, AF_SSB, 1040), "st")
        p.dma("sync", c["dbg_sag"][128:160, :], c["ssbB"][:, :], "st")
    if "rec" in regions:
        p.dma("sync", c["dbg_rec"][:, :], c["recmr"][:, :], "st")
    if "cumd" in regions:
        p.dma("sync", c["dbg_cum"][:, :], c["cum"][:, :], "st")
    if "widxd" in regions:
        p.dma("sync", c["dbg_widx"][:, :], c["widx"][:, :], "st")
    if "woutd" in regions:
        for o in range(0, EPC * CAP, 512):
            p.dma("gpsimd", c["dbg_wout"][o:o + 512, :], c["woutl"][o:o + 512, :], "st")
    p.wait("gpsimd", "st")
    p.wait("sync", "st")



def make_inputs(inputs):
    x = np.asarray(inputs["L0_attn_hidden_states"])
    Wq = np.asarray(inputs["L0_attn_Wq"])
    Wk = np.asarray(inputs["L0_attn_Wk"])
    Wv = np.asarray(inputs["L0_attn_Wv"])
    Wo = np.asarray(inputs["L0_attn_Wo"])
    gate = np.asarray(inputs["L0_ffn_gate_w"])
    sgw = np.asarray(inputs["L0_ffn_shared_gate_w"])
    sgu = np.asarray(inputs["L0_ffn_shared_gate_up_w"])
    sdw = np.asarray(inputs["L0_ffn_shared_down_w"])
    egu = np.asarray(inputs["L0_ffn_expert_gate_up_w"])
    edw = np.asarray(inputs["L0_ffn_expert_down_w"])

    wqk = np.concatenate([_pad_qk_weight(Wq), _pad_qk_weight(Wk)], axis=1)
    wvT = np.ascontiguousarray(Wv.T)
    woT = np.ascontiguousarray(Wo.T)
    Ls, Rs = _build_selection()
    msk = _causal_masks()
    iota64 = np.broadcast_to(np.arange(NE, dtype=np.float32)[None, :], (128, NE)).copy()

    in_maps = []
    for core in range(NCORES):
        b, half = core // 2, core % 2
        shard = x[b, half * T:(half + 1) * T, :]
        x_fm = np.ascontiguousarray(shard.T)
        tokid = (core * T + np.arange(T, dtype=np.float32)).reshape(TT, 128).T.copy()
        rankm = (np.arange(8) < core).astype(np.float32).reshape(8, 1)
        selm_v = np.zeros(8, np.float32)
        if half == 1:
            selm_v[core - 1] = 1.0
        selm = np.broadcast_to(selm_v[None, :], (128, 8)).copy()
        ridx = np.zeros((128, EPC * 4), np.int32)
        pcol = np.arange(128, dtype=np.int32)
        for e_i in range(EPC):
            for g in range(4):
                ridx[:, e_i * 4 + g] = (core * EPC + e_i) * 513 + g * 128 + pcol
        in_maps.append({
            "x_fm": x_fm,
            "wqk": wqk, "wv": wvT, "wo": woT,
            "gate": gate, "sgw": sgw, "sgu": sgu, "sdw": sdw,
            "egu": np.ascontiguousarray(egu[core * EPC:(core + 1) * EPC]),
            "edw": np.ascontiguousarray(edw[core * EPC:(core + 1) * EPC]),
            "lsel": Ls.astype(ml_dtypes.bfloat16),
            "rsel": Rs.astype(ml_dtypes.bfloat16),
            "msk": msk.astype(ml_dtypes.bfloat16),
            "idf": np.eye(128, dtype=np.float32),
            "idb": np.eye(128, dtype=ml_dtypes.bfloat16),
            "iota64": iota64,
            "tokid": tokid,
            "rankm": rankm,
            "selm": selm,
            "recidx": ridx,
            "cvals": np.broadcast_to(
                np.array([FSC, FB, LNEPS, 0, 0, 0, 0, 0], np.float32)[None, :],
                (128, 8)).copy(),
        })
    return in_maps


def kernel(**inputs):
    in_maps = make_inputs(inputs)
    nc = build_program()
    res = run_bass_kernel_spmd(nc, in_maps, core_ids=list(range(NCORES)))
    out = np.empty((B, S, HID), np.float32)
    for core in range(NCORES):
        b, half = core // 2, core % 2
        out[b, half * T:(half + 1) * T, :] = res.results[core]["out_fm"].T
    return out


# version 04 (final)
# speedup vs baseline: 1.0523x; 1.0185x over previous
import math
from collections import defaultdict

import numpy as np
import ml_dtypes

import concourse.bass as bass
import concourse.mybir as mybir
from concourse.bass_utils import run_bass_kernel_spmd

F32 = mybir.dt.float32
BF16 = mybir.dt.bfloat16
I32 = mybir.dt.int32
U32 = mybir.dt.uint32
AF_ = mybir.ActivationFunctionType
ALU = mybir.AluOpType
AX = mybir.AxisListType

B, S, HID = 4, 4096, 1024
NCORES = 8
T = 2048
TT = 16
TC = 4
FC = 8
NH = 16
DPHI = 153
CH = 8
NE = 64
EPC = 8
CAP = 512
TOPK = 8
ISH = 2048
IEX = 512

RDs = 4.0
RRD = 2.0
R2 = math.sqrt(2.0)
FSC = 1.0 / math.sqrt(32.0)
FB = 3.0 * FSC
BIG = 1.0e9
LNEPS = 1e-5

AB_XN = 0
AB_QF = 16384
AB_KF = 24576
AB_VT = 32768
AB_YF = 49408
AB_N = 65792
AB_PHIQA = AB_XN + 0
AB_PHIQB = AB_XN + 2048
AB_PHIK = AB_XN + 4096
AB_A4 = AB_XN + 5120
AB_STG = AB_XN + 7168
AB_BUF_FM = AB_QF + 0
AB_BUF_T = AB_QF + 4096
AB_EW = AB_KF + 0
AB_ACT = AB_VT + 0
AB_WOUT = AB_VT + 2048
AB_ACT4 = AB_VT + 4096
AB_SHACC = AB_YF

AF_S1 = 0
AF_S2 = 2048
AF_DEN = 4096
AF_SSB = 4352
AF_CAR = 5392
AF_RS = 5408
AF_RK = 6944
AF_ROWS = 7456
AF_LNR = 8480
AF_N = 12608


def _build_selection():
    L = np.zeros((32, DPHI), np.float32)
    R = np.zeros((32, DPHI), np.float32)
    L[16, 0] = 1.0
    R[16, 0] = 1.0
    for i in range(16):
        L[i, 1 + i] = 1.0 / RRD
        R[16, 1 + i] = 1.0
    for i in range(16):
        L[i, 17 + i] = 1.0 / (RDs * R2)
        R[i, 17 + i] = 1.0
    cc = 33
    for i in range(16):
        for j in range(i + 1, 16):
            L[i, cc] = 1.0 / RDs
            R[j, cc] = 1.0
            cc += 1
    assert cc == DPHI
    Ls = np.zeros((128, DPHI), np.float32)
    Rs = np.zeros((128, DPHI), np.float32)
    for g in range(4):
        Ls[g * 32:(g + 1) * 32] = L
        Rs[g * 32:(g + 1) * 32] = R
    return Ls, Rs


def _pad_qk_weight(W):
    Wp = np.zeros((HID, 512), np.float32)
    for h in range(NH):
        Wp[:, h * 32: h * 32 + 16] = W[h * 16:(h + 1) * 16, :].T
    return Wp


def _causal_masks():
    j = np.arange(256)[None, :]
    p = np.arange(128)[:, None]
    m0 = (p <= j).astype(np.float32)
    m1 = (p + 128 <= j).astype(np.float32)
    return np.concatenate([m0, m1], axis=1)


ENGINES = ("sync", "scalar", "vector", "tensor", "gpsimd")


class P:
    def __init__(self, nc, sems):
        self.nc = nc
        self.sems = sems
        self.streams = {e: [] for e in ENGINES}
        self.cnt = defaultdict(int)
        self.lastw = {}

    def op(self, eng, fn, sem=None, n=1):
        if eng == "vector":
            assert sem == "dve", "vector ops must increment dve"
            self.wait(eng, "dve")
        elif eng == "scalar" and sem == "act":
            self.wait(eng, "act")
        if sem is None:
            self.streams[eng].append(fn)
        else:
            self.cnt[sem] += n
            s = self.sems[sem]
            self.streams[eng].append(lambda e, fn=fn, s=s, n=n: fn(e).then_inc(s, n))

    def wait(self, eng, sem, val=None):
        v = self.cnt[sem] if val is None else val
        if v <= 0:
            return
        key = (eng, sem)
        if self.lastw.get(key, -1) >= v:
            return
        self.lastw[key] = v
        s = self.sems[sem]
        self.streams[eng].append(lambda e, s=s, v=v: e.wait_ge(s, v))

    def dma(self, eng, out, in_, sem, **kw):
        self.op(eng, lambda e, out=out, in_=in_, kw=kw: e.dma_start(out=out, in_=in_, **kw),
                sem=sem, n=16)

    def replay(self, block):
        def runner(name):
            def run(e):
                for f in self.streams[name]:
                    f(e)
            return run
        block.sync(runner("sync"))
        block.scalar(runner("scalar"))
        block.vector(runner("vector"))
        block.tensor(runner("tensor"))
        block.gpsimd(runner("gpsimd"))


def build_program(debug_stage=None):
    nc = bass.Bass("TRN2", target_bir_lowering=False)
    dt = nc.dram_tensor

    io = {}
    io["x_fm"] = dt("x_fm", [HID, T], F32, kind="ExternalInput").ap()
    io["wqk"] = dt("wqk", [HID, 1024], F32, kind="ExternalInput").ap()
    io["wv"] = dt("wv", [HID, HID], F32, kind="ExternalInput").ap()
    io["wo"] = dt("wo", [HID, HID], F32, kind="ExternalInput").ap()
    io["gate"] = dt("gate", [HID, NE], F32, kind="ExternalInput").ap()
    io["sgw"] = dt("sgw", [HID, 1], F32, kind="ExternalInput").ap()
    io["sgu"] = dt("sgu", [HID, 2 * ISH], F32, kind="ExternalInput").ap()
    io["sdw"] = dt("sdw", [ISH, HID], F32, kind="ExternalInput").ap()
    io["egu"] = dt("egu", [EPC, HID, 2 * IEX], F32, kind="ExternalInput").ap()
    io["edw"] = dt("edw", [EPC, IEX, HID], F32, kind="ExternalInput").ap()
    io["lsel"] = dt("lsel", [128, DPHI], BF16, kind="ExternalInput").ap()
    io["rsel"] = dt("rsel", [128, DPHI], BF16, kind="ExternalInput").ap()
    io["msk"] = dt("msk", [128, 512], BF16, kind="ExternalInput").ap()
    io["idf"] = dt("idf", [128, 128], F32, kind="ExternalInput").ap()
    io["idb"] = dt("idb", [128, 128], BF16, kind="ExternalInput").ap()
    io["iota64"] = dt("iota64", [128, NE], F32, kind="ExternalInput").ap()
    io["tokid"] = dt("tokid", [128, TT], F32, kind="ExternalInput").ap()
    io["rankm"] = dt("rankm", [8, 1], F32, kind="ExternalInput").ap()
    io["selm"] = dt("selm", [128, 8], F32, kind="ExternalInput").ap()
    io["recidx"] = dt("recidx", [128, EPC * 4], I32, kind="ExternalInput").ap()
    io["cvals"] = dt("cvals", [128, 8], F32, kind="ExternalInput").ap()
    io["out_fm"] = dt("out_fm", [HID, T], F32, kind="ExternalOutput").ap()

    io["x1d"] = dt("x1d", [HID, T], F32).ap()
    io["onesd"] = dt("onesd", [2048], BF16).ap()
    io["lnd"] = dt("lnd", [4096], F32).ap()
    io["sgd"] = dt("sgd", [T], F32).ap()
    io["sagi"] = dt("sagi", [160, NH * 65], F32).ap()
    io["sago"] = dt("sago", [8 * 160, NH * 65], F32).ap()
    io["cnti"] = dt("cnti", [NE, 1], F32).ap()
    io["cnto"] = dt("cnto", [8 * NE, 1], F32).ap()
    io["recm"] = dt("recm", [NE * 513, 2], F32).ap()
    io["recmr"] = dt("recmr", [NE * 513, 2], F32).ap()
    io["xgi"] = dt("xgi", [T, HID], BF16).ap()
    io["xgo"] = dt("xgo", [NCORES * T, HID], BF16).ap()
    io["woutl"] = dt("woutl", [EPC * CAP, HID], BF16).ap()
    io["ymd"] = dt("ymd", [T, HID], F32).ap()
    io["woutg"] = dt("woutg", [NE * CAP, HID], BF16).ap()

    if debug_stage is not None:
        io["dbg_ab"] = dt("dbg_ab", [128, AB_N], F32, kind="ExternalOutput").ap()
        io["dbg_af"] = dt("dbg_af", [128, AF_N], F32, kind="ExternalOutput").ap()
        io["dbg_rec"] = dt("dbg_rec", [NE * 513, 2], F32, kind="ExternalOutput").ap()
        io["dbg_sag"] = dt("dbg_sag", [160, NH * 65], F32, kind="ExternalOutput").ap()
        io["dbg_cum"] = dt("dbg_cum", [64, 2048], F32, kind="ExternalOutput").ap()
        io["dbg_widx"] = dt("dbg_widx", [128, 128], I32, kind="ExternalOutput").ap()
        io["dbg_wout"] = dt("dbg_wout", [EPC * CAP, HID], F32, kind="ExternalOutput").ap()

    from contextlib import ExitStack
    with ExitStack() as ctx:
        en = ctx.enter_context
        AB = en(nc.sbuf_tensor("AB", [128, AB_N], BF16))
        AFr = en(nc.sbuf_tensor("AFr", [128, AF_N], F32))
        cumt = en(nc.sbuf_tensor("cum", [64, 2048], F32))
        ssbB = en(nc.sbuf_tensor("ssbB", [32, NH * 65], F32))
        sbf = en(nc.sbuf_tensor("sbf", [128, NH * 65], BF16))
        sbfB = en(nc.sbuf_tensor("sbfB", [32, NH * 65], BF16))
        spv = en(nc.sbuf_tensor("spv", [128, NH * 65], BF16))
        spvB = en(nc.sbuf_tensor("spvB", [32, NH * 65], BF16))
        lselt = en(nc.sbuf_tensor("lsel_sb", [128, DPHI], BF16))
        rselt = en(nc.sbuf_tensor("rsel_sb", [128, DPHI], BF16))
        mskt = en(nc.sbuf_tensor("mskt", [128, 512], BF16))
        idft = en(nc.sbuf_tensor("idf_sb", [128, 128], F32))
        idbt = en(nc.sbuf_tensor("idb_sb", [128, 128], BF16))
        iota64t = en(nc.sbuf_tensor("iota64_sb", [128, NE], F32))
        tokidt = en(nc.sbuf_tensor("tokid_sb", [128, TT], F32))
        selmt = en(nc.sbuf_tensor("selm_sb", [128, 8], F32))
        rankmt = en(nc.sbuf_tensor("rankm_sb", [8, 1], F32))
        recidxt = en(nc.sbuf_tensor("recidxt", [128, EPC * 4], I32))
        ones1 = en(nc.sbuf_tensor("ones1", [1, 512], F32))
        onesb = en(nc.sbuf_tensor("onesb", [1, 2048], BF16))
        onesc = en(nc.sbuf_tensor("onesc", [128, 1], F32))
        sgwb = en(nc.sbuf_tensor("sgwb", [128, 8], BF16))
        cvalt = en(nc.sbuf_tensor("cvalt", [128, 8], F32))
        idx8 = en(nc.sbuf_tensor("idx8", [128, 8], U32))
        sidx = en(nc.sbuf_tensor("sidx", [128, 8], I32))
        psA = en(nc.psum_tensor("psA", [128, 512], F32))
        psB = en(nc.psum_tensor("psB", [128, 512], F32))
        psC = en(nc.psum_tensor("psC", [128, 512], F32))
        psD = en(nc.psum_tensor("psD", [128, 512], F32))
        psY = en(nc.psum_tensor("psY", [65, 1024], F32))
        psS = en(nc.psum_tensor("psS", [128, 260], F32))
        psSB = en(nc.psum_tensor("psSB", [32, 260], F32))
        s_ld = en(nc.semaphore("ld"))
        s_wld = en(nc.semaphore("wld"))
        s_pe = en(nc.semaphore("pe"))
        s_act = en(nc.semaphore("act"))
        s_dve = en(nc.semaphore("dve"))
        s_gd = en(nc.semaphore("gd"))
        s_cc = en(nc.semaphore("cc"))
        s_st = en(nc.semaphore("st"))
        block = en(nc.Block())
        sems = dict(ld=s_ld, wld=s_wld, pe=s_pe, act=s_act, dve=s_dve,
                    gd=s_gd, cc=s_cc, st=s_st)
        p = P(nc, sems)
        c = dict(io)
        c.update(nc=nc, p=p, AB=AB, AF=AFr, cum=cumt, ssbB=ssbB, sbf=sbf,
                 sbfB=sbfB, spv=spv, spvB=spvB, lselt=lselt, rselt=rselt,
                 mskt=mskt, idft=idft, idbt=idbt, iota64t=iota64t,
                 tokidt=tokidt, selmt=selmt, rankmt=rankmt, recidxt=recidxt,
                 ones1=ones1, onesb=onesb, onesc=onesc,
                 onesf=AFr[0:64, AF_LNR + 2048:AF_LNR + 2048 + 128],
                 widx=AFr[:, AF_LNR + 2304:AF_LNR + 2304 + 128].bitcast(I32),
                 sgwb=sgwb, idx8=idx8, sidx=sidx,
                 psA=psA, psB=psB, psC=psC, psD=psD, psY=psY, psS=psS,
                 psSB=psSB, cvalt=cvalt, debug_stage=debug_stage)
        nc.const_aps.register(F32, FSC, cvalt[:, 0:1])
        nc.const_aps.register(F32, FB, cvalt[:, 1:2])
        nc.const_aps.register(F32, LNEPS, cvalt[:, 2:3])
        emit_all(c)
        p.replay(block)
    return nc


def ab(c, off, n):
    return c["AB"][:, off:off + n]


def abr(c, r0, r1, off, n):
    return c["AB"][r0:r1, off:off + n]


def af(c, off, n):
    return c["AF"][:, off:off + n]


def afr(c, r0, r1, off, n):
    return c["AF"][r0:r1, off:off + n]


def emit_all(c):
    p = c["p"]
    emit_setup(c)
    emit_ln(c, src="x0")
    emit_projections(c)
    if c["debug_stage"] == "proj":
        emit_dump(c, regions=["ab", "af"])
        return
    emit_scan(c)
    if c["debug_stage"] == "scan":
        emit_dump(c, regions=["ab", "af", "sag"])
        return
    emit_handoff(c)
    emit_correction(c)
    if c["debug_stage"] == "attn":
        emit_dump(c, regions=["ab", "af", "sag"])
        return
    emit_norm_wo_resid(c)
    emit_ln(c, src="x1")
    if c["debug_stage"] == "ln2":
        emit_dump(c, regions=["ab"])
        return
    emit_xg(c)
    emit_routing(c)
    if c["debug_stage"] == "routing":
        emit_dump(c, regions=["af", "rec", "cumd", "widxd"])
        return
    emit_experts(c)
    if c["debug_stage"] == "experts":
        emit_dump(c, regions=["af", "rec", "widxd", "woutd"])
        return
    emit_shared(c)
    if c["debug_stage"] == "shared":
        emit_dump(c, regions=["ab", "af"])
        return
    emit_combine(c)
    if c["debug_stage"] == "full":
        emit_dump(c, regions=["ab", "af"])
        return
    p.wait("gpsimd", "st")


def emit_setup(c):
    p = c["p"]
    c["bregs"] = {}

    def _alloc_bounds(e):
        for v in (NE * 513 - 1, NCORES * T - 1, NE * CAP - 1):
            r = e.alloc_register()
            e.reg_mov(r, v)
            c["bregs"][v] = r
        return None
    p.streams["gpsimd"].append(lambda e: _alloc_bounds(e))
    for name, tile in [("lsel", c["lselt"]), ("rsel", c["rselt"]),
                       ("msk", c["mskt"]), ("idf", c["idft"]), ("idb", c["idbt"]),
                       ("iota64", c["iota64t"]), ("tokid", c["tokidt"]),
                       ("selm", c["selmt"]), ("rankm", c["rankmt"]),
                       ("recidx", c["recidxt"]), ("cvals", c["cvalt"])]:
        p.dma("sync", tile[:], c[name][:], "ld")
    p.op("vector", lambda e: e.memset(c["ones1"][:], 1.0), sem="dve")
    p.op("vector", lambda e: e.memset(c["onesb"][:], 1.0), sem="dve")
    p.op("vector", lambda e: e.memset(c["onesc"][:], 1.0), sem="dve")
    p.op("vector", lambda e: e.memset(af(c, AF_SSB, 1040), 0.0), sem="dve")
    p.op("vector", lambda e: e.memset(c["ssbB"][:], 0.0), sem="dve")
    p.op("vector", lambda e: e.memset(c["sbf"][:], 0.0), sem="dve")
    p.op("vector", lambda e: e.memset(c["sbfB"][:], 0.0), sem="dve")
    p.op("vector", lambda e: e.memset(af(c, AF_DEN, 256), 0.0), sem="dve")
    p.wait("sync", "dve")
    p.dma("sync", c["onesd"].rearrange("(o a) -> o a", o=1), c["onesb"][0:1, :], "ld")


def emit_ln(c, src):
    p = c["p"]
    xsrc = c["x_fm"] if src == "x0" else c["x1d"]
    psA, psB, psC, psD = c["psA"], c["psB"], c["psC"], c["psD"]
    p.wait("sync", "gd")

    for tc in range(TC):
        for fc in range(FC):
            slot = fc % 2
            xs = af(c, AF_S1 + slot * 512, 512)
            sq = af(c, AF_S2 + slot * 512, 512)
            p.wait("sync", "pe")
            p.wait("sync", "act")
            p.dma("sync", xs, xsrc[fc * 128:(fc + 1) * 128, tc * 512:(tc + 1) * 512], "ld")
            p.wait("scalar", "ld")
            p.wait("scalar", "pe")
            p.op("scalar", lambda e, xs=xs, sq=sq: e.activation(
                out=sq, in_=xs, func=AF_.Square), sem="act")
            if fc == 0:
                p.wait("tensor", "dve")
            p.wait("tensor", "ld")
            p.wait("tensor", "act")
            p.op("tensor", lambda e, xs=xs, fc=fc: e.matmul(
                psA[0:1, 0:512], c["onesc"][:, 0:1], xs,
                start=(fc == 0), stop=(fc == FC - 1)), sem="pe")
            p.op("tensor", lambda e, sq=sq, fc=fc: e.matmul(
                psB[0:1, 0:512], c["onesc"][:, 0:1], sq,
                start=(fc == 0), stop=(fc == FC - 1)), sem="pe")
        p.wait("vector", "pe")
        p.wait("vector", "ld")
        p.op("vector", lambda e: e.tensor_copy(af(c, AF_ROWS, 512)[0:1, :], psA[0:1, 0:512]), sem="dve")
        p.wait("sync", "dve")
        p.dma("sync", c["lnd"][tc * 512:(tc + 1) * 512].rearrange("(o a) -> o a", o=1),
              af(c, AF_ROWS, 512)[0:1, :], "ld")
        p.wait("vector", "ld")
        p.op("vector", lambda e: e.tensor_copy(af(c, AF_ROWS, 512)[0:1, :], psB[0:1, 0:512]), sem="dve")
        p.wait("sync", "dve")
        p.dma("sync", c["lnd"][2048 + tc * 512:2048 + (tc + 1) * 512].rearrange("(o a) -> o a", o=1),
              af(c, AF_ROWS, 512)[0:1, :], "ld")

    p.wait("sync", "ld")
    p.wait("sync", "pe")
    p.wait("sync", "act")
    p.dma("sync", af(c, AF_S1, 32), c["lnd"].rearrange("(p a) -> p a", p=128), "ld")
    p.wait("vector", "ld")
    inv = 1.0 / HID
    p.op("vector", lambda e: e.tensor_copy(afr(c, 0, 64, AF_S1 + 32, 32),
                                           afr(c, 64, 128, AF_S1, 32)), sem="dve")
    p.op("vector", lambda e: e.tensor_scalar(afr(c, 0, 64, AF_S2, 32), afr(c, 0, 64, AF_S1, 32),
                                             inv, None, op0=ALU.mult), sem="dve")
    p.op("vector", lambda e: e.tensor_scalar(afr(c, 0, 64, AF_S2 + 32, 32),
                                             afr(c, 0, 64, AF_S1 + 32, 32),
                                             inv, None, op0=ALU.mult), sem="dve")
    p.op("vector", lambda e: e.tensor_mul(afr(c, 0, 64, AF_S1 + 64, 32),
                                          afr(c, 0, 64, AF_S2, 32),
                                          afr(c, 0, 64, AF_S2, 32)), sem="dve")
    p.op("vector", lambda e: e.tensor_sub(afr(c, 0, 64, AF_S1 + 96, 32),
                                          afr(c, 0, 64, AF_S2 + 32, 32),
                                          afr(c, 0, 64, AF_S1 + 64, 32)), sem="dve")
    p.wait("scalar", "dve")
    p.op("scalar", lambda e: e.activation(out=afr(c, 0, 64, AF_S2 + 64, 32),
                                          in_=afr(c, 0, 64, AF_S1 + 96, 32),
                                          func=AF_.Sqrt, bias=LNEPS), sem="act")
    p.wait("vector", "act")
    p.op("vector", lambda e: e.reciprocal(afr(c, 0, 64, AF_S2 + 96, 32),
                                          afr(c, 0, 64, AF_S2 + 64, 32)), sem="dve")
    p.op("vector", lambda e: e.tensor_mul(afr(c, 0, 64, AF_S1 + 64, 32),
                                          afr(c, 0, 64, AF_S2, 32),
                                          afr(c, 0, 64, AF_S2 + 96, 32)), sem="dve")
    p.wait("sync", "dve")
    p.dma("sync", c["lnd"][0:2048].rearrange("(p a) -> p a", p=64),
          afr(c, 0, 64, AF_S2 + 96, 32), "ld")
    p.dma("sync", c["lnd"][2048:4096].rearrange("(p a) -> p a", p=64),
          afr(c, 0, 64, AF_S1 + 64, 32), "ld")
    p.wait("sync", "ld")
    p.dma("sync", af(c, AF_LNR, 2048)[0:1, :],
          c["lnd"][0:2048].rearrange("(o a) -> o a", o=1), "ld")
    p.dma("sync", af(c, AF_LNR + 2048, 2048)[0:1, :],
          c["lnd"][2048:4096].rearrange("(o a) -> o a", o=1), "ld")

    for tc in range(TC):
        p.wait("tensor", "ld")
        p.wait("tensor", "dve")
        p.op("tensor", lambda e, tc=tc: e.matmul(
            psC[:, 0:512], c["ones1"][0:1, 0:128],
            af(c, AF_LNR + tc * 512, 512)[0:1, :], start=True, stop=True), sem="pe")
        p.op("tensor", lambda e, tc=tc: e.matmul(
            psD[:, 0:512], c["ones1"][0:1, 0:128],
            af(c, AF_LNR + 2048 + tc * 512, 512)[0:1, :], start=True, stop=True), sem="pe")
        for fc in range(FC):
            slot = fc % 2
            xs = af(c, AF_S1 + slot * 512, 512)
            tmp = af(c, AF_S2 + slot * 512, 512)
            p.wait("sync", "dve")
            p.dma("sync", xs, xsrc[fc * 128:(fc + 1) * 128, tc * 512:(tc + 1) * 512], "ld")
            p.wait("vector", "ld")
            p.wait("vector", "pe")
            p.op("vector", lambda e, xs=xs, tmp=tmp: e.tensor_mul(tmp, xs, psC[:, 0:512]), sem="dve")
            p.op("vector", lambda e, tmp=tmp, fc=fc, tc=tc: e.tensor_sub(
                ab(c, AB_XN + fc * T + tc * 512, 512), tmp, psD[:, 0:512]), sem="dve")


def emit_projections(c):
    p = c["p"]
    psA, psB = c["psA"], c["psB"]
    WQK = AB_YF
    for fc in range(FC):
        p.dma("gpsimd", ab(c, WQK + fc * 1024, 1024),
              c["wqk"][fc * 128:(fc + 1) * 128, :], "wld")
    p.wait("tensor", "wld")
    for mj in range(8):
        for tc in range(TC):
            ps = psA if (mj * TC + tc) % 2 == 0 else psB
            p.wait("tensor", "dve")
            for fc in range(FC):
                p.op("tensor", lambda e, ps=ps, mj=mj, tc=tc, fc=fc: e.matmul(
                    ps[:, 0:512],
                    ab(c, WQK + fc * 1024 + mj * 128, 128),
                    ab(c, AB_XN + fc * T + tc * 512, 512),
                    start=(fc == 0), stop=(fc == FC - 1)),
                    sem=("pe" if fc == FC - 1 else None))
            dst_off = (AB_QF + mj * T if mj < 4 else AB_KF + (mj - 4) * T) + tc * 512
            p.wait("vector", "pe")
            p.op("vector", lambda e, ps=ps, dst_off=dst_off: e.tensor_copy(
                ab(c, dst_off, 512), ps[:, 0:512]), sem="dve")
    p.wait("gpsimd", "dve")
    for base in (AB_QF, AB_KF):
        for g in range(4):
            row = 16 + 32 * g
            src = c["onesd"].rearrange("(o b a) -> o b a", o=1, b=1).to_broadcast([1, 4, 2048])
            p.dma("gpsimd",
                  c["AB"][row:row + 1, base:base + 8192].rearrange("p (a b) -> p a b", b=2048),
                  src, "wld")
    p.wait("gpsimd", "pe")
    for fc in range(FC):
        p.dma("gpsimd", ab(c, WQK + fc * 1024, 1024),
              c["wv"][fc * 128:(fc + 1) * 128, :], "wld")
    p.wait("tensor", "wld")
    for i in range(TT):
        for n in range(2):
            ps = psA if (i * 2 + n) % 2 == 0 else psB
            p.wait("tensor", "dve")
            for fc in range(FC):
                p.op("tensor", lambda e, ps=ps, i=i, n=n, fc=fc: e.matmul(
                    ps[:, 0:512],
                    ab(c, AB_XN + fc * T + i * 128, 128),
                    ab(c, WQK + fc * 1024 + n * 512, 512),
                    start=(fc == 0), stop=(fc == FC - 1)),
                    sem=("pe" if fc == FC - 1 else None))
            p.wait("vector", "pe")
            dst = c["AB"][:, AB_VT + i * 1040 + n * 8 * 65: AB_VT + i * 1040 + (n * 8 + 8) * 65]
            dstv = dst.rearrange("p (h d) -> p h d", d=65)[:, :, 0:64]
            srcv = ps[:, 0:512].rearrange("p (h d) -> p h d", d=64)
            p.op("vector", lambda e, dstv=dstv, srcv=srcv: e.tensor_copy(dstv, srcv), sem="dve")
        dstones = c["AB"][:, AB_VT + i * 1040: AB_VT + (i + 1) * 1040]
        dstones = dstones.rearrange("p (h d) -> p h d", d=65)[:, :, 64:65]
        p.op("vector", lambda e, dstones=dstones: e.memset(dstones, 1.0), sem="dve")


SCAN_G = 4
SCAN_CH = CH
SCAN_PARTS = 5
SCAN_MASK = True
SCAN_F = True


def emit_scan(c):
    p = c["p"]
    psA, psB, psC, psD = c["psA"], c["psB"], c["psC"], c["psD"]
    psY, psS, psSB = c["psY"], c["psS"], c["psSB"]
    p.wait("tensor", "wld")

    for g in range(SCAN_G):
        for ch in range(SCAN_CH):
            tq0 = ch * 256
            pss = (psA, psB, psC, psD)
            p.wait("tensor", "dve")
            for hp in range(4):
                p.op("tensor", lambda e, hp=hp, g=g, tq0=tq0: e.matmul(
                    pss[hp][:, 0:256],
                    c["lselt"][32 * hp:32 * hp + 32, 0:128],
                    ab(c, AB_QF + g * T + tq0, 256)[32 * hp:32 * hp + 32, :],
                    start=True, stop=True, tile_position=(32 * hp, 0)), sem="pe")
                p.op("tensor", lambda e, hp=hp, g=g, tq0=tq0: e.matmul(
                    pss[hp][0:25, 256:512],
                    c["lselt"][32 * hp:32 * hp + 32, 128:153],
                    ab(c, AB_QF + g * T + tq0, 256)[32 * hp:32 * hp + 32, :],
                    start=True, stop=True, tile_position=(32 * hp, 0)), sem="pe")
            p.wait("vector", "pe")
            for hp in range(4):
                p.op("vector", lambda e, hp=hp: e.tensor_copy(
                    ab(c, AB_STG + hp * 256, 256), pss[hp][:, 0:256]), sem="dve")
                p.op("vector", lambda e, hp=hp: e.tensor_copy(
                    abr(c, 0, 25, AB_STG + 1024 + hp * 256, 256),
                    pss[hp][0:25, 256:512]), sem="dve")
            p.wait("tensor", "dve")
            for hp in range(4):
                p.op("tensor", lambda e, hp=hp, g=g, tq0=tq0: e.matmul(
                    pss[hp][:, 0:256],
                    c["rselt"][32 * hp:32 * hp + 32, 0:128],
                    ab(c, AB_QF + g * T + tq0, 256)[32 * hp:32 * hp + 32, :],
                    start=True, stop=True, tile_position=(32 * hp, 0)), sem="pe")
                p.op("tensor", lambda e, hp=hp, g=g, tq0=tq0: e.matmul(
                    pss[hp][0:25, 256:512],
                    c["rselt"][32 * hp:32 * hp + 32, 128:153],
                    ab(c, AB_QF + g * T + tq0, 256)[32 * hp:32 * hp + 32, :],
                    start=True, stop=True, tile_position=(32 * hp, 0)), sem="pe")
            p.wait("vector", "pe")
            for hp in range(4):
                p.op("vector", lambda e, hp=hp: e.tensor_mul(
                    ab(c, AB_PHIQA + hp * 256, 256),
                    ab(c, AB_STG + hp * 256, 256), pss[hp][:, 0:256]), sem="dve")
                p.op("vector", lambda e, hp=hp: e.tensor_mul(
                    abr(c, 0, 25, AB_PHIQB + hp * 256, 256),
                    abr(c, 0, 25, AB_STG + 1024 + hp * 256, 256),
                    pss[hp][0:25, 256:512]), sem="dve")

            for tk in range(2 if SCAN_PARTS >= 2 else 0):
                p.wait("tensor", "dve")
                if tk == 1:
                    p.wait("tensor", "act")
                for hp in range(4):
                    p.op("tensor", lambda e, hp=hp, g=g, tq0=tq0, tk=tk: e.matmul(
                        pss[hp][:, 0:256],
                        ab(c, AB_KF + g * T + tq0 + tk * 128, 128)[32 * hp:32 * hp + 32, :],
                        ab(c, AB_QF + g * T + tq0, 256)[32 * hp:32 * hp + 32, :],
                        start=True, stop=True, tile_position=(32 * hp, 0)), sem="pe")
                a4o = AB_A4 + tk * 1024
                p.wait("scalar", "pe")
                for hp in range(4):
                    p.op("scalar", lambda e, hp=hp, a4o=a4o: e.activation(
                        out=ab(c, a4o + hp * 256, 256), in_=pss[hp][:, 0:256],
                        func=AF_.Square, scale=FSC, bias=FB), sem="act")
                p.wait("vector", "act")
                a4v = ab(c, a4o, 1024).rearrange("p (h n) -> p h n", n=256)
                mskv = c["mskt"][:, tk * 256:(tk + 1) * 256].rearrange(
                    "p (o n) -> p o n", o=1).to_broadcast([128, 4, 256])
                if SCAN_MASK:
                    p.op("vector", lambda e, a4v=a4v, mskv=mskv: e.scalar_tensor_tensor(
                        out=a4v, in0=a4v, scalar=0.5, in1=mskv,
                        op0=ALU.add, op1=ALU.mult), sem="dve")

            p.wait("tensor", "dve")
            for hp in range(4 if SCAN_PARTS >= 3 else 0):
                h = 4 * g + hp
                col = hp * 256
                for tk in range(2):
                    p.op("tensor", lambda e, h=h, hp=hp, col=col, tk=tk, ch=ch: e.matmul(
                        psY[0:65, col:col + 256],
                        ab(c, AB_VT + (2 * ch + tk) * 1040 + h * 65, 65),
                        ab(c, AB_A4 + tk * 1024 + hp * 256, 256),
                        start=(tk == 0), stop=False), sem="pe")
                p.op("tensor", lambda e, h=h, hp=hp, col=col: e.matmul(
                    psY[0:65, col:col + 256],
                    c["sbf"][:, h * 65:h * 65 + 65],
                    ab(c, AB_PHIQA + hp * 256, 256),
                    start=False, stop=False), sem="pe")
                p.op("tensor", lambda e, h=h, hp=hp, col=col: e.matmul(
                    psY[0:65, col:col + 256],
                    c["sbfB"][0:25, h * 65:h * 65 + 65],
                    abr(c, 0, 25, AB_PHIQB + hp * 256, 256),
                    start=False, stop=True), sem="pe")
            p.wait("vector", "pe")
            p.wait("vector", "gd")
            for hp in range(4 if SCAN_PARTS >= 3 else 0):
                h = 4 * g + hp
                p.op("vector", lambda e, h=h, hp=hp, tq0=tq0: e.tensor_copy(
                    abr(c, (h % 2) * 64, (h % 2) * 64 + 64, AB_YF + (h // 2) * T + tq0, 256),
                    psY[0:64, hp * 256:hp * 256 + 256]), sem="dve")
            if SCAN_PARTS >= 3:
                p.op("vector", lambda e: e.tensor_copy(
                    af(c, AF_ROWS, 1024)[0:1, :], psY[64:65, 0:1024]), sem="dve")
                p.wait("sync", "dve")
                dstden = afr(c, ch * 16 + g * 4, ch * 16 + g * 4 + 4, AF_DEN, 256)
                srcden = af(c, AF_ROWS, 1024)[0:1, :].rearrange("p (a b) -> p a b", b=256)
                p.op("sync", lambda e, dstden=dstden, srcden=srcden: e.dma_start(
                    out=dstden, in_=srcden), sem="gd", n=16)

            first_state = True
            for tk in range(2 if SCAN_PARTS >= 4 else 0):
                p.wait("tensor", "dve")
                for hp in range(4):
                    p.op("tensor", lambda e, hp=hp, g=g, tq0=tq0, tk=tk: e.matmul(
                        pss[hp][:, 0:153],
                        ab(c, AB_KF + g * T + tq0 + tk * 128, 128)[32 * hp:32 * hp + 32, :],
                        c["lselt"][32 * hp:32 * hp + 32, 0:153],
                        start=True, stop=True, tile_position=(32 * hp, 0)), sem="pe")
                p.wait("vector", "pe")
                for hp in range(4):
                    p.op("vector", lambda e, hp=hp: e.tensor_copy(
                        ab(c, AB_STG + hp * 153, 153), pss[hp][:, 0:153]), sem="dve")
                p.wait("tensor", "dve")
                for hp in range(4):
                    p.op("tensor", lambda e, hp=hp, g=g, tq0=tq0, tk=tk: e.matmul(
                        pss[hp][:, 0:153],
                        ab(c, AB_KF + g * T + tq0 + tk * 128, 128)[32 * hp:32 * hp + 32, :],
                        c["rselt"][32 * hp:32 * hp + 32, 0:153],
                        start=True, stop=True, tile_position=(32 * hp, 0)), sem="pe")
                p.wait("vector", "pe")
                for hp in range(4):
                    p.op("vector", lambda e, hp=hp: e.tensor_mul(
                        ab(c, AB_PHIK + hp * 153, 153),
                        ab(c, AB_STG + hp * 153, 153), pss[hp][:, 0:153]), sem="dve")
                p.wait("tensor", "dve")
                for hp in range(4):
                    h = 4 * g + hp
                    p.op("tensor", lambda e, hp=hp, h=h, ch=ch, tk=tk, fs=first_state: e.matmul(
                        psS[:, hp * 65:hp * 65 + 65],
                        ab(c, AB_PHIK + hp * 153, 128),
                        ab(c, AB_VT + (2 * ch + tk) * 1040 + h * 65, 65),
                        start=(fs and hp == 0), stop=(tk == 1 and hp == 3)), sem="pe")
                    p.op("tensor", lambda e, hp=hp, h=h, ch=ch, tk=tk, fs=first_state: e.matmul(
                        psSB[0:25, hp * 65:hp * 65 + 65],
                        ab(c, AB_PHIK + hp * 153 + 128, 25),
                        ab(c, AB_VT + (2 * ch + tk) * 1040 + h * 65, 65),
                        start=(fs and hp == 0), stop=(tk == 1 and hp == 3)), sem="pe")
                first_state = False
            if SCAN_PARTS < 5:
                continue
            p.wait("vector", "pe")
            p.op("vector", lambda e, g=g: e.tensor_add(
                af(c, AF_SSB + g * 260, 260), af(c, AF_SSB + g * 260, 260),
                psS[:, 0:260]), sem="dve")
            p.op("vector", lambda e, g=g: e.tensor_add(
                c["ssbB"][0:25, g * 260:g * 260 + 260],
                c["ssbB"][0:25, g * 260:g * 260 + 260],
                psSB[0:25, 0:260]), sem="dve")
            p.op("vector", lambda e, g=g: e.tensor_copy(
                c["sbf"][:, g * 260:g * 260 + 260], af(c, AF_SSB + g * 260, 260)), sem="dve")
            p.op("vector", lambda e, g=g: e.tensor_copy(
                c["sbfB"][0:25, g * 260:g * 260 + 260],
                c["ssbB"][0:25, g * 260:g * 260 + 260]), sem="dve")


def emit_handoff(c):
    p = c["p"]
    p.wait("sync", "dve")
    p.dma("sync", c["sagi"][0:128, :], af(c, AF_SSB, 1040), "gd")
    p.dma("sync", c["sagi"][128:160, :], c["ssbB"][:, :], "gd")
    p.wait("gpsimd", "gd")
    p.op("gpsimd", lambda e: e.collective_compute(
        "AllGather", ALU.bypass, replica_groups=[list(range(NCORES))],
        ins=[c["sagi"].opt()], outs=[c["sago"].opt()]), sem="cc")
    p.wait("sync", "cc")
    p.op("vector", lambda e: e.memset(af(c, AF_S1, 1040), 0.0), sem="dve")
    p.op("vector", lambda e: e.memset(afr(c, 32, 64, AF_S2, 1040), 0.0), sem="dve")
    for r in range(NCORES):
        p.wait("sync", "dve")
        p.dma("sync", af(c, AF_RS, 1040), c["sago"][r * 160:r * 160 + 128, :], "ld")
        p.dma("sync", afr(c, 32, 64, AF_LNR, 1040), c["sago"][r * 160 + 128:(r + 1) * 160, :], "ld")
        p.wait("vector", "ld")
        p.op("vector", lambda e, r=r: e.scalar_tensor_tensor(
            out=af(c, AF_S1, 1040), in0=af(c, AF_RS, 1040),
            scalar=c["selmt"][:, r:r + 1], in1=af(c, AF_S1, 1040),
            op0=ALU.mult, op1=ALU.add), sem="dve")
        p.op("vector", lambda e, r=r: e.scalar_tensor_tensor(
            out=afr(c, 32, 64, AF_S2, 1040), in0=afr(c, 32, 64, AF_LNR, 1040),
            scalar=c["selmt"][32:64, r:r + 1], in1=afr(c, 32, 64, AF_S2, 1040),
            op0=ALU.mult, op1=ALU.add), sem="dve")
    p.op("vector", lambda e: e.tensor_copy(c["spv"][:, :], af(c, AF_S1, 1040)), sem="dve")
    p.op("vector", lambda e: e.tensor_copy(c["spvB"][:, :], afr(c, 32, 64, AF_S2, 1040)), sem="dve")


def emit_correction(c):
    p = c["p"]
    psA, psB, psC, psD, psY = c["psA"], c["psB"], c["psC"], c["psD"], c["psY"]
    pss = (psA, psB, psC, psD)
    for g in range(4):
        for c2 in range(TC):
            tq0 = c2 * 512
            p.wait("tensor", "dve")
            for hp in range(4):
                p.op("tensor", lambda e, hp=hp, g=g, tq0=tq0: e.matmul(
                    pss[hp][:, 0:512],
                    c["lselt"][32 * hp:32 * hp + 32, 0:128],
                    ab(c, AB_QF + g * T + tq0, 512)[32 * hp:32 * hp + 32, :],
                    start=True, stop=True, tile_position=(32 * hp, 0)), sem="pe")
            p.wait("vector", "pe")
            for hp in range(4):
                p.op("vector", lambda e, hp=hp: e.tensor_copy(
                    ab(c, AB_STG + hp * 512, 512), pss[hp][:, 0:512]), sem="dve")
            p.wait("tensor", "dve")
            for hp in range(4):
                p.op("tensor", lambda e, hp=hp, g=g, tq0=tq0: e.matmul(
                    pss[hp][:, 0:512],
                    c["rselt"][32 * hp:32 * hp + 32, 0:128],
                    ab(c, AB_QF + g * T + tq0, 512)[32 * hp:32 * hp + 32, :],
                    start=True, stop=True, tile_position=(32 * hp, 0)), sem="pe")
            p.wait("vector", "pe")
            for hp in range(4):
                p.op("vector", lambda e, hp=hp: e.tensor_mul(
                    ab(c, AB_PHIQA + hp * 512, 512),
                    ab(c, AB_STG + hp * 512, 512), pss[hp][:, 0:512]), sem="dve")
            p.wait("tensor", "dve")
            for hp in range(4):
                p.op("tensor", lambda e, hp=hp, g=g, tq0=tq0: e.matmul(
                    pss[hp][0:25, 0:512],
                    c["lselt"][32 * hp:32 * hp + 32, 128:153],
                    ab(c, AB_QF + g * T + tq0, 512)[32 * hp:32 * hp + 32, :],
                    start=True, stop=True, tile_position=(32 * hp, 0)), sem="pe")
            p.wait("vector", "pe")
            for hp in range(4):
                p.op("vector", lambda e, hp=hp: e.tensor_copy(
                    abr(c, 0, 25, AB_STG + hp * 512, 512), pss[hp][0:25, 0:512]), sem="dve")
            p.wait("tensor", "dve")
            for hp in range(4):
                p.op("tensor", lambda e, hp=hp, g=g, tq0=tq0: e.matmul(
                    pss[hp][0:25, 0:512],
                    c["rselt"][32 * hp:32 * hp + 32, 128:153],
                    ab(c, AB_QF + g * T + tq0, 512)[32 * hp:32 * hp + 32, :],
                    start=True, stop=True, tile_position=(32 * hp, 0)), sem="pe")
            p.wait("vector", "pe")
            for hp in range(4):
                p.op("vector", lambda e, hp=hp: e.tensor_mul(
                    abr(c, 0, 25, AB_PHIQB + hp * 512, 512),
                    abr(c, 0, 25, AB_STG + hp * 512, 512), pss[hp][0:25, 0:512]), sem="dve")
            for pr in range(2):
                p.wait("tensor", "dve")
                if pr == 1:
                    p.wait("tensor", "gd")
                for hh in range(2):
                    hp = pr * 2 + hh
                    h = 4 * g + hp
                    p.op("tensor", lambda e, h=h, hp=hp, hh=hh: e.matmul(
                        psY[0:65, hh * 512:hh * 512 + 512],
                        c["spv"][:, h * 65:h * 65 + 65],
                        ab(c, AB_PHIQA + hp * 512, 512),
                        start=True, stop=False), sem="pe")
                    p.op("tensor", lambda e, h=h, hp=hp, hh=hh: e.matmul(
                        psY[0:65, hh * 512:hh * 512 + 512],
                        c["spvB"][0:25, h * 65:h * 65 + 65],
                        abr(c, 0, 25, AB_PHIQB + hp * 512, 512),
                        start=False, stop=True), sem="pe")
                p.wait("vector", "pe")
                p.wait("vector", "gd")
                for hh in range(2):
                    hp = pr * 2 + hh
                    h = 4 * g + hp
                    ysl = abr(c, (h % 2) * 64, (h % 2) * 64 + 64, AB_YF + (h // 2) * T + tq0, 512)
                    p.op("vector", lambda e, ysl=ysl, hh=hh: e.tensor_add(
                        ysl, ysl, psY[0:64, hh * 512:hh * 512 + 512]), sem="dve")
                p.op("vector", lambda e: e.tensor_copy(
                    af(c, AF_ROWS, 1024)[0:1, :], psY[64:65, 0:1024]), sem="dve")
                p.wait("gpsimd", "dve")
                for hh in range(2):
                    hp = pr * 2 + hh
                    for cc2 in range(2):
                        row = (2 * c2 + cc2) * 16 + g * 4 + hp
                        p.op("gpsimd", lambda e, row=row, hh=hh, cc2=cc2: e.dma_start(
                            out=afr(c, row, row + 1, AF_DEN, 256),
                            in_=af(c, AF_ROWS + hh * 512 + cc2 * 256, 256)[0:1, :],
                            accum_op=ALU.add), sem="gd", n=16)


def emit_norm_wo_resid(c):
    p = c["p"]
    psA, psB, psC = c["psA"], c["psB"], c["psC"]
    p.wait("vector", "gd")
    p.op("vector", lambda e: e.tensor_scalar(af(c, AF_DEN, 256), af(c, AF_DEN, 256),
                                             1e-12, None, op0=ALU.add), sem="dve")
    p.op("vector", lambda e: e.reciprocal(af(c, AF_DEN, 256), af(c, AF_DEN, 256)), sem="dve")
    p.wait("sync", "gd")
    for fc in range(FC):
        for tc in range(TC):
            p.wait("sync", "dve")
            p.wait("sync", "pe")
            for hh in range(2):
                h = 2 * fc + hh
                for cc2 in range(2):
                    chv = 2 * tc + cc2
                    row = chv * 16 + (h // 4) * 4 + (h % 4)
                    p.dma("sync", af(c, AF_ROWS + hh * 512 + cc2 * 256, 256)[0:1, :],
                          afr(c, row, row + 1, AF_DEN, 256), "ld")
            p.wait("tensor", "ld")
            p.wait("tensor", "dve")
            p.op("tensor", lambda e: e.matmul(
                psC[0:64, 0:512], c["ones1"][0:1, 0:64],
                af(c, AF_ROWS, 512)[0:1, :], start=True, stop=True), sem="pe")
            p.op("tensor", lambda e: e.matmul(
                psC[64:128, 0:512], c["ones1"][0:1, 0:64],
                af(c, AF_ROWS + 512, 512)[0:1, :], start=True, stop=True), sem="pe")
            p.wait("vector", "pe")
            ysl = ab(c, AB_YF + fc * T + tc * 512, 512)
            p.op("vector", lambda e, ysl=ysl: e.tensor_mul(ysl, ysl, psC[:, 0:512]), sem="dve")
    WO = AB_QF
    p.wait("gpsimd", "dve")
    for fcw in range(FC):
        p.dma("gpsimd", ab(c, WO + fcw * 1024, 1024),
              c["wo"][fcw * 128:(fcw + 1) * 128, :], "wld")
    p.wait("tensor", "wld")
    for oc in range(FC):
        for tc in range(TC):
            ps = psA if (oc * TC + tc) % 2 == 0 else psB
            p.wait("tensor", "dve")
            for hd in range(FC):
                p.op("tensor", lambda e, ps=ps, oc=oc, tc=tc, hd=hd: e.matmul(
                    ps[:, 0:512],
                    ab(c, WO + hd * 1024 + oc * 128, 128),
                    ab(c, AB_YF + hd * T + tc * 512, 512),
                    start=(hd == 0), stop=(hd == FC - 1)),
                    sem=("pe" if hd == FC - 1 else None))
            slot = (oc * TC + tc) % 2
            xs = af(c, AF_S1 + slot * 512, 512)
            res = af(c, AF_S2 + slot * 512, 512)
            p.wait("sync", "dve")
            p.dma("sync", xs, c["x_fm"][oc * 128:(oc + 1) * 128, tc * 512:(tc + 1) * 512], "ld")
            p.wait("vector", "ld")
            p.wait("vector", "pe")
            p.wait("vector", "gd")
            p.op("vector", lambda e, ps=ps, xs=xs, res=res: e.tensor_add(res, xs, ps[:, 0:512]), sem="dve")
            p.wait("sync", "dve")
            p.dma("sync", c["x1d"][oc * 128:(oc + 1) * 128, tc * 512:(tc + 1) * 512], res, "gd")


def emit_xg(c):
    p = c["p"]
    psts = (c["psA"][:, 0:64].bitcast(BF16), c["psB"][:, 0:64].bitcast(BF16))
    dvec = [0, 0]
    for i in range(TT):
        for fc in range(FC):
            k = fc % 2
            pst = psts[k]
            p.wait("tensor", "dve", val=dvec[k])
            if fc == 0 and i == 0:
                p.wait("tensor", "dve")
                p.wait("tensor", "gd")
            if fc == 0:
                p.wait("tensor", "gd")
            p.op("tensor", lambda e, i=i, fc=fc, pst=pst: e.transpose(
                pst, ab(c, AB_XN + fc * T + i * 128, 128), c["idbt"][:, :]), sem="pe")
            p.wait("vector", "pe")
            p.op("vector", lambda e, fc=fc, pst=pst: e.tensor_copy(
                ab(c, AB_BUF_T + fc * 128, 128), pst), sem="dve")
            dvec[k] = p.cnt["dve"]
        p.wait("sync", "dve")
        p.dma("sync", c["xgi"][i * 128:(i + 1) * 128, :], ab(c, AB_BUF_T, 1024), "gd")
    p.wait("gpsimd", "gd")
    p.op("gpsimd", lambda e: e.collective_compute(
        "AllGather", ALU.bypass, replica_groups=[list(range(NCORES))],
        ins=[c["xgi"].opt()], outs=[c["xgo"].opt()]), sem="cc")


def emit_routing(c):
    p = c["p"]
    psA, psS = c["psA"], c["psS"]
    GW = AB_EW
    p.wait("gpsimd", "pe")
    for fc in range(FC):
        p.dma("gpsimd", ab(c, GW + fc * 64, 64),
              c["gate"][fc * 128:(fc + 1) * 128, :], "wld")
    p.op("vector", lambda e: e.memset(afr(c, 0, 64, AF_CAR, 1), 0.0), sem="dve")
    p.op("vector", lambda e: e.memset(c["onesf"][:, :], 1.0), sem="dve")
    p.wait("tensor", "wld")
    for i in range(TT):
        p.wait("tensor", "dve")
        for fc in range(FC):
            p.op("tensor", lambda e, i=i, fc=fc: e.matmul(
                psA[:, 0:64],
                ab(c, AB_XN + fc * T + i * 128, 128),
                ab(c, GW + fc * 64, 64),
                start=(fc == 0), stop=(fc == FC - 1)),
                sem=("pe" if fc == FC - 1 else None))
        p.wait("vector", "pe")
        p.op("vector", lambda e: e.tensor_copy(af(c, AF_RS, 64), psA[:, 0:64]), sem="dve")
        p.op("vector", lambda e: e.max(out=af(c, AF_RS + 64, 8), in_=af(c, AF_RS, 64)), sem="dve")
        p.op("vector", lambda e: e.max_index(out=c["idx8"][:, :], in_max=af(c, AF_RS + 64, 8),
                                             in_values=af(c, AF_RS, 64)), sem="dve")
        p.op("vector", lambda e: e.tensor_copy(af(c, AF_RS + 96, 8), c["idx8"][:, :]), sem="dve")
        p.op("vector", lambda e: e.tensor_scalar(af(c, AF_RS + 72, 1), af(c, AF_RS + 64, 1),
                                                 -1.0, None, op0=ALU.mult), sem="dve")
        p.wait("scalar", "dve")
        p.op("scalar", lambda e: e.activation(out=af(c, AF_RS + 80, 8), in_=af(c, AF_RS + 64, 8),
                                              func=AF_.Exp, bias=af(c, AF_RS + 72, 1)), sem="act")
        p.wait("vector", "act")
        p.op("vector", lambda e: e.reduce_sum(af(c, AF_RS + 73, 1), af(c, AF_RS + 80, 8), axis=AX.X), sem="dve")
        p.op("vector", lambda e: e.reciprocal(af(c, AF_RS + 74, 1), af(c, AF_RS + 73, 1)), sem="dve")
        p.op("vector", lambda e: e.tensor_scalar(af(c, AF_RS + 88, 8), af(c, AF_RS + 80, 8),
                                                 af(c, AF_RS + 74, 1), None, op0=ALU.mult), sem="dve")
        p.op("vector", lambda e, i=i: e.tensor_copy(af(c, AF_RK + i * 24, 8), af(c, AF_RS + 96, 8)), sem="dve")
        p.op("vector", lambda e, i=i: e.tensor_copy(af(c, AF_RK + i * 24 + 8, 8), af(c, AF_RS + 88, 8)), sem="dve")
        io64 = c["iota64t"][:, :].rearrange("p (e z) -> p e z", z=1).to_broadcast([128, 64, 8])
        idv = af(c, AF_RS + 96, 8).rearrange("p (z j) -> p z j", z=1).to_broadcast([128, 64, 8])
        p.op("vector", lambda e, io64=io64, idv=idv: e.tensor_tensor(
            af(c, AF_RS + 512, 512).rearrange("p (ee j) -> p ee j", j=8), io64, idv,
            op=ALU.is_equal), sem="dve")
        wvv = af(c, AF_RS + 88, 8).rearrange("p (z j) -> p z j", z=1).to_broadcast([128, 64, 8])
        p.op("vector", lambda e, wvv=wvv: e.tensor_tensor(
            af(c, AF_RS + 1024, 512).rearrange("p (ee j) -> p ee j", j=8),
            af(c, AF_RS + 512, 512).rearrange("p (ee j) -> p ee j", j=8), wvv,
            op=ALU.mult), sem="dve")
        p.op("vector", lambda e: e.reduce_sum(
            af(c, AF_RS + 128, 64), af(c, AF_RS + 1024, 512).rearrange("p (ee j) -> p ee j", j=8),
            axis=AX.X), sem="dve")
        p.op("vector", lambda e: e.tensor_scalar(af(c, AF_RS + 192, 64), af(c, AF_RS + 128, 64),
                                                 0.0, None, op0=ALU.is_gt), sem="dve")
        p.wait("tensor", "dve")
        p.op("tensor", lambda e: e.transpose(psS[0:64, 0:128], af(c, AF_RS + 192, 64),
                                             c["idft"][:, :]), sem="pe")
        p.wait("vector", "pe")
        p.op("vector", lambda e: e.tensor_copy(afr(c, 0, 64, AF_RS + 256, 128), psS[0:64, 0:128]), sem="dve")
        p.op("vector", lambda e, i=i: e.tensor_tensor_scan(
            out=c["cum"][:, i * 128:(i + 1) * 128],
            data0=c["onesf"][:, :], data1=afr(c, 0, 64, AF_RS + 256, 128),
            initial=afr(c, 0, 64, AF_CAR, 1), op0=ALU.mult, op1=ALU.add), sem="dve")
        p.op("vector", lambda e, i=i: e.tensor_copy(
            afr(c, 0, 64, AF_CAR, 1), c["cum"][:, i * 128 + 127:(i + 1) * 128]), sem="dve")
    p.wait("sync", "dve")
    p.dma("sync", c["cnti"][:, :], afr(c, 0, 64, AF_CAR, 1), "gd")
    p.wait("gpsimd", "gd")
    p.op("gpsimd", lambda e: e.collective_compute(
        "AllGather", ALU.bypass, replica_groups=[list(range(NCORES))],
        ins=[c["cnti"].opt()], outs=[c["cnto"].opt()]), sem="cc")
    p.wait("sync", "cc")
    p.dma("sync", afr(c, 0, 8, AF_RS, 64),
          c["cnto"].rearrange("(r e) o -> r (e o)", e=NE), "ld")
    p.wait("tensor", "ld")
    p.wait("tensor", "dve")
    p.op("tensor", lambda e: e.matmul(psS[0:64, 128:129], afr(c, 0, 8, AF_RS, 64),
                                      c["rankmt"][:, :], start=True, stop=True), sem="pe")
    p.wait("vector", "pe")
    p.op("vector", lambda e: e.tensor_copy(afr(c, 0, 64, AF_CAR + 1, 1), psS[0:64, 128:129]), sem="dve")
    p.op("vector", lambda e: e.tensor_scalar(afr(c, 0, 64, AF_CAR + 2, 1),
                                             afr(c, 0, 64, AF_CAR + 1, 1),
                                             -1.0, None, op0=ALU.add), sem="dve")
    p.op("vector", lambda e: e.memset(af(c, AF_S1, 513), -1.0), sem="dve")
    p.wait("gpsimd", "dve")
    p.dma("gpsimd", c["recm"].flatten().rearrange("(p a) -> p a", p=128),
          af(c, AF_S1, 513), "gd")
    p.wait("gpsimd", "gd")
    for i in range(TT):
        p.wait("vector", "dve")
        p.wait("vector", "gd")
        p.op("vector", lambda e, i=i: e.tensor_scalar(
            afr(c, 0, 64, AF_S2, 128), c["cum"][:, i * 128:(i + 1) * 128],
            afr(c, 0, 64, AF_CAR + 2, 1), None, op0=ALU.add), sem="dve")
        p.wait("tensor", "dve")
        p.op("tensor", lambda e: e.transpose(psS[:, 0:64], afr(c, 0, 64, AF_S2, 128),
                                             c["idft"][0:64, 0:64]), sem="pe")
        p.wait("vector", "pe")
        p.op("vector", lambda e: e.tensor_copy(af(c, AF_S2 + 128, 64), psS[:, 0:64]), sem="dve")
        io64 = c["iota64t"][:, :].rearrange("p (z e) -> p z e", z=1).to_broadcast([128, 8, 64])
        idv = af(c, AF_RK + i * 24, 8).rearrange("p (j z) -> p j z", z=1).to_broadcast([128, 8, 64])
        p.op("vector", lambda e, io64=io64, idv=idv: e.tensor_tensor(
            af(c, AF_RS + 512, 512).rearrange("p (j ee) -> p j ee", ee=64), io64, idv,
            op=ALU.is_equal), sem="dve")
        pos = af(c, AF_S2 + 128, 64).rearrange("p (z e) -> p z e", z=1).to_broadcast([128, 8, 64])
        p.op("vector", lambda e, pos=pos: e.tensor_tensor(
            af(c, AF_RS + 1024, 512).rearrange("p (j ee) -> p j ee", ee=64),
            af(c, AF_RS + 512, 512).rearrange("p (j ee) -> p j ee", ee=64), pos,
            op=ALU.mult), sem="dve")
        p.op("vector", lambda e, i=i: e.reduce_sum(
            af(c, AF_RK + i * 24 + 16, 8),
            af(c, AF_RS + 1024, 512).rearrange("p (j ee) -> p j ee", ee=64), axis=AX.X), sem="dve")
        sl = af(c, AF_RK + i * 24 + 16, 8)
        p.op("vector", lambda e, sl=sl: e.tensor_scalar(af(c, AF_RS, 8), sl, 512.0, None,
                                                        op0=ALU.is_ge), sem="dve")
        p.op("vector", lambda e, i=i: e.tensor_scalar(
            af(c, AF_RS + 8, 8), af(c, AF_RK + i * 24, 8), 513.0, None, op0=ALU.mult), sem="dve")
        p.op("vector", lambda e, sl=sl: e.tensor_add(af(c, AF_RS + 8, 8), af(c, AF_RS + 8, 8), sl), sem="dve")
        p.op("vector", lambda e: e.scalar_tensor_tensor(
            out=af(c, AF_RS + 16, 8), in0=af(c, AF_RS, 8), scalar=BIG,
            in1=af(c, AF_RS + 8, 8), op0=ALU.mult, op1=ALU.add), sem="dve")
        p.op("vector", lambda e: e.tensor_copy(c["sidx"][:, :], af(c, AF_RS + 16, 8)), sem="dve")
        p.op("vector", lambda e, i=i: e.tensor_scalar(
            af(c, AF_RS + 24, 8), af(c, AF_RK + i * 24, 8), 512.0, None, op0=ALU.mult), sem="dve")
        p.op("vector", lambda e, sl=sl: e.tensor_add(af(c, AF_RS + 24, 8), af(c, AF_RS + 24, 8), sl), sem="dve")
        p.op("vector", lambda e: e.scalar_tensor_tensor(
            out=af(c, AF_RS + 32, 8), in0=af(c, AF_RS, 8), scalar=BIG,
            in1=af(c, AF_RS + 24, 8), op0=ALU.mult, op1=ALU.add), sem="dve")
        p.op("vector", lambda e, i=i: e.tensor_copy(c["widx"][:, i * 8:(i + 1) * 8],
                                                    af(c, AF_RS + 32, 8)), sem="dve")
        tok = c["tokidt"][:, i:i + 1].rearrange("p (o z) -> p o z", z=1).to_broadcast([128, 8, 1])
        rec = af(c, AF_RS + 40, 16).rearrange("p (j two) -> p j two", two=2)
        p.op("vector", lambda e, tok=tok, rec=rec: e.tensor_copy(rec[:, :, 0:1], tok), sem="dve")
        wk = af(c, AF_RK + i * 24 + 8, 8).rearrange("p (j z) -> p j z", z=1)
        p.op("vector", lambda e, wk=wk, rec=rec: e.tensor_copy(rec[:, :, 1:2], wk), sem="dve")
        p.wait("gpsimd", "dve")
        for j in range(TOPK):
            p.op("gpsimd", lambda e, j=j: e.indirect_dma_start(
                out=c["recm"][:, :],
                out_offset=bass.IndirectOffsetOnAxis(ap=c["sidx"][:, j:j + 1], axis=0),
                in_=af(c, AF_RS + 40 + 2 * j, 2),
                in_offset=None, bounds_check=c["bregs"][NE * 513 - 1], oob_is_err=False), sem="gd", n=16)
    p.wait("gpsimd", "gd")
    p.op("gpsimd", lambda e: e.collective_compute(
        "AllReduce", ALU.max, replica_groups=[list(range(NCORES))],
        ins=[c["recm"].opt()], outs=[c["recmr"].opt()]), sem="cc")


def emit_experts(c):
    p = c["p"]
    psA, psB, psC, psD = c["psA"], c["psB"], c["psC"], c["psD"]
    p.wait("gpsimd", "cc")
    pst = c["psS"][:, 0:64].bitcast(BF16)
    for e_i in range(EPC):
        p.wait("gpsimd", "dve")
        for gg in range(4):
            p.op("gpsimd", lambda e, e_i=e_i, gg=gg: e.indirect_dma_start(
                out=af(c, AF_S1 + gg * 2, 2), out_offset=None,
                in_=c["recmr"][:, :],
                in_offset=bass.IndirectOffsetOnAxis(
                    ap=c["recidxt"][:, e_i * 4 + gg:e_i * 4 + gg + 1], axis=0),
                bounds_check=c["bregs"][NE * 513 - 1], oob_is_err=False), sem="gd", n=16)
        p.wait("vector", "gd")
        tv = af(c, AF_S1, 8).rearrange("p (g two) -> p g two", two=2)[:, :, 0:1]
        wv = af(c, AF_S1, 8).rearrange("p (g two) -> p g two", two=2)[:, :, 1:2]
        p.op("vector", lambda e, tv=tv: e.tensor_copy(
            af(c, AF_S2, 4).rearrange("p (g z) -> p g z", z=1), tv), sem="dve")
        p.op("vector", lambda e, wv=wv: e.tensor_copy(
            af(c, AF_S2 + 8, 4).rearrange("p (g z) -> p g z", z=1), wv), sem="dve")
        p.op("vector", lambda e: e.tensor_scalar(af(c, AF_S2 + 4, 4), af(c, AF_S2, 4),
                                                 0.0, None, op0=ALU.is_lt), sem="dve")
        p.op("vector", lambda e: e.scalar_tensor_tensor(
            out=af(c, AF_S2 + 16, 4), in0=af(c, AF_S2 + 4, 4), scalar=BIG,
            in1=af(c, AF_S2, 4), op0=ALU.mult, op1=ALU.add), sem="dve")
        p.op("vector", lambda e: e.tensor_copy(c["sidx"][:, 0:4], af(c, AF_S2 + 16, 4)), sem="dve")
        p.wait("gpsimd", "dve")
        p.wait("gpsimd", "pe")
        for gg in range(4):
            p.op("gpsimd", lambda e, gg=gg: e.memset(ab(c, AB_BUF_T + gg * 1024, 1024), 0.0), sem="gd", n=1)
        for gg in range(4):
            p.op("gpsimd", lambda e, gg=gg: e.indirect_dma_start(
                out=ab(c, AB_BUF_T + gg * 1024, 1024), out_offset=None,
                in_=c["xgo"][:, :],
                in_offset=bass.IndirectOffsetOnAxis(ap=c["sidx"][:, gg:gg + 1], axis=0),
                bounds_check=c["bregs"][NCORES * T - 1], oob_is_err=False), sem="gd", n=16)
        for hc in range(FC):
            p.dma("gpsimd", ab(c, AB_EW + hc * 1024, 1024),
                  c["egu"][e_i, hc * 128:(hc + 1) * 128, :], "wld")
        p.wait("tensor", "gd")
        p.wait("tensor", "dve")
        psts = (c["psA"][:, 0:64].bitcast(BF16), c["psB"][:, 0:64].bitcast(BF16))
        dvec = [p.cnt["dve"], p.cnt["dve"]]
        for gg in range(4):
            for hc in range(FC):
                k = (gg * FC + hc) % 2
                pstk = psts[k]
                p.wait("tensor", "dve", val=dvec[k])
                p.op("tensor", lambda e, gg=gg, hc=hc, pstk=pstk: e.transpose(
                    pstk, ab(c, AB_BUF_T + gg * 1024 + hc * 128, 128), c["idbt"][:, :]), sem="pe")
                p.wait("vector", "pe")
                p.op("vector", lambda e, gg=gg, hc=hc, pstk=pstk: e.tensor_copy(
                    ab(c, AB_BUF_FM + hc * 512 + gg * 128, 128), pstk), sem="dve")
                dvec[k] = p.cnt["dve"]
        p.wait("tensor", "wld")
        for guc in range(8):
            ps = psA if guc % 2 == 0 else psB
            p.wait("tensor", "act")
            p.wait("tensor", "dve")
            for hc in range(FC):
                p.op("tensor", lambda e, ps=ps, guc=guc, hc=hc: e.matmul(
                    ps[:, 0:512],
                    ab(c, AB_EW + hc * 1024 + guc * 128, 128),
                    ab(c, AB_BUF_FM + hc * 512, 512),
                    start=(hc == 0), stop=(hc == FC - 1)),
                    sem=("pe" if hc == FC - 1 else None))
            if guc < 4:
                p.wait("scalar", "pe")
                p.op("scalar", lambda e, ps=ps, guc=guc: e.activation(
                    out=ab(c, AB_ACT + guc * 512, 512), in_=ps[:, 0:512],
                    func=AF_.Silu), sem="act")
            else:
                p.wait("vector", "pe")
                p.op("vector", lambda e, ps=ps, guc=guc: e.tensor_copy(
                    ab(c, AB_ACT4 + (guc - 4) * 512, 512), ps[:, 0:512]), sem="dve")
        p.wait("vector", "act")
        for ic in range(4):
            p.op("vector", lambda e, ic=ic: e.tensor_mul(
                ab(c, AB_ACT + ic * 512, 512), ab(c, AB_ACT + ic * 512, 512),
                ab(c, AB_ACT4 + ic * 512, 512)), sem="dve")
        p.wait("gpsimd", "pe")
        for icw in range(4):
            p.dma("gpsimd", ab(c, AB_EW + icw * 1024, 1024),
                  c["edw"][e_i, icw * 128:(icw + 1) * 128, :], "wld")
        p.wait("tensor", "wld")
        for sg in range(4):
            for n in range(2):
                ps = psC if (sg * 2 + n) % 2 == 0 else psD
                p.wait("tensor", "dve")
                p.wait("tensor", "gd")
                for icm in range(4):
                    p.op("tensor", lambda e, ps=ps, sg=sg, n=n, icm=icm: e.matmul(
                        ps[:, 0:512],
                        ab(c, AB_ACT + icm * 512 + sg * 128, 128),
                        ab(c, AB_EW + icm * 1024 + n * 512, 512),
                        start=(icm == 0), stop=(icm == 3)),
                        sem=("pe" if icm == 3 else None))
                p.wait("vector", "pe")
                p.wait("vector", "gd")
                p.op("vector", lambda e, ps=ps, sg=sg: e.tensor_scalar(
                    ab(c, AB_WOUT, 512), ps[:, 0:512],
                    af(c, AF_S2 + 8 + sg, 1), None, op0=ALU.mult), sem="dve")
                p.wait("sync", "dve")
                p.dma("sync", c["woutl"][e_i * 512 + sg * 128: e_i * 512 + (sg + 1) * 128,
                                         n * 512:(n + 1) * 512], ab(c, AB_WOUT, 512), "gd")
    p.wait("gpsimd", "gd")
    p.op("gpsimd", lambda e: e.collective_compute(
        "AllGather", ALU.bypass, replica_groups=[list(range(NCORES))],
        ins=[c["woutl"].opt()], outs=[c["woutg"].opt()]), sem="cc")


def emit_shared(c):
    p = c["p"]
    psA, psB, psC, psD = c["psA"], c["psB"], c["psC"], c["psD"]
    for fc in range(FC):
        p.dma("gpsimd", c["sgwb"][:, fc:fc + 1], c["sgw"][fc * 128:(fc + 1) * 128, :], "wld")
    p.wait("tensor", "wld")
    for i in range(TT):
        p.wait("tensor", "act")
        p.wait("tensor", "dve")
        for fc in range(FC):
            p.op("tensor", lambda e, i=i, fc=fc: e.matmul(
                psC[:, 0:1],
                ab(c, AB_XN + fc * T + i * 128, 128),
                c["sgwb"][:, fc:fc + 1],
                start=(fc == 0), stop=(fc == FC - 1)),
                sem=("pe" if fc == FC - 1 else None))
        p.wait("scalar", "pe")
        p.op("scalar", lambda e, i=i: e.activation(
            out=af(c, AF_RK + 400 + i, 1), in_=psC[:, 0:1], func=AF_.Sigmoid), sem="act")
    p.wait("sync", "act")
    for i in range(TT):
        p.dma("sync", c["sgd"][i * 128:(i + 1) * 128].rearrange("(p o) -> p o", o=1),
              af(c, AF_RK + 400 + i, 1), "gd")
    p.wait("sync", "gd")
    p.dma("sync", af(c, AF_LNR, 2048)[0:1, :], c["sgd"].rearrange("(o a) -> o a", o=1), "ld")
    for i4 in range(4):
        for icl in range(4):
            ic4 = i4 * 4 + icl
            p.wait("gpsimd", "pe")
            gsrc = c["sgu"][:, ic4 * 128:(ic4 + 1) * 128].rearrange("(h p) n -> p h n", p=128)
            usrc = c["sgu"][:, ISH + ic4 * 128:ISH + (ic4 + 1) * 128].rearrange("(h p) n -> p h n", p=128)
            gdst = ab(c, AB_EW, 2048).rearrange("p (h n) -> p h n", n=256)[:, :, 0:128]
            udst = ab(c, AB_EW, 2048).rearrange("p (h n) -> p h n", n=256)[:, :, 128:256]
            p.dma("gpsimd", gdst, gsrc, "wld")
            p.dma("gpsimd", udst, usrc, "wld")
            p.wait("tensor", "wld")
            for tc in range(TC):
                p.wait("tensor", "act")
                p.wait("tensor", "dve")
                for hc in range(FC):
                    p.op("tensor", lambda e, tc=tc, hc=hc: e.matmul(
                        psA[:, 0:512],
                        ab(c, AB_EW + hc * 256, 128),
                        ab(c, AB_XN + hc * T + tc * 512, 512),
                        start=(hc == 0), stop=(hc == FC - 1)),
                        sem=("pe" if hc == FC - 1 else None))
                for hc in range(FC):
                    p.op("tensor", lambda e, tc=tc, hc=hc: e.matmul(
                        psB[:, 0:512],
                        ab(c, AB_EW + hc * 256 + 128, 128),
                        ab(c, AB_XN + hc * T + tc * 512, 512),
                        start=(hc == 0), stop=(hc == FC - 1)),
                        sem=("pe" if hc == FC - 1 else None))
                p.wait("scalar", "pe")
                p.op("scalar", lambda e: e.activation(
                    out=ab(c, AB_WOUT, 512), in_=psA[:, 0:512], func=AF_.Silu), sem="act")
                p.wait("vector", "act")
                p.op("vector", lambda e, icl=icl, tc=tc: e.tensor_mul(
                    ab(c, AB_ACT4 + icl * 2048 + tc * 512, 512),
                    ab(c, AB_WOUT, 512), psB[:, 0:512]), sem="dve")
        p.wait("gpsimd", "pe")
        for icl in range(4):
            ic4 = i4 * 4 + icl
            p.dma("gpsimd", ab(c, AB_EW + 2048 + icl * 1024, 1024),
                  c["sdw"][ic4 * 128:(ic4 + 1) * 128, :], "wld")
        p.wait("tensor", "wld")
        p.wait("tensor", "dve")
        for hc in range(FC):
            for tc in range(TC):
                ps = psC if (hc * TC + tc) % 2 == 0 else psD
                p.wait("tensor", "dve")
                for icl in range(4):
                    p.op("tensor", lambda e, ps=ps, hc=hc, tc=tc, icl=icl: e.matmul(
                        ps[:, 0:512],
                        ab(c, AB_EW + 2048 + icl * 1024 + hc * 128, 128),
                        ab(c, AB_ACT4 + icl * 2048 + tc * 512, 512),
                        start=(icl == 0), stop=(icl == 3)),
                        sem=("pe" if icl == 3 else None))
                p.wait("vector", "pe")
                dst = ab(c, AB_SHACC + hc * T + tc * 512, 512)
                if i4 == 0:
                    p.op("vector", lambda e, ps=ps, dst=dst: e.tensor_copy(dst, ps[:, 0:512]), sem="dve")
                else:
                    p.op("vector", lambda e, ps=ps, dst=dst: e.tensor_add(dst, dst, ps[:, 0:512]), sem="dve")


COMB_PARTS = 3


def emit_combine(c):
    p = c["p"]
    psD, psS = c["psD"], c["psS"]
    p.wait("gpsimd", "cc")
    for tc in range(TC):
        p.wait("tensor", "dve")
        p.wait("tensor", "ld")
        p.op("tensor", lambda e, tc=tc: e.matmul(
            psD[:, 0:512], c["ones1"][0:1, 0:128],
            af(c, AF_LNR + tc * 512, 512)[0:1, :], start=True, stop=True), sem="pe")
        for ii in range(4):
            i = tc * 4 + ii
            slot_a = AF_S1 + (i % 2) * 1024
            slot_x = AF_S2 + (i % 2) * 1024
            p.wait("sync", "dve")
            x1src = c["x1d"][:, i * 128:(i + 1) * 128].rearrange("(h p) n -> p h n", p=128)
            x1dst = af(c, slot_x, 1024).rearrange("p (h n) -> p h n", n=128)
            p.dma("sync", x1dst, x1src, "ld")
            p.wait("gpsimd", "dve")
            p.op("gpsimd", lambda e: e.memset(ab(c, AB_QF, 8192), 0.0), sem="gd", n=1)
            for j in range(TOPK):
                p.op("gpsimd", lambda e, i=i, j=j: e.indirect_dma_start(
                    out=ab(c, AB_QF + j * 1024, 1024), out_offset=None,
                    in_=c["woutg"][:, :],
                    in_offset=bass.IndirectOffsetOnAxis(
                        ap=c["widx"][:, i * 8 + j:i * 8 + j + 1], axis=0),
                    bounds_check=c["bregs"][NE * CAP - 1], oob_is_err=False), sem="gd", n=16)
            p.wait("vector", "gd")
            p.wait("vector", "pe")
            p.op("vector", lambda e, slot_a=slot_a: e.tensor_add(
                af(c, slot_a, 1024), ab(c, AB_QF, 1024), ab(c, AB_QF + 1024, 1024)), sem="dve")
            for j in range(2, TOPK):
                p.op("vector", lambda e, j=j, slot_a=slot_a: e.tensor_add(
                    af(c, slot_a, 1024), af(c, slot_a, 1024),
                    ab(c, AB_QF + j * 1024, 1024)), sem="dve")
            p.wait("tensor", "ld")
            p.wait("tensor", "dve")
            trc = [p.cnt["dve"], p.cnt["dve"]]
            for fc in range(FC):
                k = fc % 2
                pstr = (c["psA"], c["psB"])[k]
                p.wait("tensor", "dve", val=trc[k])
                p.op("tensor", lambda e, fc=fc, slot_a=slot_a, pstr=pstr: e.transpose(
                    pstr[:, 0:128], af(c, slot_a + fc * 128, 128), c["idft"][:, :]), sem="pe")
                slot = fc % 2
                p.wait("vector", "pe")
                p.wait("vector", "ld")
                p.wait("vector", "st")
                p.op("vector", lambda e, fc=fc, i=i, slot=slot: e.tensor_mul(
                    af(c, AF_RS + slot * 256, 128),
                    ab(c, AB_SHACC + fc * T + i * 128, 128),
                    psD[:, (i % 4) * 128:(i % 4) * 128 + 128]), sem="dve")
                p.op("vector", lambda e, slot=slot, pstr=pstr: e.tensor_add(
                    af(c, AF_RS + slot * 256 + 128, 128),
                    af(c, AF_RS + slot * 256, 128), pstr[:, 0:128]), sem="dve")
                p.op("vector", lambda e, fc=fc, slot=slot, slot_x=slot_x: e.tensor_add(
                    af(c, AF_RS + slot * 256 + 128, 128),
                    af(c, AF_RS + slot * 256 + 128, 128),
                    af(c, slot_x + fc * 128, 128)), sem="dve")
                trc[k] = p.cnt["dve"]
                p.wait("sync", "dve")
                p.dma("sync", c["out_fm"][fc * 128:(fc + 1) * 128, i * 128:(i + 1) * 128],
                      af(c, AF_RS + slot * 256 + 128, 128), "st")


def emit_dump(c, regions):
    p = c["p"]
    p.wait("gpsimd", "dve")
    p.wait("gpsimd", "pe")
    p.wait("gpsimd", "act")
    p.wait("gpsimd", "gd")
    p.wait("gpsimd", "wld")
    p.wait("sync", "dve")
    p.wait("sync", "pe")
    p.wait("sync", "gd")
    p.wait("sync", "act")
    p.wait("sync", "wld")
    p.wait("sync", "cc")
    if "ab" in regions:
        CHW = 8192
        for o in range(0, AB_N, CHW):
            w = min(CHW, AB_N - o)
            p.dma("gpsimd", c["dbg_ab"][:, o:o + w], c["AB"][:, o:o + w], "st")
    if "af" in regions:
        p.dma("sync", c["dbg_af"][:, :], c["AF"][:, :], "st")
    if "sag" in regions:
        p.dma("sync", c["dbg_sag"][0:128, :], af(c, AF_SSB, 1040), "st")
        p.dma("sync", c["dbg_sag"][128:160, :], c["ssbB"][:, :], "st")
    if "rec" in regions:
        p.dma("sync", c["dbg_rec"][:, :], c["recmr"][:, :], "st")
    if "cumd" in regions:
        p.dma("sync", c["dbg_cum"][:, :], c["cum"][:, :], "st")
    if "widxd" in regions:
        p.dma("sync", c["dbg_widx"][:, :], c["widx"][:, :], "st")
    if "woutd" in regions:
        for o in range(0, EPC * CAP, 512):
            p.dma("gpsimd", c["dbg_wout"][o:o + 512, :], c["woutl"][o:o + 512, :], "st")
    p.wait("gpsimd", "st")
    p.wait("sync", "st")



def make_inputs(inputs):
    x = np.asarray(inputs["L0_attn_hidden_states"])
    Wq = np.asarray(inputs["L0_attn_Wq"])
    Wk = np.asarray(inputs["L0_attn_Wk"])
    Wv = np.asarray(inputs["L0_attn_Wv"])
    Wo = np.asarray(inputs["L0_attn_Wo"])
    gate = np.asarray(inputs["L0_ffn_gate_w"])
    sgw = np.asarray(inputs["L0_ffn_shared_gate_w"])
    sgu = np.asarray(inputs["L0_ffn_shared_gate_up_w"])
    sdw = np.asarray(inputs["L0_ffn_shared_down_w"])
    egu = np.asarray(inputs["L0_ffn_expert_gate_up_w"])
    edw = np.asarray(inputs["L0_ffn_expert_down_w"])

    wqk = np.concatenate([_pad_qk_weight(Wq), _pad_qk_weight(Wk)], axis=1)
    wvT = np.ascontiguousarray(Wv.T)
    woT = np.ascontiguousarray(Wo.T)
    Ls, Rs = _build_selection()
    msk = _causal_masks()
    iota64 = np.broadcast_to(np.arange(NE, dtype=np.float32)[None, :], (128, NE)).copy()

    in_maps = []
    for core in range(NCORES):
        b, half = core // 2, core % 2
        shard = x[b, half * T:(half + 1) * T, :]
        x_fm = np.ascontiguousarray(shard.T)
        tokid = (core * T + np.arange(T, dtype=np.float32)).reshape(TT, 128).T.copy()
        rankm = (np.arange(8) < core).astype(np.float32).reshape(8, 1)
        selm_v = np.zeros(8, np.float32)
        if half == 1:
            selm_v[core - 1] = 1.0
        selm = np.broadcast_to(selm_v[None, :], (128, 8)).copy()
        ridx = np.zeros((128, EPC * 4), np.int32)
        pcol = np.arange(128, dtype=np.int32)
        for e_i in range(EPC):
            for g in range(4):
                ridx[:, e_i * 4 + g] = (core * EPC + e_i) * 513 + g * 128 + pcol
        in_maps.append({
            "x_fm": x_fm,
            "wqk": wqk, "wv": wvT, "wo": woT,
            "gate": gate, "sgw": sgw, "sgu": sgu, "sdw": sdw,
            "egu": np.ascontiguousarray(egu[core * EPC:(core + 1) * EPC]),
            "edw": np.ascontiguousarray(edw[core * EPC:(core + 1) * EPC]),
            "lsel": Ls.astype(ml_dtypes.bfloat16),
            "rsel": Rs.astype(ml_dtypes.bfloat16),
            "msk": msk.astype(ml_dtypes.bfloat16),
            "idf": np.eye(128, dtype=np.float32),
            "idb": np.eye(128, dtype=ml_dtypes.bfloat16),
            "iota64": iota64,
            "tokid": tokid,
            "rankm": rankm,
            "selm": selm,
            "recidx": ridx,
            "cvals": np.broadcast_to(
                np.array([FSC, FB, LNEPS, 0, 0, 0, 0, 0], np.float32)[None, :],
                (128, 8)).copy(),
        })
    return in_maps


def kernel(**inputs):
    in_maps = make_inputs(inputs)
    nc = build_program()
    res = run_bass_kernel_spmd(nc, in_maps, core_ids=list(range(NCORES)))
    out = np.empty((B, S, HID), np.float32)
    for core in range(NCORES):
        b, half = core // 2, core % 2
        out[b, half * T:(half + 1) * T, :] = res.results[core]["out_fm"].T
    return out
